# version 09
# baseline (speedup 1.0000x reference)
from contextlib import ExitStack

import numpy as np
import ml_dtypes

import concourse.bass as bass
import concourse.mybir as mybir
from concourse.bass_utils import run_bass_kernel_spmd

F32 = mybir.dt.float32
BF16 = mybir.dt.bfloat16
AF = mybir.ActivationFunctionType
ALU = mybir.AluOpType
AX = mybir.AxisListType
BF = ml_dtypes.bfloat16

D = 1024
T = 4096
ATTN_PARTS = "ABC"


def _tile_pp(v, dtype=np.float32):
    v = np.asarray(v)
    n = v.shape[0] // 128
    return np.ascontiguousarray(v.reshape(n, 128).T).astype(dtype)


def _dft_consts():
    j = np.arange(D, dtype=np.float64)
    ang = 2.0 * np.pi * np.outer(j, j) / D
    c = lambda a: np.ascontiguousarray(a).astype(BF)
    Fr, Fi = np.cos(ang), -np.sin(ang)
    Gr, Gi = np.cos(ang) / D, np.sin(ang) / D
    return c(Fr), c(Fi), c(-Fi), c(Gr), c(-Gi)


def _rope_consts():
    inv_freq = 1.0 / (10000.0 ** (np.arange(0, D, 2, dtype=np.float32) / D))
    t = np.arange(T, dtype=np.float32)
    freqs = np.outer(t, inv_freq).astype(np.float32)
    cs = np.stack([np.cos(freqs).T, np.sin(freqs).T])
    return np.ascontiguousarray(cs.reshape(2, 4, 128, T)).astype(BF)


def host_prep(inp):
    sh = {}
    f32 = lambda a: np.ascontiguousarray(np.asarray(a)).astype(np.float32)

    sh["w1t"] = f32(np.asarray(inp["L0_ffn_W1"]).T)
    sh["w2t"] = f32(np.asarray(inp["L0_ffn_W2"]).T)
    sh["b1_pp"] = _tile_pp(inp["L0_ffn_b1"])
    sh["b2_row"] = f32(inp["L0_ffn_b2"]).reshape(1, 1024)
    sh["scale_pp"] = np.ascontiguousarray(
        np.broadcast_to(np.asarray(inp["L0_ffn_scale"], np.float32), (128, 1)))

    sh["wqkvt"] = f32(np.asarray(inp["L0_attn_c_attn_w"]).T)
    sh["bqkv_row"] = f32(inp["L0_attn_c_attn_b"]).reshape(1, 3072)
    sh["wprojt"] = f32(np.asarray(inp["L0_attn_c_proj_w"]).T)
    sh["bproj_row"] = f32(inp["L0_attn_c_proj_b"]).reshape(1, 1024)
    sh["rope_csp"] = _rope_consts()

    sh["wfr"] = f32(inp["L1_ffn_w_freq_real"])
    sh["wfi"] = f32(inp["L1_ffn_w_freq_imag"])
    sh["c_fr"], sh["c_fi"], sh["c_fin"], sh["c_gr"], sh["c_gin"] = _dft_consts()
    sh["bsp_row"] = f32(inp["L1_ffn_bias"]).reshape(1, 1024)

    sh["w_in"] = f32(inp["L1_attn_W_in"])
    cw = np.asarray(inp["L1_attn_conv_w"], np.float32)
    sh["convw_pp"] = np.ascontiguousarray(
        np.transpose(cw.reshape(17, 128, 16), (1, 0, 2)).reshape(128, 272))
    sh["convb_pp"] = _tile_pp(inp["L1_attn_conv_b"])
    sh["dtb_col"] = f32(inp["L1_attn_dt_bias"]).reshape(8, 1)
    sh["alog_col"] = f32(inp["L1_attn_A_log"]).reshape(8, 1)
    Dv = np.asarray(inp["L1_attn_D"], np.float32)
    sh["d_pp"] = np.ascontiguousarray(
        np.broadcast_to(np.repeat(Dv, 2)[None, :], (128, 16)))
    ist = np.asarray(inp["L1_attn_init_states"], np.float32)
    sh["h0"] = np.ascontiguousarray(np.transpose(ist, (2, 0, 1)).reshape(64, 2048))
    sh["normw_pp"] = _tile_pp(inp["L1_attn_norm_w"])
    sh["w_out"] = f32(inp["L1_attn_W_out"])
    sh["w_nl"] = f32(inp["L1_attn_W_nl"])
    sh["bnl_row"] = f32(inp["L1_attn_b_nl"]).reshape(1, 1024)

    x = np.asarray(inp["L0_attn_x"], np.float32)
    per_core = [{"x0": np.ascontiguousarray(x[i].T)} for i in range(8)]
    return sh, per_core



ENGS = ("sync", "scalar", "vector", "gpsimd", "tensor")


AUTO_SEM = {"vector": "dv", "scalar": "ac", "gpsimd": "gp"}


class Rec:

    def __init__(self):
        self.items = []
        self.cnt = {}

    def emit(self, eng, fn, wait=(), inc=None, dma=False):
        wd_ = {}
        for x in wait:
            if x is not None:
                wd_[x[0]] = max(wd_.get(x[0], 0), x[1])
        w = list(wd_.items())
        if not dma and eng in AUTO_SEM:
            s = AUTO_SEM[eng]
            prev = self.cnt.get(s, 0)
            if prev > 0:
                w.append((s, prev))
            assert inc is None
            inc = (s, 1)
        if inc is not None:
            self.cnt[inc[0]] = self.cnt.get(inc[0], 0) + inc[1]
        self.items.append((eng, fn, tuple(w), inc))
        return None if inc is None else (inc[0], self.cnt[inc[0]])

    def val(self, sem):
        return self.cnt.get(sem, 0)

    def mark(self, sem):
        return (sem, self.cnt.get(sem, 0))

    def dma(self, eng, out, in_, sem, wait=(), **kw):
        return self.emit(eng, lambda h: h.dma_start(out=out, in_=in_, **kw),
                         wait=wait, inc=(sem, 16), dma=True)

    def replay(self, block, sems):
        def mk(engname):
            def run(h):
                for (eng, fn, waits, inc) in self.items:
                    if eng != engname:
                        continue
                    for (s, v) in waits:
                        h.wait_ge(sems[s], v)
                    ins = fn(h)
                    if inc is not None:
                        ins.then_inc(sems[inc[0]], inc[1])
            return run
        for engname in ENGS:
            getattr(block, engname)(mk(engname))


class Res:
    def __init__(self):
        self.m = None

    def wait(self):
        return self.m

    def set(self, mark):
        self.m = mark


class Buf(Res):

    def __init__(self, rec, sem):
        super().__init__()
        self.r = rec
        self.sem = sem

    def load(self, eng, out, in_, wait=(), **kw):
        sem = self.sem + ("g" if eng == "gpsimd" else "")
        return self.r.dma(eng, out, in_, sem,
                          wait=list(wait) + [self.m], **kw)


BUF_NAMES = ["wa", "wb", "wc", "wd", "xt", "xsq", "xlt", "big0", "big1",
             "ini", "xs", "us", "ms", "s0", "s1", "s2", "s3", "s4", "s5",
             "hsb", "xr0", "xr1", "odd", "tok", "evs", "e2t", "usb",
             "urow", "vsb", "eo", "uh", "vh", "qk", "oo", "brow",
             "kb0", "kb1", "cs", "lcr", "car", "hts", "cvh",
             "s6", "s7", "s8", "s9"]
BUF_SEMS = [n for b in BUF_NAMES for n in (b, b + "g")]
SEM_NAMES = ["mm", "dv", "ac", "gp"] + BUF_SEMS
ALL_PHASES = ("ffn0_a", "attn", "ffn0_b", "mkM", "ffn1_a", "mamba", "ffn1_b")


SCRATCH_SIZE = 16384
BIG1_W = 9216


def build_program(phases=ALL_PHASES, debug_out=(), mamba_tc=4, dram_out=()):
    nc = bass.Bass("TRN2", target_bir_lowering=False,
                   dynamic_dma_scratch_size=SCRATCH_SIZE)
    r = Rec()

    dr = {}
    def din(name, shape, dt=F32):
        dr[name] = nc.dram_tensor(name, list(shape), dt, kind="ExternalInput").ap()
        return dr[name]

    def dscr(name, shape, dt=F32):
        kind = "ExternalOutput" if name in dram_out else "Internal"
        dr[name] = nc.dram_tensor(name, list(shape), dt, kind=kind).ap()
        return dr[name]

    x0 = din("x0", [D, T])
    out_t = nc.dram_tensor("out", [D, T], F32, kind="ExternalOutput").ap()

    din("w1t", [1024, 4096]); din("w2t", [4096, 1024])
    din("b1_pp", [128, 32]); din("b2_row", [1, 1024]); din("scale_pp", [128, 1])
    din("wqkvt", [1024, 3072]); din("bqkv_row", [1, 3072])
    din("wprojt", [1024, 1024]); din("bproj_row", [1, 1024])
    din("rope_csp", [2, 4, 128, T], BF16)
    din("wfr", [1024, 1024]); din("wfi", [1024, 1024])
    for n in ("c_fr", "c_fi", "c_fin", "c_gr", "c_gin"):
        din(n, [1024, 1024], BF16)
    din("bsp_row", [1, 1024])
    din("w_in", [1024, 4232]); din("convw_pp", [128, 272]); din("convb_pp", [128, 17])
    din("dtb_col", [8, 1]); din("alog_col", [8, 1]); din("d_pp", [128, 16])
    din("h0", [64, 2048]); din("normw_pp", [128, 16])
    din("w_out", [2048, 1024]); din("w_nl", [1024, 1024]); din("bnl_row", [1, 1024])

    x_cur = dscr("x_cur", [D, T])
    m_hbm = dscr("m_hbm", [1024, 1024], BF16)
    eo_hbm = dscr("eo_hbm", [16, 1024], BF16)
    ul_hbm = dscr("ul_hbm", [4, 1024], BF16)
    v_hbm = dscr("v_hbm", [4, 4096], BF16)
    conv_hbm = dscr("conv_hbm", [34816, 128], BF16)
    lc_hbm = dscr("lc_hbm", [8, 1024])
    ca_hbm = dscr("ca_hbm", [8, 1024])
    qf_hbm = dscr("qf_hbm", [1024, T], BF16)
    kf_hbm = dscr("kf_hbm", [1024, T], BF16)
    vt_hbm = dscr("vt_hbm", [T, 1024], BF16)
    of_hbm = dscr("of_hbm", [1024, T], BF16)
    u_hbm = dscr("u_hbm", [1024, 1024])
    u2_hbm = dscr("u2_hbm", [1024, 1024])

    dbg = {}
    for (name, shape, dt) in debug_out:
        dbg[name] = nc.dram_tensor(
            "dbg_" + name, list(shape), {"f32": F32, "bf16": BF16}[dt],
            kind="ExternalOutput").ap()

    _dbg_ctr = [0]
    def dbg_dump(r, name, src, wait):
        if name in dbg:
            sem = ["s1", "s2", "s3", "s4", "s5", "s6", "s7", "s8",
                   "s9"][_dbg_ctr[0] % 9]
            _dbg_ctr[0] += 1
            return r.dma("sync", dbg[name], src, sem, wait=wait)
        return None

    es = ExitStack()
    with es:
        sb = {}
        def S(name, shape, dt=F32):
            sb[name] = es.enter_context(nc.sbuf_tensor("sb_" + name, list(shape), dt))
            return sb[name]

        S("consts", [128, 8])
        S("ones_rowb", [1, 512], BF16)
        S("ones_rowf", [1, 128])
        S("ones_colf", [128, 1])
        S("ones_colb", [128, 1], BF16)
        S("idb", [128, 128], BF16)
        S("cs", [128, 1040], BF16)
        S("brow", [1, 1024], BF16)
        S("b1_pp", [128, 32])
        S("bqkv_row", [1, 3072], BF16)
        S("scale_pp", [128, 1])
        S("xt", [128, 8 * 512])
        S("xsq", [128, 2 * 512])
        S("stat", [1, 4 * 512])

        S("xlt", [128, 8 * 1024], BF16)
        S("big0", [128, 9216], BF16)
        S("big1", [128, BIG1_W], BF16)
        sb["mks"] = sb["big0"][:, 4096:4608]
        S("wa", [128, 2048], BF16)
        S("wb", [128, 2048], BF16)
        S("wc", [128, 2048], BF16)
        S("wd", [128, 2048], BF16)
        S("hsb", [128, 32 * 1024], BF16)
        S("xr", [128, 1024])
        S("odd", [128, 64], BF16)
        sb["tok"] = sb["wd"][0:16, 0:1024]
        S("evs", [1, 4096], BF16)
        S("e2t", [4, 1024], BF16)
        S("usb", [4, 1024], BF16)
        S("urow", [1, 4096], BF16)
        S("scr8", [128, 2048])
        sb["vsb"] = sb["scr8"][:, :].bitcast(BF16)[0:4, :]
        sb["hts"] = sb["scr8"][0:64, :]
        sb["nmb"] = sb["usb"][0:1, 0:1024]
        sb["lcr"] = sb["urow"][0:1, 0:2048].bitcast(F32)
        sb["car"] = sb["urow"][0:1, 2048:3072]
        S("sm128", [128, 768])
        sb["idf"] = sb["sm128"][:, 0:128]
        sb["ngt"] = sb["sm128"][:, 128:256]
        sb["cvw"] = sb["sm128"][:, 256:528]
        sb["cvb"] = sb["sm128"][:, 528:545]
        sb["dpp"] = sb["sm128"][:, 545:561]
        sb["nw"] = sb["sm128"][:, 561:577]
        sb["z8"] = sb["sm128"][0:8, 577:705]
        sb["acol"] = sb["sm128"][0:8, 705:707]
        sb["dtb8"] = sb["sm128"][0:8, 707:708]

        psum = {}
        for i in range(8):
            psum[i] = es.enter_context(nc.psum_tensor(f"p{i}", [128, 512], F32))

        def weng(b):
            return "gpsimd" if b in ("wa", "wc") else "sync"

        bankres = {i: Res() for i in range(8)}
        bufres = {n: Buf(r, n) for n in (
            "wa", "wb", "wc", "wd", "xt", "xlt", "big0", "big1", "xsq",
            "hsb", "xr0", "xr1", "odd", "tok", "eor", "evs", "e2t", "usb",
            "urow", "vsb", "eo", "uh", "vh", "qk", "oo", "brow",
            "kb0", "kb1", "cs", "lcr", "car", "hts", "cvh")}
        ini = Buf(r, "ini")
        xcur_ready = []

        cvals = [1e-5, -0.5, 0.5, -1.0, 2.0, 1.0, 30.0, 0.0]
        for i, v in enumerate(cvals):
            r.emit("gpsimd", lambda h, i=i, v=v: h.memset(sb["consts"][:, i:i + 1], v))
            nc.const_aps.register(F32, v, sb["consts"][:, i:i + 1])
        r.emit("gpsimd", lambda h: h.memset(sb["ones_rowb"][:, :], 1.0))
        r.emit("gpsimd", lambda h: h.memset(sb["ones_rowf"][:, :], 1.0))
        r.emit("gpsimd", lambda h: h.memset(sb["ones_colf"][:, :], 1.0))
        r.emit("gpsimd", lambda h: h.memset(sb["ones_colb"][:, :], 1.0))
        r.emit("gpsimd", lambda h: h.memset(sb["idb"][:, :], 1.0))
        r.emit("gpsimd", lambda h: h.affine_select(
            sb["idb"][:, :], sb["idb"][:, :], pattern=[[1, 128]],
            compare_op=ALU.is_equal, fill=0.0, base=0, channel_multiplier=-1))

        ini.load("sync", sb["b1_pp"][:, :], dr["b1_pp"])
        ini.load("gpsimd", sb["bqkv_row"][0:1, :], dr["bqkv_row"])
        ini.load("sync", sb["scale_pp"][:, :], dr["scale_pp"])
        ini.load("sync", sb["cvw"][:, :], dr["convw_pp"])
        ini.load("sync", sb["cvb"][:, :], dr["convb_pp"])
        ini.load("sync", sb["dtb8"][0:8, :], dr["dtb_col"])
        ini.load("sync", sb["dpp"][:, :], dr["d_pp"])
        ini.load("sync", sb["nw"][:, :], dr["normw_pp"])
        ini.load("sync", sb["acol"][0:8, 1:2], dr["alog_col"])
        r.emit("gpsimd", lambda h: h.memset(sb["idf"][:, :], 1.0))
        r.emit("gpsimd", lambda h: h.affine_select(
            sb["idf"][:, :], sb["idf"][:, :], pattern=[[1, 128]],
            compare_op=ALU.is_equal, fill=0.0, base=0, channel_multiplier=-1))
        r.emit("gpsimd", lambda h: h.memset(sb["ngt"][:, :], 0.0))
        r.emit("gpsimd", lambda h: h.affine_select(
            sb["ngt"][:, :], sb["ngt"][:, :], pattern=[[1, 128]],
            compare_op=ALU.is_ge, fill=-1e9, base=0, channel_multiplier=-1))
        r.emit("gpsimd", lambda h: h.memset(sb["z8"][0:8, :], 0.0))
        xcur_ready[:] = [r.dma("sync", x_cur, x0, "xs")]
        WSHAPES = {"w1t": [1024, 4096], "w2t": [4096, 1024],
                   "wqkvt": [1024, 3072], "wprojt": [1024, 1024],
                   "wfr": [1024, 1024], "wfi": [1024, 1024],
                   "w_in": [1024, 4232], "w_out": [2048, 1024],
                   "w_nl": [1024, 1024]}
        wbf = {}
        castm = {}
        CAST_SEMS = ["s6g", "s7g", "s8g", "s9g", "usg", "msg", "qkg",
                     "uhg", "vhg"]
        for i, wn in enumerate(("w1t", "w2t", "wqkvt", "wprojt", "wfr", "wfi",
                                "w_in", "w_out", "w_nl")):
            wbf[wn] = dscr(wn + "_bf", WSHAPES[wn], BF16)
            castm[wn] = r.dma("gpsimd", wbf[wn], dr[wn], CAST_SEMS[i])
        INIT = [r.mark("gp"), r.mark("ini"), r.mark("inig")]


        def emit_ln(t0, tw, xl_base, xl_stride, keep_x=True):
            if not hasattr(emit_ln, "dump_this"):
                emit_ln.dump_this = False
            st = sb["stat"]
            ST_TMP, ST_MEAN, ST_VAR, ST_RSTD, ST_NM = 0, 512, 1024, 1536, 1024
            xv = bufres["xt"].load(
                "sync", sb["xt"][:, 0:8 * tw].rearrange("p (c w) -> p c w", c=8),
                x_cur.rearrange("(c p) t -> p c t", p=128)[:, :, t0:t0 + tw],
                wait=INIT + list(xcur_ready))
            for c in range(8):
                r.emit("tensor", lambda h, c=c, tw=tw: h.matmul(
                    psum[6][0:1, 0:tw], sb["ones_colf"][:, 0:1],
                    sb["xt"][:, c * tw:(c + 1) * tw],
                    start=(c == 0), stop=(c == 7)),
                    wait=[xv, bankres[6].wait()] if c == 0 else ())
            sqmm = {}
            for c in range(8):
                w = [xv] if c == 0 else []
                if c >= 2:
                    w.append(sqmm[c - 2])
                if c == 0:
                    w.append(bufres["xsq"].wait())
                sqm = r.emit("scalar", lambda h, c=c, tw=tw: h.activation(
                    out=sb["xsq"][:, (c % 2) * tw:(c % 2) * tw + tw],
                    in_=sb["xt"][:, c * tw:(c + 1) * tw], func=AF.Square),
                    wait=w)
                smm = r.emit("tensor", lambda h, c=c, tw=tw: h.matmul(
                    psum[7][0:1, 0:tw], sb["ones_colf"][:, 0:1],
                    sb["xsq"][:, (c % 2) * tw:(c % 2) * tw + tw],
                    start=(c == 0), stop=(c == 7)),
                    wait=[sqm] + ([bankres[7].wait()] if c == 0 else []),
                    inc=("mm", 1))
                sqmm[c] = smm
            bufres["xsq"].set(smm)
            r.emit("vector", lambda h, tw=tw: h.tensor_scalar_mul(
                st[0:1, ST_MEAN:ST_MEAN + tw], psum[6][0:1, 0:tw], 1.0 / 1024.0),
                wait=[smm])
            r.emit("vector", lambda h, tw=tw: h.tensor_scalar_mul(
                st[0:1, ST_VAR:ST_VAR + tw], psum[7][0:1, 0:tw], 1.0 / 1024.0))
            r.emit("vector", lambda h, tw=tw: h.tensor_mul(
                st[0:1, ST_TMP:ST_TMP + tw], st[0:1, ST_MEAN:ST_MEAN + tw],
                st[0:1, ST_MEAN:ST_MEAN + tw]))
            r.emit("vector", lambda h, tw=tw: h.tensor_sub(
                st[0:1, ST_VAR:ST_VAR + tw], st[0:1, ST_VAR:ST_VAR + tw],
                st[0:1, ST_TMP:ST_TMP + tw]))
            dvm = r.mark("dv")
            r.emit("scalar", lambda h, tw=tw: h.activation(
                out=st[0:1, ST_TMP:ST_TMP + tw], in_=st[0:1, ST_VAR:ST_VAR + tw],
                func=AF.Ln, bias=1e-5), wait=[dvm])
            r.emit("scalar", lambda h, tw=tw: h.activation(
                out=st[0:1, ST_RSTD:ST_RSTD + tw], in_=st[0:1, ST_TMP:ST_TMP + tw],
                func=AF.Exp, scale=-0.5))
            acm = r.mark("ac")
            r.emit("vector", lambda h, tw=tw: h.tensor_mul(
                st[0:1, ST_NM:ST_NM + tw], st[0:1, ST_MEAN:ST_MEAN + tw],
                st[0:1, ST_RSTD:ST_RSTD + tw]), wait=[acm])
            r.emit("vector", lambda h, tw=tw: h.tensor_scalar_mul(
                st[0:1, ST_NM:ST_NM + tw], st[0:1, ST_NM:ST_NM + tw], -1.0))
            dvm2 = r.mark("dv")
            r.emit("tensor", lambda h, tw=tw: h.matmul(
                psum[6][:, 0:tw], sb["ones_rowf"][0:1, :],
                st[0:1, ST_RSTD:ST_RSTD + tw], start=True, stop=True),
                wait=[dvm2])
            bmm = r.emit("tensor", lambda h, tw=tw: h.matmul(
                psum[7][:, 0:tw], sb["ones_rowf"][0:1, :],
                st[0:1, ST_NM:ST_NM + tw], start=True, stop=True),
                inc=("mm", 1))
            lastdv = None
            for c in range(8):
                r.emit("vector", lambda h, c=c, tw=tw: h.tensor_mul(
                    sb["xsq"][:, 0:tw], sb["xt"][:, c * tw:(c + 1) * tw],
                    psum[6][:, 0:tw]),
                    wait=[bmm, bufres["xlt"].wait()] if c == 0 else ())
                lastdv = r.emit("vector", lambda h, c=c, tw=tw, xl_base=xl_base,
                                xl_stride=xl_stride: h.tensor_add(
                    sb["xlt"][:, xl_base + c * xl_stride:xl_base + c * xl_stride + tw],
                    sb["xsq"][:, 0:tw], psum[7][:, 0:tw]))
            if "ln_ps" in dbg and emit_ln.dump_this:
                emit_ln.dump_this = False
                pe6 = r.emit("scalar", lambda h, tw=tw: h.activation(
                    out=sb["xsq"][:, 0:tw], in_=psum[6][:, 0:tw],
                    func=AF.Identity), wait=[lastdv])
                pe7 = r.emit("scalar", lambda h, tw=tw: h.activation(
                    out=sb["xsq"][:, tw:2 * tw], in_=psum[7][:, 0:tw],
                    func=AF.Identity))
                pdm = r.dma("sync", dbg["ln_ps"], sb["xsq"][:, 0:2 * tw], "s4",
                            wait=[pe7])
                lastdv = r.emit("vector", lambda h: h.tensor_copy(
                    sb["xsq"][:, 0:1], sb["xsq"][:, 0:1]), wait=[pdm, pe6])
                bufres["xsq"].set(lastdv)
            bankres[6].set(lastdv)
            bankres[7].set(lastdv)
            bufres["xt"].set(lastdv)
            return lastdv, xv

        def phase_ffn0():
            ptb = psum[5][:, :].bitcast(BF16)
            brm = bufres["brow"].load("gpsimd", sb["brow"][0:1, :], dr["b2_row"])

            def lora_prep(lnm):
                tkm = None
                for c in range(8):
                    tm = r.emit("tensor", lambda h, c=c: h.transpose(
                        ptb[0:16, 0:128], sb["xlt"][:, c * 1024 + 1:c * 1024 + 17],
                        sb["idb"][:, :]),
                        wait=[lnm, bankres[5].wait()] if c == 0 else [bankres[5].wait()],
                        inc=("mm", 1))
                    tkm = r.emit("scalar", lambda h, c=c: h.activation(
                        out=sb["tok"][0:16, c * 128:(c + 1) * 128],
                        in_=ptb[0:16, 0:128], func=AF.Identity),
                        wait=[tm, bufres["tok"].wait()])
                    bankres[5].set(tkm)
                som = r.dma("sync", eo_hbm, sb["tok"][0:16, :], "eo",
                            wait=[tkm, r.mark("evs"), r.mark("e2t")])
                bufres["tok"].set(som)
                erm = bufres["evs"].load(
                    "sync", sb["evs"][0:1, :].rearrange("o (v d) -> o v d", v=4),
                    eo_hbm[0:8:2, :].rearrange("v d -> () v d"), wait=[som])
                e2m = None
                for ip in range(4):
                    e2m = bufres["e2t"].load(
                        "sync", sb["e2t"][0:4, ip * 256:(ip + 1) * 256],
                        eo_hbm[8 + 2 * ip:9 + 2 * ip, :].rearrange(
                            "o (q j) -> (o j) q", j=4),
                        wait=[som], allow_slow_non_contiguous=True)
                evm_ = r.emit("vector", lambda h: h.tensor_scalar_mul(
                    sb["evs"][0:1, :], sb["evs"][0:1, :],
                    sb["scale_pp"][0:1, 0:1]), wait=[erm])
                _d = dbg_dump(r, "l_evs", sb["evs"][0:1, :], [evm_])
                evm_ = _d or evm_
                _d = dbg_dump(r, "l_e2t", sb["e2t"][0:4, :], [e2m])
                e2m = _d or e2m

                om = None
                for c in range(8):
                    om = r.emit("vector", lambda h, c=c: h.tensor_copy(
                        sb["odd"][:, c * 8:(c + 1) * 8],
                        sb["xlt"][:, c * 1024 + 2:c * 1024 + 18:2]),
                        wait=[bufres["odd"].wait()] if c == 0 else ())
                _d = dbg_dump(r, "l_odd", sb["odd"][:, :], [om])
                om = _d or om
                return evm_, om, e2m

            lora = [None, None, None]
            for tb in range(4):
                t0 = tb * 1024
                if tb == 1:
                    emit_ln.dump_this = True
                ln0, _ = emit_ln(t0, 512, 0, 1024)
                ln1, _ = emit_ln(t0 + 512, 512, 512, 1024)
                if tb == 0:
                    lora[0], lora[1], lora[2] = lora_prep(ln0)
                if tb == 1:
                    _d = dbg_dump(r, "l_xlt1", sb["xlt"][:, :], [ln0, ln1])
                    if _d is not None:
                        ln0 = ln1 = _d
                    _d = dbg_dump(r, "l_xt1", sb["xt"][:, :], [ln1])
                    _d = dbg_dump(r, "l_stat1", sb["stat"][0:1, :], [ln1])
                um = None
                for half in range(2):
                    for c in range(8):
                        r.emit("tensor", lambda h, c=c, half=half: h.matmul(
                            psum[4][0:4, 0:512],
                            sb["odd"][:, c * 8:c * 8 + 4],
                            sb["xlt"][:, c * 1024 + half * 512:c * 1024 + half * 512 + 512],
                            start=(c == 0), stop=(c == 7)),
                            wait=[lora[1], ln0, ln1, bankres[4].wait()] if c == 0 else (),
                            inc=("mm", 1) if c == 7 else None)
                    um = r.emit("scalar", lambda h, half=half: h.activation(
                        out=sb["usb"][0:4, half * 512:half * 512 + 512],
                        in_=psum[4][0:4, 0:512], func=AF.Identity),
                        wait=[r.mark("mm"), bufres["usb"].wait()])
                    bankres[4].set(um)
                usm = r.dma("sync", ul_hbm, sb["usb"][0:4, :], "uh",
                            wait=[um, bufres["uh"].wait()])
                bufres["usb"].set(usm)
                urm = bufres["urow"].load(
                    "sync", sb["urow"][0:1, :],
                    ul_hbm.rearrange("i t -> () (i t)"), wait=[usm])
                if tb == 1:
                    _d = dbg_dump(r, "l_urow", sb["urow"][0:1, 0:4096], [urm])
                    urm = _d or urm
                bufres["uh"].set(urm)
                hm = None
                w1last = None
                for ht in range(32):
                    wbuf = "wa" if ht % 2 == 0 else "wb"
                    wm = bufres[wbuf].load(
                        weng(wbuf), sb[wbuf][:, 0:1024].rearrange("p (k m) -> p k m", k=8),
                        wbf["w1t"].rearrange("(k p) m -> p k m", p=128)
                        [:, :, ht * 128:(ht + 1) * 128], wait=INIT + [castm["w1t"]])
                    for half in range(2):
                        bk = (ht * 2 + half) % 4
                        for kch in range(8):
                            r.emit("tensor", lambda h, kch=kch, half=half, bk=bk,
                                   wbuf=wbuf: h.matmul(
                                psum[bk][:, :],
                                sb[wbuf][:, kch * 128:(kch + 1) * 128],
                                sb["xlt"][:, kch * 1024 + half * 512:
                                          kch * 1024 + half * 512 + 512],
                                start=(kch == 0), stop=False),
                                wait=([wm, bankres[bk].wait()] if kch == 0 else []) +
                                     ([ln0, ln1] if (kch == 0 and ht == 0) else []))
                        gl = r.emit("tensor", lambda h, ht=ht, half=half, bk=bk: h.matmul(
                            psum[bk][:, :],
                            sb["evs"][0:1, (ht // 8) * 1024 + (ht % 8) * 128:
                                      (ht // 8) * 1024 + (ht % 8) * 128 + 128],
                            sb["urow"][0:1, (ht // 8) * 1024 + half * 512:
                                       (ht // 8) * 1024 + half * 512 + 512],
                            start=False, stop=True),
                            wait=[urm, lora[0]] if ht == 0 and half == 0 else (),
                            inc=("mm", 1))
                        w1last = gl
                        hm = r.emit("scalar", lambda h, ht=ht, half=half, bk=bk:
                                    h.activation(
                            out=sb["hsb"][:, ht * 1024 + half * 512:
                                          ht * 1024 + half * 512 + 512],
                            in_=psum[bk][:, :], func=AF.Gelu,
                            bias=sb["b1_pp"][:, ht:ht + 1]),
                            wait=[gl] + ([bufres["hsb"].wait()] if ht == 0 and half == 0 else []))
                        bankres[bk].set(hm)
                    bufres[wbuf].set(w1last)
                bufres["xlt"].set(w1last)
                if tb == 1:
                    _d = dbg_dump(r, "l_h", sb["hsb"][:, 0:4096], [hm])
                    hm = _d or hm
                vm = None
                for j in range(4):
                    for half in range(2):
                        for c in range(8):
                            r.emit("tensor", lambda h, c=c, j=j, half=half: h.matmul(
                                psum[4][0:4, 0:512],
                                sb["odd"][:, c * 8 + 4:c * 8 + 8],
                                sb["hsb"][:, (j * 8 + c) * 1024 + half * 512:
                                          (j * 8 + c) * 1024 + half * 512 + 512],
                                start=(c == 0), stop=(c == 7)),
                                wait=[hm, bankres[4].wait()] if c == 0 else (),
                                inc=("mm", 1) if c == 7 else None)
                        vm = r.emit("scalar", lambda h, j=j, half=half: h.activation(
                            out=sb["vsb"][0:4, j * 1024 + half * 512:
                                          j * 1024 + half * 512 + 512],
                            in_=psum[4][0:4, 0:512], func=AF.Identity,
                            scale=sb["scale_pp"][0:4, 0:1]),
                            wait=[r.mark("mm")] +
                                 ([bufres["vsb"].wait()] if j == 0 and half == 0 else []))
                        bankres[4].set(vm)
                vsm = r.dma("sync", v_hbm, sb["vsb"][0:4, :], "vh",
                            wait=[vm, bufres["vh"].wait()])
                bufres["vsb"].set(vsm)
                vtm = None
                for ip in range(4):
                    vtm = bufres["vsb"].load(
                        "sync", sb["vsb"][0:4, ip * 1024:(ip + 1) * 1024],
                        v_hbm[ip:ip + 1, :].rearrange("o (j t) -> (o j) t", j=4),
                        wait=[vsm])
                if tb == 1:
                    _d = dbg_dump(r, "l_vsb", sb["vsb"][0:4, :], [vtm])
                    vtm = _d or vtm
                bufres["vh"].set(vtm)
                w2last = None
                for ot in range(8):
                    wbuf = ("wa", "wc") if ot % 2 == 0 else ("wb", "wd")
                    wms = []
                    for gi, wb_ in enumerate(wbuf):
                        wms.append(bufres[wb_].load(
                            weng(wb_),
                            sb[wb_][:, 0:2048].rearrange("p (k m) -> p k m", k=16),
                            wbf["w2t"].rearrange("(k p) m -> p k m", p=128)
                            [:, gi * 16:(gi + 1) * 16, ot * 128:(ot + 1) * 128],
                            wait=INIT + [castm["w2t"]]))
                    ip = ot // 2
                    qh = ot % 2
                    for half in range(2):
                        bk = (ot * 2 + half) % 4
                        for kch in range(32):
                            wb_ = wbuf[kch // 16]
                            r.emit("tensor", lambda h, kch=kch, half=half, bk=bk,
                                   wb_=wb_: h.matmul(
                                psum[bk][:, :],
                                sb[wb_][:, (kch % 16) * 128:(kch % 16 + 1) * 128],
                                sb["hsb"][:, kch * 1024 + half * 512:
                                          kch * 1024 + half * 512 + 512],
                                start=(kch == 0), stop=False),
                                wait=(wms + [bankres[bk].wait()]) if kch == 0 else ())
                        r.emit("tensor", lambda h, ip=ip, qh=qh, half=half, bk=bk: h.matmul(
                            psum[bk][:, :],
                            sb["e2t"][0:4, ip * 256 + qh * 128:ip * 256 + qh * 128 + 128],
                            sb["vsb"][0:4, ip * 1024 + half * 512:
                                      ip * 1024 + half * 512 + 512],
                            start=False, stop=False),
                            wait=[vtm, lora[2]] if ot == 0 and half == 0 else ())
                        gl = r.emit("tensor", lambda h, ot=ot, bk=bk: h.matmul(
                            psum[bk][:, :],
                            sb["brow"][0:1, ot * 128:(ot + 1) * 128],
                            sb["ones_rowb"][0:1, 0:512],
                            start=False, stop=True),
                            wait=[brm] if ot == 0 else (), inc=("mm", 1))
                        w2last = gl
                        xrb = "xr0" if half == 0 else "xr1"
                        xv = bufres[xrb].load(
                            "sync", sb["xr"][:, half * 512:half * 512 + 512],
                            x_cur.rearrange("(c p) t -> p c t", p=128)
                            [:, ot:ot + 1, t0 + half * 512:t0 + half * 512 + 512]
                            .rearrange("p o t -> p (o t)"),
                            wait=list(xcur_ready))
                        sm = r.emit("vector", lambda h, half=half, bk=bk:
                                    h.scalar_tensor_tensor(
                            sb["xr"][:, half * 512:half * 512 + 512],
                            psum[bk][:, :], 0.5,
                            sb["xr"][:, half * 512:half * 512 + 512],
                            op0=ALU.mult, op1=ALU.add), wait=[gl, xv])
                        bankres[bk].set(sm)
                        stm = r.dma("sync",
                                    x_cur.rearrange("(c p) t -> p c t", p=128)
                                    [:, ot:ot + 1, t0 + half * 512:t0 + half * 512 + 512]
                                    .rearrange("p o t -> p (o t)"),
                                    sb["xr"][:, half * 512:half * 512 + 512],
                                    xrb, wait=[sm])
                        bufres[xrb].set(stm)
                    for wb_ in wbuf:
                        bufres[wb_].set(w2last)
                bufres["hsb"].set(w2last)
                bufres["vsb"].set(w2last)
                bufres["urow"].set(w1last)
            xcur_ready[:] = [r.mark("xr0"), r.mark("xr1")]
            bufres["brow"].set(r.mark("mm"))
            bufres["odd"].set(r.mark("mm"))
            bufres["e2t"].set(r.mark("mm"))
            bufres["evs"].set(r.mark("mm"))

        def phase_attn():
            XS0, XS1 = 0, 512
            for tb in range(4 if "A" in ATTN_PARTS else 0):
                t0 = tb * 1024
                ln0, _ = emit_ln(t0, 512, 0, 1024)
                ln1, _ = emit_ln(t0 + 512, 512, 512, 1024)

                def qkv_group(mt, bk, wm, half, wbuf, extra_wait=()):
                    for kch in range(8):
                        r.emit("tensor", lambda h, kch=kch, bk=bk, half=half,
                               wbuf=wbuf: h.matmul(
                            psum[bk][:, :],
                            sb[wbuf][:, kch * 128:(kch + 1) * 128],
                            sb["xlt"][:, kch * 1024 + half * 512:
                                      kch * 1024 + half * 512 + 512],
                            start=(kch == 0), stop=False),
                            wait=([wm, bankres[bk].wait(), ln0, ln1] +
                                  list(extra_wait)) if kch == 0 else ())
                    return r.emit("tensor", lambda h, mt=mt, bk=bk: h.matmul(
                        psum[bk][:, :],
                        sb["bqkv_row"][0:1, mt * 128:(mt + 1) * 128],
                        sb["ones_rowb"][0:1, 0:512],
                        start=False, stop=True), wait=INIT, inc=("mm", 1))

                def wload(mt, wbuf):
                    return bufres[wbuf].load(
                        weng(wbuf),
                        sb[wbuf][:, 0:1024].rearrange("p (k m) -> p k m", k=8),
                        wbf["wqkvt"].rearrange("(k p) m -> p k m", p=128)
                        [:, :, mt * 128:(mt + 1) * 128], wait=INIT + [castm["wqkvt"]])

                QK = (("q", 0, qf_hbm), ("k", 8, kf_hbm))
                iters = [(qi, i, half) for qi in range(2)
                         for i in range(4) for half in range(2)]
                csres = [None, None]

                def cs_load(n, t0=t0):
                    qi, i, half = iters[n]
                    slot = n % 2
                    m = r.dma(
                        "sync",
                        sb["big1"][:, slot * 1024:(slot + 1) * 1024]
                        .rearrange("p (u w) -> p u w", u=2),
                        dr["rope_csp"][:, i:i + 1, :,
                                       t0 + half * 512:t0 + half * 512 + 512]
                        .rearrange("u o p t -> p (u o) t"),
                        ("cs", "qk")[slot], wait=[csres[slot]])
                    return m

                csm_pref = {0: cs_load(0)}
                wlo = whi = None
                glo = ghi = None
                for n, (qi, i, half) in enumerate(iters):
                    qk, base, dsthbm = QK[qi]
                    slot = n % 2
                    cs_b = sb["big1"][:, slot * 1024:(slot + 1) * 1024]
                    if half == 0:
                        wblo = "wa" if i % 2 == 0 else "wc"
                        wbhi = "wb" if i % 2 == 0 else "wd"
                        wlo = wload(base + i, wblo)
                        whi = wload(base + i + 4, wbhi)
                    blo, bhi = (0, 1) if (i + half) % 2 == 0 else (2, 3)
                    glo = qkv_group(base + i, blo, wlo, half, wblo)
                    ghi = qkv_group(base + i + 4, bhi, whi, half, wbhi)
                    csm = csm_pref[n]
                    if n + 1 < len(iters):
                        csm_pref[n + 1] = cs_load(n + 1)
                    r.emit("vector", lambda h, blo=blo, cs_b=cs_b: h.tensor_mul(
                        sb["xsq"][:, XS0:XS0 + 512], psum[blo][:, :],
                        cs_b[:, 0:512]),
                        wait=[glo, ghi, csm, bufres["xsq"].wait()])
                    r.emit("vector", lambda h, bhi=bhi, cs_b=cs_b: h.tensor_mul(
                        sb["xsq"][:, XS1:XS1 + 512], psum[bhi][:, :],
                        cs_b[:, 512:1024]))
                    r.emit("vector", lambda h: h.tensor_sub(
                        sb["wc"][:, 1024:1536],
                        sb["xsq"][:, XS0:XS0 + 512],
                        sb["xsq"][:, XS1:XS1 + 512]),
                        wait=[bufres["wc"].wait()])
                    r.emit("vector", lambda h, bhi=bhi, cs_b=cs_b: h.tensor_mul(
                        sb["xsq"][:, XS0:XS0 + 512], psum[bhi][:, :],
                        cs_b[:, 0:512]))
                    rlast = r.emit("vector", lambda h, blo=blo, cs_b=cs_b: h.tensor_mul(
                        sb["xsq"][:, XS1:XS1 + 512], psum[blo][:, :],
                        cs_b[:, 512:1024]))
                    bankres[blo].set(rlast)
                    bankres[bhi].set(rlast)
                    csres[slot] = rlast
                    rlast = r.emit("vector", lambda h: h.tensor_add(
                        sb["wc"][:, 1536:2048],
                        sb["xsq"][:, XS0:XS0 + 512],
                        sb["xsq"][:, XS1:XS1 + 512]))
                    bufres["xsq"].set(rlast)
                    sm = r.dma("sync",
                               dsthbm.rearrange("(g r p) t -> p g r t",
                                                g=2, p=128)
                               [:, :, i, t0 + half * 512:t0 + half * 512 + 512],
                               sb["wc"][:, 1024:2048].rearrange(
                                   "p (g w) -> p g w", g=2),
                               "wc", wait=[rlast])
                    bufres["wc"].set(sm)
                    if half == 1:
                        if wblo != "wc":
                            bufres[wblo].set(glo)
                        bufres[wbhi].set(ghi)
                bufres["big1"].set(r.mark("dv"))
                wcres = [None, None]
                pend = None

                def vflush(pend, tb=tb):
                    em, bk, dv_, half_, slot = pend
                    ptv = psum[bk][:, :].bitcast(BF16)
                    trm = None
                    for s4 in range(4):
                        trm = r.emit("tensor", lambda h, s4=s4, ptv=ptv, slot=slot:
                               h.transpose(
                            ptv[:, s4 * 128:(s4 + 1) * 128],
                            sb["wc"][:, slot * 512 + s4 * 128:
                                     slot * 512 + (s4 + 1) * 128],
                            sb["idb"][:, :]),
                            wait=[em] if s4 == 0 else (),
                            inc=("mm", 1) if s4 == 3 else None)
                    wcres[slot] = trm
                    evm2 = r.emit("scalar", lambda h, ptv=ptv, slot=slot: h.activation(
                        out=sb["wd"][:, slot * 512:slot * 512 + 512],
                        in_=ptv[:, 0:512], func=AF.Identity),
                        wait=[trm, bufres["wd"].wait()])
                    bankres[bk].set(evm2)
                    sm = r.dma("sync",
                               vt_hbm.rearrange("(g p) d -> p g d", p=128)
                               [:, tb * 8 + half_ * 4:tb * 8 + half_ * 4 + 4,
                                dv_ * 128:(dv_ + 1) * 128],
                               sb["wd"][:, slot * 512:slot * 512 + 512]
                               .rearrange("p (s w) -> p s w", s=4),
                               "wd", wait=[evm2])
                    bufres["wd"].set(sm)

                wm = None
                vwb = None
                for n in range(16):
                    dv, half = n // 2, n % 2
                    if half == 0:
                        vwb = "wa" if dv % 2 == 0 else "wb"
                        wm = wload(16 + dv, vwb)
                    bk = 4 if n % 2 == 0 else 5
                    gm = qkv_group(16 + dv, bk, wm, half, vwb)
                    if half == 1:
                        bufres[vwb].set(gm)
                    slot = n % 2
                    em = r.emit("scalar", lambda h, bk=bk, slot=slot: h.activation(
                        out=sb["wc"][:, slot * 512:slot * 512 + 512],
                        in_=psum[bk][:, :], func=AF.Identity),
                        wait=[gm, wcres[slot], bufres["wc"].wait()
                              if n == 0 else None])
                    pend_new = (em, bk, dv, half, slot)
                    if pend is not None:
                        vflush(pend)
                    pend = pend_new
                vflush(pend)
                bufres["wc"].set(max(w for w in wcres if w is not None))
            qkms = [r.mark("wc"), r.mark("wd")]
            dbg_dump(r, "a_qf", qf_hbm, qkms)
            dbg_dump(r, "a_kf", kf_hbm, qkms)
            dbg_dump(r, "a_vt", vt_hbm, qkms)
            if "B" not in ATTN_PARTS:
                vlm = None
            else:
                vlm = bufres["hsb"].load(
                "sync", sb["hsb"][:, :].rearrange("p (tt d) -> p tt d", tt=32),
                vt_hbm.rearrange("(tt p) d -> p tt d", p=128), wait=qkms)
            M_, L_, NGM, RS = 0, 1, 2, 3
            KMX, KSM = 8, 16
            prev_mask_use = None
            prev_stdm = None
            prev_pem = None
            prev_oev = None
            for tq in range(32 if "B" in ATTN_PARTS else 0):
                qm = bufres["wc"].load(
                    "sync", sb["wc"][:, 0:1024].rearrange("p (c m) -> p c m", c=8),
                    qf_hbm.rearrange("(c p) t -> p c t", p=128)
                    [:, :, tq * 128:(tq + 1) * 128], wait=qkms)
                dkb = tq // 4
                r.emit("gpsimd", lambda h: h.memset(sb["mks"][:, :], 0.0),
                       wait=[prev_mask_use])
                mkm = r.emit("gpsimd", lambda h, tq=tq: h.affine_select(
                    sb["mks"][:, :], sb["mks"][:, :], pattern=[[-1, 512]],
                    compare_op=ALU.is_ge, fill=-1e9,
                    base=(tq % 4) * 128, channel_multiplier=1))

                def emit_S(kb, qm=qm, dkb=dkb, mkm=mkm):
                    kbuf = "kb0" if kb % 2 == 0 else "kb1"
                    koff = 0 if kb % 2 == 0 else 4608
                    km = bufres[kbuf].load(
                        "sync",
                        sb["big0"][:, koff:koff + 4096].rearrange(
                            "p (c w) -> p c w", c=8),
                        kf_hbm.rearrange("(c p) t -> p c t", p=128)
                        [:, :, kb * 512:(kb + 1) * 512],
                        wait=qkms + [bufres["big0"].wait()])
                    bs = 0 if kb % 2 == 0 else 1
                    for kch in range(8):
                        r.emit("tensor", lambda h, kch=kch, koff=koff, bs=bs: h.matmul(
                            psum[bs][:, :],
                            sb["wc"][:, kch * 128:(kch + 1) * 128],
                            sb["big0"][:, koff + kch * 512:koff + (kch + 1) * 512],
                            start=(kch == 0), stop=(kch == 7)),
                            wait=[km, qm, bankres[bs].wait()] if kch == 0 else (),
                            inc=("mm", 1) if kch == 7 else None)
                    bufres[kbuf].set(r.mark("mm"))
                    smm = r.mark("mm")
                    if kb == dkb:
                        smm = r.emit("vector", lambda h, bs=bs: h.tensor_add(
                            psum[bs][:, :], psum[bs][:, :], sb["mks"][:, :]),
                            wait=[smm, mkm])
                    return smm

                for kb in range(dkb + 1):
                    smm = emit_S(kb)
                    if kb == dkb:
                        prev_mask_use = smm
                    bs = 0 if kb % 2 == 0 else 1
                    tmx = r.emit("vector", lambda h, bs=bs, kb=kb: h.reduce_max(
                        sb["xsq"][:, KMX + kb:KMX + kb + 1], psum[bs][:, :],
                        axis=AX.X),
                        wait=[smm, prev_stdm] +
                             ([prev_pem] if kb == 0 else []))
                    sev = r.emit("scalar", lambda h, bs=bs, kb=kb: h.activation(
                        out=sb["big1"][:, kb * 512:(kb + 1) * 512],
                        in_=psum[bs][:, :], func=AF.Identity),
                        wait=[smm, tmx] +
                             ([bufres["big1"].wait()] if kb == 0 else []))
                    bankres[bs].set(sev)
                gmx = r.emit("vector", lambda h, dkb=dkb: h.reduce_max(
                    sb["xsq"][:, M_:M_ + 1], sb["xsq"][:, KMX:KMX + dkb + 1],
                    axis=AX.X))
                ngm = r.emit("vector", lambda h: h.tensor_scalar_mul(
                    sb["xsq"][:, NGM:NGM + 1], sb["xsq"][:, M_:M_ + 1], -1.0))
                pem = None
                pvm = {}
                for kb in range(dkb + 1):
                    pem = r.emit("scalar", lambda h, kb=kb: h.activation(
                        out=sb["wd"][:, 0:512],
                        in_=sb["big1"][:, kb * 512:(kb + 1) * 512],
                        func=AF.Exp, bias=sb["xsq"][:, NGM:NGM + 1],
                        accum_out=sb["xsq"][:, KSM + kb:KSM + kb + 1]),
                        wait=[ngm] + ([bufres["wd"].wait()] if kb == 0 else []))
                    ptp = psum[4][:, :].bitcast(BF16)
                    trm = None
                    for s4 in range(4):
                        trm = r.emit("tensor", lambda h, s4=s4, ptp=ptp: h.transpose(
                            ptp[:, s4 * 128:(s4 + 1) * 128],
                            sb["wd"][:, s4 * 128:(s4 + 1) * 128],
                            sb["idb"][:, :]),
                            wait=[pem, bankres[4].wait()] if s4 == 0 else (),
                            inc=("mm", 1) if s4 == 3 else None)
                    ptev = r.emit("scalar", lambda h, ptp=ptp: h.activation(
                        out=sb["wd"][:, 512:1024], in_=ptp[:, 0:512],
                        func=AF.Identity), wait=[trm])
                    bankres[4].set(ptev)
                    for dh in range(2):
                        for kc in range(4):
                            pvm[dh] = r.emit("tensor",
                                   lambda h, dh=dh, kc=kc, kb=kb, dkb=dkb: h.matmul(
                                psum[2 + dh][:, :],
                                sb["wd"][:, 512 + kc * 128:512 + (kc + 1) * 128],
                                sb["hsb"][:, (kb * 4 + kc) * 1024 + dh * 512:
                                          (kb * 4 + kc) * 1024 + dh * 512 + 512],
                                start=(kb == 0 and kc == 0),
                                stop=(kb == dkb and kc == 3)),
                                wait=([ptev, vlm] +
                                      ([bankres[2 + dh].wait()] if kb == 0 else []))
                                     if kc == 0 else (),
                                inc=("mm", 1) if kc == 3 else None)
                prev_pem = pem
                bufres["wd"].set(pvm[1])
                lsum = r.emit("vector", lambda h, dkb=dkb: h.reduce_sum(
                    sb["xsq"][:, L_:L_ + 1], sb["xsq"][:, KSM:KSM + dkb + 1],
                    axis=AX.X), wait=[pem])
                stdm = None
                if "fl_st" in dbg:
                    stdm = r.dma("sync", dbg["fl_st"][tq], sb["xsq"][:, 0:8],
                                 ["s1", "s2", "s3"][tq % 3],
                                 wait=[r.mark("dv"), r.mark("ac")])
                    prev_stdm = stdm
                r.emit("vector", lambda h: h.reciprocal(
                    sb["xsq"][:, RS:RS + 1], sb["xsq"][:, L_:L_ + 1]),
                    wait=[lsum, stdm, prev_oev])
                rm = r.emit("vector", lambda h: h.tensor_scalar_mul(
                    sb["xsq"][:, RS:RS + 1], sb["xsq"][:, RS:RS + 1], 1.0 / 32.0))
                oev = None
                for dh in range(2):
                    oev = r.emit("scalar", lambda h, dh=dh: h.activation(
                        out=sb["wc"][:, dh * 512:dh * 512 + 512],
                        in_=psum[2 + dh][:, :], func=AF.Identity,
                        scale=sb["xsq"][:, RS:RS + 1]),
                        wait=[pvm[dh], rm])
                    bankres[2 + dh].set(oev)
                prev_oev = oev
                pto = psum[4][:, :].bitcast(BF16)
                trm = None
                for s8 in range(8):
                    trm = r.emit("tensor", lambda h, s8=s8, pto=pto: h.transpose(
                        pto[:, s8 * 128:(s8 + 1) * 128],
                        sb["wc"][:, s8 * 128:(s8 + 1) * 128],
                        sb["idb"][:, :]),
                        wait=[oev, bankres[4].wait()] if s8 == 0 else (),
                        inc=("mm", 1) if s8 == 7 else None)
                bufres["wc"].set(trm)
                otev = r.emit("scalar", lambda h, pto=pto: h.activation(
                    out=sb["wd"][:, 1024:2048], in_=pto[:, 0:1024],
                    func=AF.Identity), wait=[trm, bufres["wd"].wait()])
                bankres[4].set(otev)
                sm = r.dma("sync",
                           of_hbm.rearrange("(c p) t -> p c t", p=128)
                           [:, :, tq * 128:(tq + 1) * 128],
                           sb["wd"][:, 1024:2048].rearrange("p (c w) -> p c w", c=8),
                           "oo", wait=[otev])
                bufres["wd"].set(sm)
                bufres["big1"].set(pem)
            oom = r.mark("oo")
            dbg_dump(r, "a_of", of_hbm, [oom])
            bufres["hsb"].set(r.mark("mm"))
            bufres["big0"].set(r.mark("mm"))
            brm = bufres["brow"].load("gpsimd", sb["brow"][0:1, :], dr["bproj_row"])
            for tb in range(4 if "C" in ATTN_PARTS else 0):
                t0 = tb * 1024
                ofm = bufres["xlt"].load(
                    "sync", sb["xlt"][:, :].rearrange("p (c w) -> p c w", c=8),
                    of_hbm.rearrange("(c p) t -> p c t", p=128)
                    [:, :, t0:t0 + 1024], wait=[oom])
                for ot in range(8):
                    wbuf = "wa" if ot % 2 == 0 else "wb"
                    wm = bufres[wbuf].load(
                        weng(wbuf),
                        sb[wbuf][:, 0:1024].rearrange("p (k m) -> p k m", k=8),
                        wbf["wprojt"].rearrange("(k p) m -> p k m", p=128)
                        [:, :, ot * 128:(ot + 1) * 128], wait=INIT + [castm["wprojt"]])
                    glast = None
                    for half in range(2):
                        bk = (ot * 2 + half) % 4
                        for kch in range(8):
                            r.emit("tensor", lambda h, kch=kch, half=half, bk=bk,
                                   wbuf=wbuf: h.matmul(
                                psum[bk][:, :],
                                sb[wbuf][:, kch * 128:(kch + 1) * 128],
                                sb["xlt"][:, kch * 1024 + half * 512:
                                          kch * 1024 + half * 512 + 512],
                                start=(kch == 0), stop=False),
                                wait=[wm, ofm, bankres[bk].wait()] if kch == 0 else ())
                        gl = r.emit("tensor", lambda h, ot=ot, bk=bk: h.matmul(
                            psum[bk][:, :],
                            sb["brow"][0:1, ot * 128:(ot + 1) * 128],
                            sb["ones_rowb"][0:1, 0:512],
                            start=False, stop=True),
                            wait=[brm] if ot == 0 and half == 0 else (),
                            inc=("mm", 1))
                        glast = gl
                        xrb = "xr0" if half == 0 else "xr1"
                        xv = bufres[xrb].load(
                            "sync", sb["xr"][:, half * 512:half * 512 + 512],
                            x_cur.rearrange("(c p) t -> p c t", p=128)
                            [:, ot:ot + 1, t0 + half * 512:t0 + half * 512 + 512]
                            .rearrange("p o t -> p (o t)"),
                            wait=list(xcur_ready))
                        sm = r.emit("vector", lambda h, half=half, bk=bk: h.tensor_add(
                            sb["xr"][:, half * 512:half * 512 + 512],
                            sb["xr"][:, half * 512:half * 512 + 512],
                            psum[bk][:, :]), wait=[gl, xv])
                        bankres[bk].set(sm)
                        stm = r.dma("sync",
                                    x_cur.rearrange("(c p) t -> p c t", p=128)
                                    [:, ot:ot + 1, t0 + half * 512:t0 + half * 512 + 512]
                                    .rearrange("p o t -> p (o t)"),
                                    sb["xr"][:, half * 512:half * 512 + 512],
                                    xrb, wait=[sm])
                        bufres[xrb].set(stm)
                    bufres[wbuf].set(glast)
                bufres["xlt"].set(r.mark("mm"))
            xcur_ready[:] = [r.mark("xr0"), r.mark("xr1")]
            bufres["brow"].set(r.mark("mm"))

        def phase_mkM():
            wrm = bufres["big0"].load(
                "gpsimd", sb["big0"][:, 0:8192].rearrange("p (k n) -> p k n", k=8),
                wbf["wfr"].rearrange("(k p) n -> p k n", p=128), wait=INIT + [castm["wfr"]])
            wim = bufres["big1"].load(
                "gpsimd", sb["big1"][:, 0:8192].rearrange("p (k n) -> p k n", k=8),
                wbf["wfi"].rearrange("(k p) n -> p k n", p=128), wait=INIT + [castm["wfi"]])
            for (uname, terms) in (
                    ("u_hbm", (("big0", "c_fr", wrm), ("big1", "c_fin", wim))),
                    ("u2_hbm", (("big1", "c_fr", wim), ("big0", "c_fi", wrm)))):
                for nh in range(2):
                    first = True
                    for (wsb, fname, wmark) in terms:
                        for pch in range(8):
                            buf = "wa" if pch % 2 == 0 else "wb"
                            fv = bufres[buf].load(
                                "sync", sb[buf][:, 0:512],
                                dr[fname][pch * 128:(pch + 1) * 128,
                                          nh * 512:(nh + 1) * 512], wait=INIT)
                            for m in range(8):
                                w = [fv, wmark] if m == 0 else []
                                if first and pch == 0 and m == 0:
                                    w += [bankres[i].wait() for i in range(8)]
                                gm = r.emit("tensor",
                                       lambda h, pch=pch, m=m, buf=buf, wsb=wsb,
                                              first=first: h.matmul(
                                           psum[m][:, :],
                                           sb[wsb][:, pch * 1024 + m * 128:
                                                   pch * 1024 + (m + 1) * 128],
                                           sb[buf][:, 0:512],
                                           start=(first and pch == 0),
                                           stop=((not first) and pch == 7)),
                                       wait=w,
                                       inc=("mm", 1) if m == 7 else None)
                            bufres[buf].set(gm)
                        first = False
                    mmv = r.mark("mm")
                    evm = None
                    for m in range(8):
                        evm = r.emit("scalar", lambda h, m=m: h.activation(
                            out=sb["xt"][:, m * 512:(m + 1) * 512],
                            in_=psum[m][:, :], func=AF.Identity),
                            wait=[mmv, bufres["xt"].wait()] if m == 0 else ())
                    for m in range(8):
                        bankres[m].set(evm)
                    sv = r.dma("scalar",
                               dr[uname].rearrange("(mc p) n -> p mc n", p=128)
                               [:, :, nh * 512:(nh + 1) * 512],
                               sb["xt"][:, 0:8 * 512].rearrange(
                                   "p (mc n) -> p mc n", mc=8),
                               "us", wait=[evm])
                    bufres["xt"].set(sv)
            uq = r.mark("us")
            bufres["big0"].set(r.mark("mm"))
            bufres["big1"].set(r.mark("mm"))
            for nh in range(2):
                first = True
                for (uname, gname) in (("u_hbm", "c_gr"), ("u2_hbm", "c_gin")):
                    for kch in range(8):
                        ubuf = "wc" if kch % 2 == 0 else "wd"
                        gbuf = "wa" if kch % 2 == 0 else "wb"
                        uv = bufres[ubuf].load(
                            "gpsimd", sb[ubuf][:, 0:1024],
                            dr[uname][kch * 128:(kch + 1) * 128, :], wait=[uq])
                        gv = bufres[gbuf].load(
                            "sync", sb[gbuf][:, 0:512],
                            dr[gname][kch * 128:(kch + 1) * 128,
                                      nh * 512:(nh + 1) * 512], wait=INIT)
                        for m in range(8):
                            w = [uv, gv] if m == 0 else []
                            if first and kch == 0 and m == 0:
                                w += [bankres[i].wait() for i in range(8)]
                            gm = r.emit("tensor",
                                   lambda h, m=m, ubuf=ubuf, gbuf=gbuf, kch=kch,
                                          first=first: h.matmul(
                                       psum[m][:, :],
                                       sb[ubuf][:, m * 128:(m + 1) * 128],
                                       sb[gbuf][:, 0:512],
                                       start=(first and kch == 0),
                                       stop=((not first) and kch == 7)),
                                   wait=w,
                                   inc=("mm", 1) if m == 7 else None)
                        bufres[ubuf].set(gm)
                        bufres[gbuf].set(gm)
                    first = False
                mmv = r.mark("mm")
                evm = None
                for m in range(8):
                    evm = r.emit("scalar", lambda h, m=m, nh=nh: h.activation(
                        out=sb["big0"][:, m * 1024 + nh * 512:
                                       m * 1024 + (nh + 1) * 512],
                        in_=psum[m][:, :], func=AF.Identity),
                        wait=[mmv, bufres["big0"].wait()] if m == 0 else ())
                for m in range(8):
                    bankres[m].set(evm)
                dm = dbg_dump(r, f"mkm_big0_{nh}", sb["big0"][:, :], [r.mark("ac")])
                if dm is not None:
                    bufres["big0"].set(dm)
            acm = r.mark("ac")
            dbg_dump(r, "mkm_big0", sb["big0"][:, :], [acm])
            sv = r.dma("scalar", m_hbm.rearrange("(mc p) n -> p mc n", p=128),
                       sb["big0"][:, 0:8192].rearrange("p (mc n) -> p mc n", mc=8),
                       "ms", wait=[acm])
            bufres["big0"].set(sv)
            return r.mark("ms")

        def phase_mamba():
            ZB, YB = 0, 16384
            def xbc(ct):
                if ct < 8:
                    return sb["big0"], ct * 1040
                if ct < 16:
                    return sb["big1"], (ct - 8) * 1040
                return sb["cs"], 0
            R0, R1, R2, R3 = 0, 1024, 2048, 3072
            WCOL, CCOL, CA64, XS0 = 896, 904, 912, 0
            ptb1 = psum[1][:, :].bitcast(BF16)

            brm = bufres["brow"].load("gpsimd", sb["brow"][0:1, :], dr["bnl_row"])
            h0m = bufres["vsb"].load("sync", sb["hts"][0:64, :], dr["h0"])
            on64 = r.emit("gpsimd", lambda h: h.memset(
                sb["scr8"][64:65, 1024:1152], 1.0), wait=[bufres["vsb"].wait()])

            statem = [h0m]
            dbgm = {}
            dmh0 = dbg_dump(r, "mb_h0", sb["hts"][0:64, :], [h0m])
            if dmh0 is not None:
                statem[0] = dmh0
            bufres["lcr"].set(bufres["urow"].wait())
            bufres["car"].set(bufres["urow"].wait())
            am = r.emit("scalar", lambda h: h.activation(
                out=sb["acol"][0:8, 0:1], in_=sb["acol"][0:8, 1:2], func=AF.Exp),
                wait=INIT)
            am = r.emit("vector", lambda h: h.tensor_scalar_mul(
                sb["acol"][0:8, 0:1], sb["acol"][0:8, 0:1], -1.0), wait=[am])
            import os
            MDBG = os.environ.get("MAMBA_DBG", "")
            cvh = None
            for ct in range(0 if "nodiag" in MDBG else 17):
                dm = None
                for k in range(16):
                    dm = r.emit("vector", lambda h, ct=ct, k=k: h.tensor_scalar_mul(
                        sb["wc"][:, k * 128:(k + 1) * 128], sb["idb"][:, :],
                        sb["cvw"][:, ct * 16 + k:ct * 16 + k + 1]),
                        wait=[INIT[0], bufres["wc"].wait()] if k == 0 else ())
                cvh = r.dma("sync",
                            conv_hbm.rearrange("(q p) m -> p q m", p=128)
                            [:, ct * 16:(ct + 1) * 16, :],
                            sb["wc"][:, 0:2048].rearrange("p (k m) -> p k m", k=16),
                            "cvh", wait=[dm])
                bufres["wc"].set(cvh)
            hm0 = None
            for ct in range(0 if "nohist" in MDBG else 17):
                tb_, base = xbc(ct)
                hm0 = r.emit("gpsimd", lambda h, tb_=tb_, base=base: h.memset(
                    tb_[:, base:base + 15], 0.0),
                    wait=[bufres["big0"].wait(), bufres["big1"].wait(),
                          bufres["cs"].wait()] if ct == 0 else ())

            for tc in range(mamba_tc):
                t0 = tc * 1024
                lnm = {}
                for half in range(2):
                    lnm[half], _ = emit_ln(t0 + half * 512, 512, half * 512, 1024)
                    r.emit("vector", lambda h, half=half: h.tensor_copy(
                        sb["nmb"][0:1, half * 512:half * 512 + 512],
                        sb["stat"][0:1, 1024:1536]))
                    r.emit("vector", lambda h, half=half: h.tensor_scalar_add(
                        sb["scr8"][64:65, half * 512:half * 512 + 512],
                        sb["stat"][0:1, 1536:2048], 1.0),
                        wait=[on64, statem[0]] if tc == 0 and half == 0 else ())
                rowm = r.mark("dv")
                ipm = {}
                for mt in range(34):
                    mw = 128 if mt < 33 else 8
                    wbuf = "wa" if mt % 2 == 0 else "wb"
                    wm = bufres[wbuf].load(
                        weng(wbuf),
                        sb[wbuf][:, 0:8 * mw].rearrange("p (k m) -> p k m", k=8),
                        wbf["w_in"].rearrange("(k p) m -> p k m", p=128)
                        [:, :, mt * 128:mt * 128 + mw], wait=INIT + [castm["w_in"]])
                    glast = None
                    for half in range(2):
                        bk = (mt * 2 + half) % 4
                        for kch in range(8):
                            glast = r.emit("tensor", lambda h, kch=kch, half=half,
                                   bk=bk, wbuf=wbuf, mw=mw: h.matmul(
                                psum[bk][0:mw, :],
                                sb[wbuf][:, kch * mw:(kch + 1) * mw],
                                sb["xlt"][:, kch * 1024 + half * 512:
                                          kch * 1024 + half * 512 + 512],
                                start=(kch == 0), stop=(kch == 7)),
                                wait=[wm, bankres[bk].wait(), lnm[0], lnm[1]]
                                     if kch == 0 else (),
                                inc=("mm", 1) if kch == 7 else None)
                        if mt < 16:
                            em = r.emit("scalar", lambda h, mt=mt, half=half, bk=bk:
                                        h.activation(
                                out=sb["hsb"][:, ZB + mt * 1024 + half * 512:
                                              ZB + mt * 1024 + half * 512 + 512],
                                in_=psum[bk][:, :], func=AF.Identity),
                                wait=[glast] + ([bufres["hsb"].wait()]
                                                if mt == 0 and half == 0 else []))
                        elif mt < 33:
                            tb_, base = xbc(mt - 16)
                            em = r.emit("scalar", lambda h, tb_=tb_, base=base,
                                        half=half, bk=bk: h.activation(
                                out=tb_[:, base + 15 + half * 512:
                                        base + 15 + half * 512 + 512],
                                in_=psum[bk][:, :], func=AF.Identity),
                                wait=[glast, hm0])
                        else:
                            em = r.emit("scalar", lambda h, half=half, bk=bk:
                                        h.activation(
                                out=sb["xt"][0:8, R0 + half * 512:
                                             R0 + half * 512 + 512],
                                in_=psum[bk][0:8, :], func=AF.Identity,
                                bias=sb["dtb8"][0:8, 0:1]),
                                wait=[glast, bufres["xt"].wait()])
                        bankres[bk].set(em)
                        ipm[(mt, half)] = em
                    bufres[wbuf].set(glast)
                bufres["xlt"].set(r.mark("mm"))
                zdone = r.mark("ac")
                if tc == 0:
                    dbgm["zx"] = dbg_dump(r, "mb_zx", sb["hsb"][:, 0:16384], [zdone])
                for ct in range(17):
                    tb_, base = xbc(ct)
                    wbuf = "wa" if ct % 2 == 0 else "wb"
                    dm = bufres[wbuf].load(
                        "sync", sb[wbuf][:, 0:2048].rearrange("p (k m) -> p k m", k=16),
                        conv_hbm.rearrange("(q p) m -> p q m", p=128)
                        [:, ct * 16:(ct + 1) * 16, :], wait=[cvh])
                    gms = []
                    for th in range(2):
                        bk = 4 + th
                        for k in range(16):
                            gm = r.emit("tensor", lambda h, k=k, th=th, bk=bk,
                                   tb_=tb_, base=base, wbuf=wbuf: h.matmul(
                                psum[bk][:, :],
                                sb[wbuf][:, k * 128:(k + 1) * 128],
                                tb_[:, base + th * 512 + k:
                                    base + th * 512 + k + 512],
                                start=(k == 0), stop=(k == 15)),
                                wait=[dm, bankres[bk].wait(),
                                      ipm[(16 + ct, 0)], ipm[(16 + ct, 1)]]
                                     if k == 0 else (),
                                inc=("mm", 1) if k == 15 else None)
                        gms.append(gm)
                    tlm = r.emit("vector", lambda h, tb_=tb_, base=base: h.tensor_copy(
                        tb_[:, base:base + 15], tb_[:, base + 1024:base + 1039]),
                        wait=gms)
                    for th in range(2):
                        bk = 4 + th
                        em = r.emit("scalar", lambda h, th=th, bk=bk, tb_=tb_,
                                    base=base, ct=ct: h.activation(
                            out=tb_[:, base + 15 + th * 512:base + 15 + th * 512 + 512],
                            in_=psum[bk][:, :], func=AF.Silu,
                            bias=sb["cvb"][:, ct:ct + 1]),
                            wait=[tlm])
                        bankres[bk].set(em)
                    bufres[wbuf].set(em)
                    ipm[("c", ct)] = em
                if tc == 0:
                    dbg_dump(r, "mb_xbc0",
                             sb["big0"][:, 0:8 * 1040].rearrange(
                                 "p (s w) -> p s w", s=8)[:, :, 0:1039],
                             [ipm[("c", ct_)] for ct_ in range(8)])
                xr_ = sb["xt"][0:8, :]
                dt0 = [ipm[(33, 0)], ipm[(33, 1)], am]
                r.emit("vector", lambda h: h.tensor_scalar_max(
                    xr_[:, R1:R1 + 1024], xr_[:, R0:R0 + 1024], 0.0), wait=dt0)
                r.emit("scalar", lambda h: h.activation(
                    out=xr_[:, R2:R2 + 1024], in_=xr_[:, R0:R0 + 1024],
                    func=AF.Abs), wait=dt0)
                acm = r.mark("ac")
                r.emit("scalar", lambda h: h.activation(
                    out=xr_[:, R3:R3 + 1024], in_=xr_[:, R2:R2 + 1024],
                    func=AF.Exp, scale=-1.0))
                r.emit("scalar", lambda h: h.activation(
                    out=xr_[:, R2:R2 + 1024], in_=xr_[:, R3:R3 + 1024],
                    func=AF.Ln, bias=1.0))
                acm = r.mark("ac")
                r.emit("vector", lambda h: h.tensor_add(
                    xr_[:, R0:R0 + 1024], xr_[:, R1:R1 + 1024],
                    xr_[:, R2:R2 + 1024]), wait=[acm])
                r.emit("vector", lambda h: h.tensor_scalar_mul(
                    xr_[:, R1:R1 + 1024], xr_[:, R0:R0 + 1024],
                    sb["acol"][0:8, 0:1]))
                dvm = r.mark("dv")
                ldtm = r.emit("scalar", lambda h: h.activation(
                    out=xr_[:, R2:R2 + 1024], in_=xr_[:, R0:R0 + 1024],
                    func=AF.Ln), wait=[dvm])
                scm = None
                for sc in range(8):
                    scm = r.emit("vector", lambda h, sc=sc: h.tensor_tensor_scan(
                        xr_[:, R3 + sc * 128:R3 + (sc + 1) * 128],
                        xr_[:, R1 + sc * 128:R1 + (sc + 1) * 128],
                        sb["z8"][0:8, :], 0.0, op0=ALU.add, op1=ALU.add),
                        wait=[INIT[0]] if sc == 0 else ())
                cmbm = r.emit("vector", lambda h: h.tensor_sub(
                    xr_[:, R1:R1 + 1024], xr_[:, R2:R2 + 1024],
                    xr_[:, R3:R3 + 1024]), wait=[ldtm])
                cam = r.emit("scalar", lambda h: h.activation(
                    out=xr_[:, R2:R2 + 1024], in_=xr_[:, R3:R3 + 1024],
                    func=AF.Exp), wait=[cmbm])
                lcs = r.dma("sync", lc_hbm, xr_[:, R3:R3 + 1024], "uh",
                            wait=[cmbm, r.mark("lcr")])
                cas = r.dma("sync", ca_hbm, xr_[:, R2:R2 + 1024], "vh",
                            wait=[cam, r.mark("carg"), r.mark("hts")])
                wlm = None
                for sc in range(8):
                    wlm = r.emit("vector", lambda h, sc=sc: h.tensor_scalar_add(
                        xr_[:, R0 + sc * 128:R0 + (sc + 1) * 128],
                        xr_[:, R1 + sc * 128:R1 + (sc + 1) * 128],
                        xr_[:, R3 + sc * 128 + 127:R3 + sc * 128 + 128]),
                        wait=[ldtm] if sc == 0 else ())
                wm_ = r.emit("scalar", lambda h: h.activation(
                    out=xr_[:, R3:R3 + 1024], in_=xr_[:, R0:R0 + 1024],
                    func=AF.Exp), wait=[wlm, lcs])
                if tc == 0:
                    dbg_dump(r, "mb_dt", sb["xt"][0:8, :], [wm_, r.mark("dv")])
                WC_BW, WC_MT, WC_EE, WC_BT, WC_C64 = 0, 512, 640, 768, 832
                bct, cbase = xbc(16)
                c64m = r.emit("vector", lambda h, bct=bct, cbase=cbase: h.tensor_copy(
                    sb["wd"][0:64, WC_C64:WC_C64 + 1024],
                    bct[64:128, cbase + 15:cbase + 15 + 1024]),
                    wait=[ipm[("c", 16)], bufres["wd"].wait()])
                yevs = []
                for sc in range(8):
                    lm = bufres["lcr"].load(
                        "sync", sb["lcr"][0:1, :].rearrange("o (g w) -> o g w", g=8),
                        lc_hbm[:, sc * 128:(sc + 1) * 128]
                        .rearrange("g w -> () g w"), wait=[lcs])
                    cm = bufres["car"].load(
                        "gpsimd", sb["car"][0:1, :].rearrange("o (g w) -> o g w", g=8),
                        ca_hbm[:, sc * 128:(sc + 1) * 128]
                        .rearrange("g w -> () g w"), wait=[cas])
                    cem = bufres["hts"].load(
                        "sync", sb["urow"][0:1, 3072:3088].bitcast(F32),
                        ca_hbm[:, sc * 128 + 127:sc * 128 + 128]
                        .rearrange("g o -> () (g o)"), wait=[cas],
                        allow_slow_non_contiguous=True)
                    xtm = None
                    for rnd in range(2):
                        tms = []
                        for j8 in range(8):
                            j = rnd * 8 + j8
                            tb_, base = xbc(j)
                            tms.append(r.emit("tensor",
                                   lambda h, j8=j8, tb_=tb_, base=base, sc=sc:
                                   h.transpose(
                                ptb1[:, j8 * 128:(j8 + 1) * 128],
                                tb_[:, base + 15 + sc * 128:
                                    base + 15 + sc * 128 + 128],
                                sb["idb"][:, :]),
                                wait=[bankres[1].wait(), ipm[("c", j)]]
                                     if j8 == 0 else (),
                                inc=("mm", 1) if j8 == 7 else None))
                        xtm = r.emit("scalar", lambda h, rnd=rnd: h.activation(
                            out=sb["wc"][:, rnd * 1024:(rnd + 1) * 1024],
                            in_=ptb1[:, 0:1024], func=AF.Identity),
                            wait=[tms[-1]] +
                                 ([bufres["wc"].wait()] if rnd == 0 else []))
                        bankres[1].set(xtm)
                    btm = r.emit("tensor", lambda h, bct=bct, cbase=cbase, sc=sc:
                                 h.transpose(
                        ptb1[0:128, 0:64],
                        bct[0:64, cbase + 15 + sc * 128:cbase + 15 + sc * 128 + 128],
                        sb["idb"][0:64, 0:64]),
                        wait=[bankres[1].wait(), ipm[("c", 16)]], inc=("mm", 1))
                    btm = r.emit("scalar", lambda h: h.activation(
                        out=sb["wd"][:, WC_BT:WC_BT + 64], in_=ptb1[0:128, 0:64],
                        func=AF.Identity), wait=[btm, bufres["wd"].wait(), c64m])
                    bankres[1].set(btm)
                    wtm = r.emit("tensor", lambda h, sc=sc: h.transpose(
                        psum[1][0:128, 0:8],
                        sb["xt"][0:8, R3 + sc * 128:R3 + (sc + 1) * 128],
                        sb["idf"][0:8, 0:8]),
                        wait=[bankres[1].wait(), wm_], inc=("mm", 1))
                    wtm = r.emit("vector", lambda h: h.tensor_copy(
                        sb["xsq"][:, WCOL:WCOL + 8], psum[1][0:128, 0:8]),
                        wait=[wtm])
                    bankres[1].set(wtm)
                    ctm = r.emit("tensor", lambda h, sc=sc: h.transpose(
                        psum[1][0:128, 0:8],
                        sb["xt"][0:8, R1 + sc * 128:R1 + (sc + 1) * 128],
                        sb["idf"][0:8, 0:8]),
                        wait=[bankres[1].wait(), cmbm], inc=("mm", 1))
                    ctm = r.emit("vector", lambda h: h.tensor_copy(
                        sb["xsq"][:, CCOL:CCOL + 8], psum[1][0:128, 0:8]),
                        wait=[ctm])
                    bankres[1].set(ctm)
                    bwm = None
                    for hd in range(8):
                        bwm = r.emit("vector", lambda h, hd=hd: h.tensor_scalar_mul(
                            sb["wd"][:, WC_BW + hd * 64:WC_BW + (hd + 1) * 64],
                            sb["wd"][:, WC_BT:WC_BT + 64],
                            sb["xsq"][:, WCOL + hd:WCOL + hd + 1]),
                            wait=[btm, wtm] if hd == 0 else ())
                    gtm = r.emit("tensor", lambda h, bct=bct, cbase=cbase, sc=sc:
                                 h.matmul(
                        psum[0][:, 0:128],
                        bct[0:64, cbase + 15 + sc * 128:cbase + 15 + sc * 128 + 128],
                        sb["wd"][0:64, WC_C64 + sc * 128:WC_C64 + (sc + 1) * 128],
                        start=True, stop=True),
                        wait=[bankres[0].wait(), c64m, ipm[("c", 16)]],
                        inc=("mm", 1))
                    ym = {}
                    for hd in range(8):
                        ebm = r.emit("tensor", lambda h, hd=hd: h.matmul(
                            psum[1][:, 0:128], sb["ones_rowf"][0:1, :],
                            sb["lcr"][0:1, hd * 128:(hd + 1) * 128],
                            start=True, stop=True),
                            wait=[bankres[1].wait(), lm], inc=("mm", 1))
                        ebm = r.emit("vector", lambda h: h.tensor_add(
                            psum[1][:, 0:128], psum[1][:, 0:128], sb["ngt"][:, :]),
                            wait=[ebm, INIT[0]])
                        eem = r.emit("scalar", lambda h, hd=hd: h.activation(
                            out=sb["wd"][:, WC_EE:WC_EE + 128], in_=psum[1][:, 0:128],
                            func=AF.Exp, bias=sb["xsq"][:, CCOL + hd:CCOL + hd + 1]),
                            wait=[ebm, ctm])
                        bankres[1].set(eem)
                        mtm = r.emit("vector", lambda h: h.tensor_mul(
                            sb["wd"][:, WC_MT:WC_MT + 128],
                            sb["wd"][:, WC_EE:WC_EE + 128], psum[0][:, 0:128]),
                            wait=[eem, gtm])
                        if tc == 0 and sc == 0 and hd == 7:
                            dm1 = dbg_dump(r, "mb_ee",
                                           sb["wd"][:, WC_EE:WC_EE + 128], [eem])
                            dm2 = dbg_dump(r, "mb_mt",
                                           sb["wd"][:, WC_MT:WC_MT + 128], [mtm])
                            if dm2 is not None:
                                bankres[1].set(dm1)
                                mtm = dm2
                        csb = r.emit("tensor", lambda h, hd=hd: h.matmul(
                            psum[1][0:64, 0:128], sb["ones_rowb"][0:1, 0:64],
                            sb["car"][0:1, hd * 128:(hd + 1) * 128],
                            start=True, stop=True),
                            wait=[bankres[1].wait(), cm], inc=("mm", 1))
                        cscm = r.emit("vector", lambda h, sc=sc:
                                      h.tensor_mul(
                            sb["xsq"][0:64, 0:128],
                            sb["wd"][0:64, WC_C64 + sc * 128:WC_C64 + (sc + 1) * 128],
                            psum[1][0:64, 0:128]),
                            wait=[csb, ym.get(hd * 2 - 1)])
                        bankres[1].set(cscm)
                        for pc in range(2):
                            j = hd * 2 + pc
                            bk = 2 + j // 4
                            cl = (j % 4) * 128
                            r.emit("tensor", lambda h, j=j, bk=bk, cl=cl: h.matmul(
                                psum[bk][:, cl:cl + 128],
                                sb["wc"][:, j * 128:(j + 1) * 128],
                                sb["wd"][:, WC_MT:WC_MT + 128],
                                start=True, stop=False),
                                wait=[mtm, xtm] + ([bankres[bk].wait()]
                                                   if j % 4 == 0 else []))
                            ym[j] = r.emit("tensor", lambda h, hd=hd, pc=pc, bk=bk,
                                   cl=cl: h.matmul(
                                psum[bk][:, cl:cl + 128],
                                sb["hts"][0:64, hd * 256 + pc * 128:
                                          hd * 256 + pc * 128 + 128],
                                sb["xsq"][0:64, 0:128],
                                start=False, stop=True),
                                wait=[cscm, statem[0]],
                                inc=("mm", 1))
                    for bk in range(2, 6):
                        yev = r.emit("scalar", lambda h, bk=bk, sc=sc: h.activation(
                            out=sb["hsb"][:, YB:YB + 16384]
                            .rearrange("p (j w) -> p j w", j=16)
                            [:, (bk - 2) * 4:(bk - 2) * 4 + 4, sc * 128:sc * 128 + 128],
                            in_=psum[bk][:, :].rearrange("p (j w) -> p j w", j=4),
                            func=AF.Identity),
                            wait=[ym[(bk - 2) * 4 + 3]] +
                                 ([bufres["hsb"].wait()] if bk == 2 else []))
                        yevs.append(yev)
                        bankres[bk].set(yev)
                    yevm = r.mark("ac")
                    dl = {}
                    for hd in range(8):
                        bk = 2 + hd // 2
                        cl = (hd % 2) * 256
                        dl[hd] = r.emit("tensor", lambda h, hd=hd, bk=bk, cl=cl:
                               h.matmul(
                            psum[bk][0:64, cl:cl + 256],
                            sb["wd"][:, WC_BW + hd * 64:WC_BW + (hd + 1) * 64],
                            sb["wc"][:, hd * 256:(hd + 1) * 256],
                            start=True, stop=True),
                            wait=[bwm, xtm, yevm] + ([bankres[bk].wait()]
                                                     if hd % 2 == 0 else []),
                            inc=("mm", 1))
                    cab = r.emit("tensor", lambda h: h.matmul(
                        psum[1][0:64, 0:8], sb["ones_rowf"][0:1, 0:64],
                        sb["urow"][0:1, 3072:3088].bitcast(F32),
                        start=True, stop=True),
                        wait=[bankres[1].wait(), cem], inc=("mm", 1))
                    cab = r.emit("vector", lambda h: h.tensor_copy(
                        sb["xsq"][0:64, CA64:CA64 + 8], psum[1][0:64, 0:8]),
                        wait=[cab])
                    bankres[1].set(cab)
                    stm_ = None
                    for hd in range(8):
                        bk = 2 + hd // 2
                        cl = (hd % 2) * 256
                        stm_ = r.emit("vector", lambda h, hd=hd, bk=bk, cl=cl:
                               h.scalar_tensor_tensor(
                            sb["hts"][0:64, hd * 256:(hd + 1) * 256],
                            sb["hts"][0:64, hd * 256:(hd + 1) * 256],
                            sb["xsq"][0:64, CA64 + hd:CA64 + hd + 1],
                            psum[bk][0:64, cl:cl + 256],
                            op0=ALU.mult, op1=ALU.add),
                            wait=[dl[hd], cab] + ([r.mark("mm")] if hd == 0 else []))
                        bankres[bk].set(stm_)
                    statem[0] = stm_
                    if tc == 0 and sc == 0:
                        dmh = dbg_dump(r, "mb_h1", sb["hts"][0:64, :], [stm_])
                        if dmh is not None:
                            statem[0] = dmh
                    if tc == 0 and sc == 1:
                        dmy = dbg_dump(r, "mb_y1",
                                       sb["hsb"][:, 16384:32768].rearrange(
                                           "p (j w) -> p j w", j=16)[:, :, 0:128],
                                       list(yevs))
                        if dmy is not None:
                            bufres["hsb"].set(dmy)
                    if tc == 0 and sc == 0:
                        dmy = dbg_dump(r, "mb_y0",
                                       sb["hsb"][:, 16384:32768].rearrange(
                                           "p (j w) -> p j w", j=16)[:, :, 0:128],
                                       list(yevs))
                        if dmy is not None:
                            bufres["hsb"].set(dmy)
                    if tc == 0 and sc == 0:
                        dmx = dbg_dump(r, "mb_xtm", sb["wc"][:, 0:2048],
                                       [r.mark("mm")])
                        if dmx is not None:
                            bufres["wc"].set(dmx)
                        else:
                            bufres["wc"].set(r.mark("mm"))
                    else:
                        bufres["wc"].set(r.mark("mm"))
                    bufres["wd"].set(r.mark("mm"))
                    bufres["lcr"].set(r.mark("mm"))
                    bufres["car"].set(r.mark("mm"))
                    bufres["hts"].set(r.mark("mm"))
                if tc == 0:
                    dbgm["y"] = dbg_dump(r, "mb_y", sb["hsb"][:, 16384:32768],
                                         yevs + [r.mark("dv")])
                dxm = None
                for j in range(16):
                    tb_, base = xbc(j)
                    dxm = r.emit("vector", lambda h, j=j, tb_=tb_, base=base:
                           h.scalar_tensor_tensor(
                        sb["hsb"][:, YB + j * 1024:YB + (j + 1) * 1024],
                        tb_[:, base + 15:base + 15 + 1024],
                        sb["dpp"][:, j:j + 1],
                        sb["hsb"][:, YB + j * 1024:YB + (j + 1) * 1024],
                        op0=ALU.mult, op1=ALU.add),
                        wait=(yevs + [INIT[0], dbgm.get("y")]) if j == 0 else ())
                sim_ = None
                for hf in range(2):
                    sim_ = r.emit("scalar", lambda h, hf=hf: h.activation(
                        out=sb["xlt"][:, 0:8192],
                        in_=sb["hsb"][:, ZB + hf * 8192:ZB + (hf + 1) * 8192],
                        func=AF.Silu),
                        wait=[zdone, bufres["xlt"].wait(), dxm])
                    gm_ = r.emit("vector", lambda h, hf=hf: h.tensor_mul(
                        sb["hsb"][:, YB + hf * 8192:YB + (hf + 1) * 8192],
                        sb["hsb"][:, YB + hf * 8192:YB + (hf + 1) * 8192],
                        sb["xlt"][:, 0:8192]), wait=[sim_])
                    bufres["xlt"].set(gm_)
                dxm = gm_
                sqm_ = r.emit("scalar", lambda h: h.activation(
                    out=sb["hsb"][:, ZB:ZB + 16384],
                    in_=sb["hsb"][:, YB:YB + 16384], func=AF.Square),
                    wait=[dxm, dbgm.get("zx")])
                for half in range(2):
                    rsm = None
                    for j in range(16):
                        rsm = r.emit("tensor", lambda h, j=j, half=half: h.matmul(
                            psum[6][0:1, 0:512], sb["ones_colb"][:, 0:1],
                            sb["hsb"][:, ZB + j * 1024 + half * 512:
                                      ZB + j * 1024 + half * 512 + 512],
                            start=(j == 0), stop=(j == 15)),
                            wait=[sqm_, bankres[6].wait()] if j == 0 else (),
                            inc=("mm", 1) if j == 15 else None)
                    mvm = r.emit("vector", lambda h, half=half: h.tensor_scalar_mul(
                        sb["xsq"][0:1, XS0:XS0 + 512], psum[6][0:1, 0:512],
                        1.0 / 2048.0), wait=[rsm])
                    bankres[6].set(mvm)
                    l1 = r.emit("scalar", lambda h: h.activation(
                        out=sb["xsq"][0:1, XS0:XS0 + 512],
                        in_=sb["xsq"][0:1, XS0:XS0 + 512], func=AF.Ln, bias=1e-5),
                        wait=[mvm])
                    l2 = r.emit("scalar", lambda h, half=half: h.activation(
                        out=sb["lcr"][0:1, half * 512:half * 512 + 512],
                        in_=sb["xsq"][0:1, XS0:XS0 + 512], func=AF.Exp, scale=-0.5),
                        wait=[l1, bufres["lcr"].wait() if half == 0 else None])
                    bufres["lcr"].set(l2)
                    bcm = r.emit("tensor", lambda h, half=half: h.matmul(
                        psum[6][:, 0:512], sb["ones_rowf"][0:1, :],
                        sb["lcr"][0:1, half * 512:half * 512 + 512],
                        start=True, stop=True),
                        wait=[l2, bankres[6].wait()], inc=("mm", 1))
                    nym = None
                    for j in range(16):
                        nym = r.emit("vector", lambda h, j=j, half=half:
                               h.scalar_tensor_tensor(
                            sb["hsb"][:, ZB + j * 1024 + half * 512:
                                      ZB + j * 1024 + half * 512 + 512],
                            sb["hsb"][:, YB + j * 1024 + half * 512:
                                      YB + j * 1024 + half * 512 + 512],
                            sb["nw"][:, j:j + 1], psum[6][:, 0:512],
                            op0=ALU.mult, op1=ALU.mult),
                            wait=[bcm] if j == 0 else ())
                    bankres[6].set(nym)
                ynm = r.mark("dv")
                if tc == 0:
                    dy2 = dbg_dump(r, "mb_yn", sb["hsb"][:, 0:16384], [ynm])
                    if dy2 is not None:
                        ynm = dy2
                for ot in range(8):
                    wbuf = "wa" if ot % 2 == 0 else "wb"
                    wm = bufres[wbuf].load(
                        weng(wbuf),
                        sb[wbuf][:, 0:2048].rearrange("p (k m) -> p k m", k=16),
                        wbf["w_out"].rearrange("(k p) m -> p k m", p=128)
                        [:, :, ot * 128:(ot + 1) * 128], wait=INIT + [castm["w_out"]])
                    for half in range(2):
                        bk = (ot * 2 + half) % 4
                        for kch in range(16):
                            gm = r.emit("tensor", lambda h, kch=kch, half=half,
                                   bk=bk, wbuf=wbuf: h.matmul(
                                psum[bk][:, :],
                                sb[wbuf][:, kch * 128:(kch + 1) * 128],
                                sb["hsb"][:, ZB + kch * 1024 + half * 512:
                                          ZB + kch * 1024 + half * 512 + 512],
                                start=(kch == 0), stop=(kch == 15)),
                                wait=[wm, bankres[bk].wait(), ynm]
                                     if kch == 0 else (),
                                inc=("mm", 1) if kch == 15 else None)
                        s1 = r.emit("scalar", lambda h, bk=bk: h.activation(
                            out=sb["xsq"][:, XS0:XS0 + 512], in_=psum[bk][:, :],
                            func=AF.Square), wait=[gm, bufres["xsq"].wait()])
                        s2 = r.emit("scalar", lambda h: h.activation(
                            out=sb["xsq"][:, XS0:XS0 + 512],
                            in_=sb["xsq"][:, XS0:XS0 + 512], func=AF.Ln, bias=1.0))
                        s3 = r.emit("scalar", lambda h: h.activation(
                            out=sb["xsq"][:, XS0:XS0 + 512],
                            in_=sb["xsq"][:, XS0:XS0 + 512], func=AF.Exp,
                            scale=-0.5))
                        tm_ = r.emit("vector", lambda h, ot=ot, half=half, bk=bk:
                               h.tensor_mul(
                            sb["xlt"][:, ot * 1024 + half * 512:
                                      ot * 1024 + half * 512 + 512],
                            psum[bk][:, :], sb["xsq"][:, XS0:XS0 + 512]),
                            wait=[s3, bufres["xlt"].wait()]
                                 if ot == 0 and half == 0 else [s3])
                        bankres[bk].set(tm_)
                        bufres["xsq"].set(tm_)
                    bufres[wbuf].set(gm)
                tdone = r.mark("dv")
                if tc == 0:
                    dy3 = dbg_dump(r, "mb_t", sb["xlt"][:, 0:8192], [tdone])
                    if dy3 is not None:
                        tdone = dy3
                rb = {}
                for half in range(2):
                    rb[half] = r.emit("tensor", lambda h, half=half: h.matmul(
                        psum[6 + half][:, 0:512], sb["scr8"][64:65, 1024:1152],
                        sb["scr8"][64:65, half * 512:half * 512 + 512],
                        start=True, stop=True),
                        wait=[rowm, bankres[6 + half].wait()], inc=("mm", 1))
                for ot in range(8):
                    wbuf = "wc" if ot % 2 == 0 else "wd"
                    wm = bufres[wbuf].load(
                        weng(wbuf),
                        sb[wbuf][:, 0:1024].rearrange("p (k m) -> p k m", k=8),
                        wbf["w_nl"].rearrange("(k p) m -> p k m", p=128)
                        [:, :, ot * 128:(ot + 1) * 128], wait=INIT + [castm["w_nl"]])
                    glast = None
                    for half in range(2):
                        bk = (ot * 2 + half) % 4
                        for kch in range(8):
                            r.emit("tensor", lambda h, kch=kch, half=half, bk=bk,
                                   wbuf=wbuf: h.matmul(
                                psum[bk][:, :],
                                sb[wbuf][:, kch * 128:(kch + 1) * 128],
                                sb["xlt"][:, kch * 1024 + half * 512:
                                          kch * 1024 + half * 512 + 512],
                                start=(kch == 0), stop=False),
                                wait=[wm, bankres[bk].wait(), tdone]
                                     if kch == 0 else ())
                        r.emit("tensor", lambda h, ot=ot, bk=bk: h.matmul(
                            psum[bk][:, :], sb["brow"][0:1, ot * 128:(ot + 1) * 128],
                            sb["ones_rowb"][0:1, 0:512], start=False, stop=False),
                            wait=[brm] if ot == 0 and half == 0 else ())
                        gl = r.emit("tensor", lambda h, half=half, bk=bk: h.matmul(
                            psum[bk][:, :], sb["ones_rowb"][0:1, 0:128],
                            sb["nmb"][0:1, half * 512:half * 512 + 512],
                            start=False, stop=True),
                            wait=[rowm] if ot == 0 and half == 0 else (),
                            inc=("mm", 1))
                        glast = gl
                        xrb = "xr0" if half == 0 else "xr1"
                        xv = bufres[xrb].load(
                            "sync", sb["xr"][:, half * 512:half * 512 + 512],
                            x_cur.rearrange("(c p) t -> p c t", p=128)
                            [:, ot:ot + 1, t0 + half * 512:t0 + half * 512 + 512]
                            .rearrange("p o t -> p (o t)"),
                            wait=list(xcur_ready))
                        m1_ = r.emit("vector", lambda h, half=half: h.tensor_mul(
                            sb["xsq"][:, XS0:XS0 + 512],
                            sb["xr"][:, half * 512:half * 512 + 512],
                            psum[6 + half][:, 0:512]),
                            wait=[xv, rb[half], bufres["xsq"].wait()])
                        bufres["xsq"].set(m1_)
                        m2_ = r.emit("vector", lambda h, half=half, bk=bk:
                               h.tensor_add(
                            sb["xr"][:, half * 512:half * 512 + 512],
                            sb["xsq"][:, XS0:XS0 + 512], psum[bk][:, :]),
                            wait=[gl])
                        bankres[bk].set(m2_)
                        stm = r.dma("sync",
                                    x_cur.rearrange("(c p) t -> p c t", p=128)
                                    [:, ot:ot + 1, t0 + half * 512:t0 + half * 512 + 512]
                                    .rearrange("p o t -> p (o t)"),
                                    sb["xr"][:, half * 512:half * 512 + 512],
                                    xrb, wait=[m2_])
                        bufres[xrb].set(stm)
                    bufres[wbuf].set(glast)
                bankres[6].set(r.mark("dv"))
                bankres[7].set(r.mark("dv"))
                bufres["xlt"].set(r.mark("mm"))
                bufres["xt"].set(r.mark("mm"))
                bufres["hsb"].set(r.mark("mm"))
                bufres["big0"].set(r.mark("mm"))
                bufres["big1"].set(r.mark("mm"))
                bufres["cs"].set(r.mark("mm"))
            xcur_ready[:] = [r.mark("xr0"), r.mark("xr1")]
            bufres["lcr"].set(r.mark("mm"))
            bufres["car"].set(r.mark("mm"))
            bufres["vsb"].set(r.mark("dv"))
            bufres["brow"].set(r.mark("mm"))

        def phase_ffn1(m_ready):
            brm = bufres["brow"].load("gpsimd", sb["brow"][0:1, :], dr["bsp_row"])
            mld = bufres["hsb"].load(
                "sync", sb["hsb"][:, 0:8192].rearrange("p (k n) -> p k n", k=8),
                m_hbm.rearrange("(k p) n -> p k n", p=128), wait=[m_ready])
            xlres = [None, None]
            for tb in range(8):
                t0 = tb * 512
                xl_base = (tb % 2) * 4096
                bufres["xlt"].set(xlres[tb % 2])
                lnm, xv = emit_ln(t0, 512, xl_base, 512)
                for mt in range(8):
                    bk = mt % 4
                    for kch in range(8):
                        gm = r.emit("tensor", lambda h, bk=bk, kch=kch, mt=mt,
                                    xl_base=xl_base: h.matmul(
                            psum[bk][:, :],
                            sb["hsb"][:, kch * 1024 + mt * 128:
                                      kch * 1024 + (mt + 1) * 128],
                            sb["xlt"][:, xl_base + kch * 512:
                                      xl_base + (kch + 1) * 512],
                            start=(kch == 0), stop=False),
                            wait=[mld, lnm, bankres[bk].wait()] if kch == 0 else ())
                    gm = r.emit("tensor", lambda h, bk=bk, mt=mt: h.matmul(
                        psum[bk][:, :], sb["brow"][0:1, mt * 128:(mt + 1) * 128],
                        sb["ones_rowb"][0:1, :], start=False, stop=True),
                        wait=[brm] if (tb == 0 and mt == 0) else (), inc=("mm", 1))
                    ev = r.emit("vector", lambda h, bk=bk, mt=mt: h.scalar_tensor_tensor(
                        sb["big0"][:, :].bitcast(F32)[:, mt * 512:(mt + 1) * 512],
                        psum[bk][:, :], 0.5,
                        sb["xt"][:, mt * 512:(mt + 1) * 512],
                        op0=ALU.mult, op1=ALU.add),
                        wait=[gm] + ([bufres["big0"].wait()]
                                     if mt == 0 else []))
                    bankres[bk].set(ev)
                dvm = r.mark("dv")
                sv = r.dma("sync",
                           x_cur.rearrange("(c p) t -> p c t", p=128)[:, :, t0:t0 + 512],
                           sb["big0"][:, :].bitcast(F32)[:, 0:8 * 512]
                           .rearrange("p (c w) -> p c w", c=8),
                           "xs", wait=[dvm])
                bufres["big0"].set(sv)
                bufres["xt"].set(dvm)
                xlres[tb % 2] = r.mark("mm")
            bufres["xlt"].set(r.mark("mm"))
            bufres["hsb"].set(r.mark("mm"))
            xcur_ready[:] = [r.mark("xs")]
            bufres["brow"].set(r.mark("mm"))

        m_ready = None
        if "ffn0_a" in phases:
            phase_ffn0()
        if "attn" in phases:
            phase_attn()
        if "ffn0_b" in phases:
            phase_ffn0()
        if "mkM" in phases:
            m_ready = phase_mkM()
        if "ffn1_a" in phases:
            phase_ffn1(m_ready)
            _dm = dbg_dump(r, "xmid", x_cur, list(xcur_ready))
            if _dm is not None:
                xcur_ready.append(_dm)
        if "mamba" in phases:
            phase_mamba()
        if "ffn1_b" in phases:
            phase_ffn1(m_ready)

        for dn, src in (("e_consts", sb["consts"][:, :]),
                        ("e_onesrf", sb["ones_rowf"][0:1, :]),
                        ("e_onesrb", sb["ones_rowb"][0:1, :]),
                        ("e_idb", sb["idb"][:, :]),
                        ("e_stat", sb["stat"][0:1, :])):
            if dn in dbg:
                ew = [(s, v) for s, v in sorted(r.cnt.items())]
                r.dma("sync", dbg[dn], src, "s5", wait=ew)

        fv = r.dma("sync", out_t, x_cur, "s0", wait=list(xcur_ready))
        finals = [(s, v) for s, v in sorted(r.cnt.items())]
        r.emit("sync", lambda h: h.nop(), wait=finals)

        sems = {}
        for sn in SEM_NAMES:
            sems[sn] = es.enter_context(nc.semaphore(sn))
        block = es.enter_context(nc.Block())
        r.replay(block, sems)
    return nc


def kernel(**inputs):
    sh, per_core = host_prep(inputs)
    nc = build_program()
    in_maps = []
    for i in range(8):
        m = dict(sh)
        m.update(per_core[i])
        in_maps.append(m)
    res = run_bass_kernel_spmd(nc, in_maps, core_ids=list(range(8)))
    outs = [np.ascontiguousarray(np.asarray(rr["out"]).T) for rr in res.results]
    return np.stack(outs).astype(np.float32)


# version 10
# speedup vs baseline: 1.0409x; 1.0409x over previous
from contextlib import ExitStack

import numpy as np
import ml_dtypes

import concourse.bass as bass
import concourse.mybir as mybir
from concourse.bass_utils import run_bass_kernel_spmd

F32 = mybir.dt.float32
BF16 = mybir.dt.bfloat16
AF = mybir.ActivationFunctionType
ALU = mybir.AluOpType
AX = mybir.AxisListType
BF = ml_dtypes.bfloat16

D = 1024
T = 4096
ATTN_PARTS = "ABC"


def _tile_pp(v, dtype=np.float32):
    v = np.asarray(v)
    n = v.shape[0] // 128
    return np.ascontiguousarray(v.reshape(n, 128).T).astype(dtype)


def _dft_consts():
    j = np.arange(D, dtype=np.float64)
    ang = 2.0 * np.pi * np.outer(j, j) / D
    c = lambda a: np.ascontiguousarray(a).astype(BF)
    Fr, Fi = np.cos(ang), -np.sin(ang)
    Gr, Gi = np.cos(ang) / D, np.sin(ang) / D
    return c(Fr), c(Fi), c(-Fi), c(Gr), c(-Gi)


def _rope_consts():
    inv_freq = 1.0 / (10000.0 ** (np.arange(0, D, 2, dtype=np.float32) / D))
    t = np.arange(T, dtype=np.float32)
    freqs = np.outer(t, inv_freq).astype(np.float32)
    cs = np.stack([np.cos(freqs).T, np.sin(freqs).T])
    return np.ascontiguousarray(cs.reshape(2, 4, 128, T)).astype(BF)


def host_prep(inp):
    sh = {}
    f32 = lambda a: np.ascontiguousarray(np.asarray(a)).astype(np.float32)

    sh["w1t"] = f32(np.asarray(inp["L0_ffn_W1"]).T)
    sh["w2t"] = f32(np.asarray(inp["L0_ffn_W2"]).T)
    sh["b1_pp"] = _tile_pp(inp["L0_ffn_b1"])
    sh["b2_row"] = f32(inp["L0_ffn_b2"]).reshape(1, 1024)
    sh["scale_pp"] = np.ascontiguousarray(
        np.broadcast_to(np.asarray(inp["L0_ffn_scale"], np.float32), (128, 1)))

    sh["wqkvt"] = f32(np.asarray(inp["L0_attn_c_attn_w"]).T)
    sh["bqkv_row"] = f32(inp["L0_attn_c_attn_b"]).reshape(1, 3072)
    sh["wprojt"] = f32(np.asarray(inp["L0_attn_c_proj_w"]).T)
    sh["bproj_row"] = f32(inp["L0_attn_c_proj_b"]).reshape(1, 1024)
    sh["rope_csp"] = _rope_consts()

    sh["wfr"] = f32(inp["L1_ffn_w_freq_real"])
    sh["wfi"] = f32(inp["L1_ffn_w_freq_imag"])
    sh["c_fr"], sh["c_fi"], sh["c_fin"], sh["c_gr"], sh["c_gin"] = _dft_consts()
    sh["bsp_row"] = f32(inp["L1_ffn_bias"]).reshape(1, 1024)

    sh["w_in"] = f32(inp["L1_attn_W_in"])
    cw = np.asarray(inp["L1_attn_conv_w"], np.float32)
    sh["convw_pp"] = np.ascontiguousarray(
        np.transpose(cw.reshape(17, 128, 16), (1, 0, 2)).reshape(128, 272))
    sh["convb_pp"] = _tile_pp(inp["L1_attn_conv_b"])
    sh["dtb_col"] = f32(inp["L1_attn_dt_bias"]).reshape(8, 1)
    sh["alog_col"] = f32(inp["L1_attn_A_log"]).reshape(8, 1)
    Dv = np.asarray(inp["L1_attn_D"], np.float32)
    sh["d_pp"] = np.ascontiguousarray(
        np.broadcast_to(np.repeat(Dv, 2)[None, :], (128, 16)))
    ist = np.asarray(inp["L1_attn_init_states"], np.float32)
    sh["h0"] = np.ascontiguousarray(np.transpose(ist, (2, 0, 1)).reshape(64, 2048))
    sh["normw_pp"] = _tile_pp(inp["L1_attn_norm_w"])
    sh["w_out"] = f32(inp["L1_attn_W_out"])
    sh["w_nl"] = f32(inp["L1_attn_W_nl"])
    sh["bnl_row"] = f32(inp["L1_attn_b_nl"]).reshape(1, 1024)

    x = np.asarray(inp["L0_attn_x"], np.float32)
    per_core = [{"x0": np.ascontiguousarray(x[i].T)} for i in range(8)]
    return sh, per_core



ENGS = ("sync", "scalar", "vector", "gpsimd", "tensor")


AUTO_SEM = {"vector": "dv", "scalar": "ac", "gpsimd": "gp"}


class Rec:

    def __init__(self):
        self.items = []
        self.cnt = {}

    def emit(self, eng, fn, wait=(), inc=None, dma=False):
        wd_ = {}
        for x in wait:
            if x is not None:
                wd_[x[0]] = max(wd_.get(x[0], 0), x[1])
        w = list(wd_.items())
        if not dma and eng in AUTO_SEM:
            s = AUTO_SEM[eng]
            prev = self.cnt.get(s, 0)
            if prev > 0:
                w.append((s, prev))
            assert inc is None
            inc = (s, 1)
        if inc is not None:
            self.cnt[inc[0]] = self.cnt.get(inc[0], 0) + inc[1]
        self.items.append((eng, fn, tuple(w), inc))
        return None if inc is None else (inc[0], self.cnt[inc[0]])

    def val(self, sem):
        return self.cnt.get(sem, 0)

    def mark(self, sem):
        return (sem, self.cnt.get(sem, 0))

    def dma(self, eng, out, in_, sem, wait=(), **kw):
        return self.emit(eng, lambda h: h.dma_start(out=out, in_=in_, **kw),
                         wait=wait, inc=(sem, 16), dma=True)

    def replay(self, block, sems):
        def mk(engname):
            def run(h):
                for (eng, fn, waits, inc) in self.items:
                    if eng != engname:
                        continue
                    for (s, v) in waits:
                        h.wait_ge(sems[s], v)
                    ins = fn(h)
                    if inc is not None:
                        ins.then_inc(sems[inc[0]], inc[1])
            return run
        for engname in ENGS:
            getattr(block, engname)(mk(engname))


class Res:
    def __init__(self):
        self.m = None

    def wait(self):
        return self.m

    def set(self, mark):
        self.m = mark


class Buf(Res):

    def __init__(self, rec, sem):
        super().__init__()
        self.r = rec
        self.sem = sem

    def load(self, eng, out, in_, wait=(), **kw):
        sem = self.sem + ("g" if eng == "gpsimd" else "")
        return self.r.dma(eng, out, in_, sem,
                          wait=list(wait) + [self.m], **kw)


BUF_NAMES = ["wa", "wb", "wc", "wd", "xt", "xsq", "xlt", "big0", "big1",
             "ini", "xs", "us", "ms", "s0", "s1", "s2", "s3", "s4", "s5",
             "hsb", "xr0", "xr1", "odd", "tok", "evs", "e2t", "usb",
             "urow", "vsb", "eo", "uh", "vh", "qk", "oo", "brow",
             "kb0", "kb1", "cs", "lcr", "car", "hts", "cvh",
             "s6", "s7", "s8", "s9"]
BUF_SEMS = [n for b in BUF_NAMES for n in (b, b + "g")]
SEM_NAMES = ["mm", "dv", "ac", "gp"] + BUF_SEMS
ALL_PHASES = ("ffn0_a", "attn", "ffn0_b", "mkM", "ffn1_a", "mamba", "ffn1_b")


SCRATCH_SIZE = 16384
BIG1_W = 9216


def build_program(phases=ALL_PHASES, debug_out=(), mamba_tc=4, dram_out=()):
    nc = bass.Bass("TRN2", target_bir_lowering=False,
                   dynamic_dma_scratch_size=SCRATCH_SIZE)
    r = Rec()

    dr = {}
    def din(name, shape, dt=F32):
        dr[name] = nc.dram_tensor(name, list(shape), dt, kind="ExternalInput").ap()
        return dr[name]

    def dscr(name, shape, dt=F32):
        kind = "ExternalOutput" if name in dram_out else "Internal"
        dr[name] = nc.dram_tensor(name, list(shape), dt, kind=kind).ap()
        return dr[name]

    x0 = din("x0", [D, T])
    out_t = nc.dram_tensor("out", [D, T], F32, kind="ExternalOutput").ap()

    din("w1t", [1024, 4096]); din("w2t", [4096, 1024])
    din("b1_pp", [128, 32]); din("b2_row", [1, 1024]); din("scale_pp", [128, 1])
    din("wqkvt", [1024, 3072]); din("bqkv_row", [1, 3072])
    din("wprojt", [1024, 1024]); din("bproj_row", [1, 1024])
    din("rope_csp", [2, 4, 128, T], BF16)
    din("wfr", [1024, 1024]); din("wfi", [1024, 1024])
    for n in ("c_fr", "c_fi", "c_fin", "c_gr", "c_gin"):
        din(n, [1024, 1024], BF16)
    din("bsp_row", [1, 1024])
    din("w_in", [1024, 4232]); din("convw_pp", [128, 272]); din("convb_pp", [128, 17])
    din("dtb_col", [8, 1]); din("alog_col", [8, 1]); din("d_pp", [128, 16])
    din("h0", [64, 2048]); din("normw_pp", [128, 16])
    din("w_out", [2048, 1024]); din("w_nl", [1024, 1024]); din("bnl_row", [1, 1024])

    x_cur = dscr("x_cur", [D, T])
    m_hbm = dscr("m_hbm", [1024, 1024], BF16)
    eo_hbm = dscr("eo_hbm", [16, 1024], BF16)
    ul_hbm = dscr("ul_hbm", [4, 1024], BF16)
    v_hbm = dscr("v_hbm", [4, 4096], BF16)
    conv_hbm = dscr("conv_hbm", [34816, 128], BF16)
    lc_hbm = dscr("lc_hbm", [8, 1024])
    ca_hbm = dscr("ca_hbm", [8, 1024])
    qf_hbm = dscr("qf_hbm", [1024, T], BF16)
    kf_hbm = dscr("kf_hbm", [1024, T], BF16)
    vt_hbm = dscr("vt_hbm", [T, 1024], BF16)
    of_hbm = dscr("of_hbm", [1024, T], BF16)
    u_hbm = dscr("u_hbm", [1024, 1024])
    u2_hbm = dscr("u2_hbm", [1024, 1024])

    dbg = {}
    for (name, shape, dt) in debug_out:
        dbg[name] = nc.dram_tensor(
            "dbg_" + name, list(shape), {"f32": F32, "bf16": BF16}[dt],
            kind="ExternalOutput").ap()

    _dbg_ctr = [0]
    def dbg_dump(r, name, src, wait):
        if name in dbg:
            sem = ["s1", "s2", "s3", "s4", "s5", "s6", "s7", "s8",
                   "s9"][_dbg_ctr[0] % 9]
            _dbg_ctr[0] += 1
            return r.dma("sync", dbg[name], src, sem, wait=wait)
        return None

    es = ExitStack()
    with es:
        sb = {}
        def S(name, shape, dt=F32):
            sb[name] = es.enter_context(nc.sbuf_tensor("sb_" + name, list(shape), dt))
            return sb[name]

        S("consts", [128, 8])
        S("ones_rowb", [1, 512], BF16)
        S("ones_rowf", [1, 128])
        S("ones_colf", [128, 1])
        S("ones_colb", [128, 1], BF16)
        S("idb", [128, 128], BF16)
        S("cs", [128, 1040], BF16)
        S("brow", [1, 1024], BF16)
        S("b1_pp", [128, 32])
        S("bqkv_row", [1, 3072], BF16)
        S("scale_pp", [128, 1])
        S("xt", [128, 8 * 512])
        S("xsq", [128, 2 * 512])
        S("stat", [1, 4 * 512])

        S("xlt", [128, 8 * 1024], BF16)
        S("big0", [128, 9216], BF16)
        S("big1", [128, BIG1_W], BF16)
        sb["mks"] = sb["big0"][:, 4096:4608]
        S("wa", [128, 2048], BF16)
        S("wb", [128, 2048], BF16)
        S("wc", [128, 2048], BF16)
        S("wd", [128, 2048], BF16)
        S("hsb", [128, 32 * 1024], BF16)
        S("xr", [128, 1024])
        S("odd", [128, 64], BF16)
        sb["tok"] = sb["wd"][0:16, 0:1024]
        S("evs", [1, 4096], BF16)
        S("e2t", [4, 1024], BF16)
        S("usb", [4, 1024], BF16)
        S("urow", [1, 4096], BF16)
        S("scr8", [128, 2048])
        sb["vsb"] = sb["scr8"][:, :].bitcast(BF16)[0:4, :]
        sb["hts"] = sb["scr8"][0:64, :]
        sb["nmb"] = sb["usb"][0:1, 0:1024]
        sb["lcr"] = sb["urow"][0:1, 0:2048].bitcast(F32)
        sb["car"] = sb["urow"][0:1, 2048:3072]
        S("sm128", [128, 768])
        sb["idf"] = sb["sm128"][:, 0:128]
        sb["ngt"] = sb["sm128"][:, 128:256]
        sb["cvw"] = sb["sm128"][:, 256:528]
        sb["cvb"] = sb["sm128"][:, 528:545]
        sb["dpp"] = sb["sm128"][:, 545:561]
        sb["nw"] = sb["sm128"][:, 561:577]
        sb["z8"] = sb["sm128"][0:8, 577:705]
        sb["acol"] = sb["sm128"][0:8, 705:707]
        sb["dtb8"] = sb["sm128"][0:8, 707:708]

        psum = {}
        for i in range(8):
            psum[i] = es.enter_context(nc.psum_tensor(f"p{i}", [128, 512], F32))

        def weng(b):
            return "gpsimd"

        bankres = {i: Res() for i in range(8)}
        bufres = {n: Buf(r, n) for n in (
            "wa", "wb", "wc", "wd", "xt", "xlt", "big0", "big1", "xsq",
            "hsb", "xr0", "xr1", "odd", "tok", "eor", "evs", "e2t", "usb",
            "urow", "vsb", "eo", "uh", "vh", "qk", "oo", "brow",
            "kb0", "kb1", "cs", "lcr", "car", "hts", "cvh")}
        ini = Buf(r, "ini")
        xcur_ready = []

        cvals = [1e-5, -0.5, 0.5, -1.0, 2.0, 1.0, 30.0, 0.0]
        for i, v in enumerate(cvals):
            r.emit("gpsimd", lambda h, i=i, v=v: h.memset(sb["consts"][:, i:i + 1], v))
            nc.const_aps.register(F32, v, sb["consts"][:, i:i + 1])
        r.emit("gpsimd", lambda h: h.memset(sb["ones_rowb"][:, :], 1.0))
        r.emit("gpsimd", lambda h: h.memset(sb["ones_rowf"][:, :], 1.0))
        r.emit("gpsimd", lambda h: h.memset(sb["ones_colf"][:, :], 1.0))
        r.emit("gpsimd", lambda h: h.memset(sb["ones_colb"][:, :], 1.0))
        r.emit("gpsimd", lambda h: h.memset(sb["idb"][:, :], 1.0))
        r.emit("gpsimd", lambda h: h.affine_select(
            sb["idb"][:, :], sb["idb"][:, :], pattern=[[1, 128]],
            compare_op=ALU.is_equal, fill=0.0, base=0, channel_multiplier=-1))

        ini.load("sync", sb["b1_pp"][:, :], dr["b1_pp"])
        ini.load("gpsimd", sb["bqkv_row"][0:1, :], dr["bqkv_row"])
        ini.load("sync", sb["scale_pp"][:, :], dr["scale_pp"])
        ini.load("sync", sb["cvw"][:, :], dr["convw_pp"])
        ini.load("sync", sb["cvb"][:, :], dr["convb_pp"])
        ini.load("sync", sb["dtb8"][0:8, :], dr["dtb_col"])
        ini.load("sync", sb["dpp"][:, :], dr["d_pp"])
        ini.load("sync", sb["nw"][:, :], dr["normw_pp"])
        ini.load("sync", sb["acol"][0:8, 1:2], dr["alog_col"])
        r.emit("gpsimd", lambda h: h.memset(sb["idf"][:, :], 1.0))
        r.emit("gpsimd", lambda h: h.affine_select(
            sb["idf"][:, :], sb["idf"][:, :], pattern=[[1, 128]],
            compare_op=ALU.is_equal, fill=0.0, base=0, channel_multiplier=-1))
        r.emit("gpsimd", lambda h: h.memset(sb["ngt"][:, :], 0.0))
        r.emit("gpsimd", lambda h: h.affine_select(
            sb["ngt"][:, :], sb["ngt"][:, :], pattern=[[1, 128]],
            compare_op=ALU.is_ge, fill=-1e9, base=0, channel_multiplier=-1))
        r.emit("gpsimd", lambda h: h.memset(sb["z8"][0:8, :], 0.0))
        xcur_ready[:] = [r.dma("sync", x_cur, x0, "xs")]
        WSHAPES = {"w1t": [1024, 4096], "w2t": [4096, 1024],
                   "wqkvt": [1024, 3072], "wprojt": [1024, 1024],
                   "wfr": [1024, 1024], "wfi": [1024, 1024],
                   "w_in": [1024, 4232], "w_out": [2048, 1024],
                   "w_nl": [1024, 1024]}
        wbf = {}
        castm = {}
        CAST_SEMS = ["s6g", "s7g", "s8g", "s9g", "usg", "msg", "qkg",
                     "uhg", "vhg"]
        for i, wn in enumerate(("w1t", "w2t", "wqkvt", "wprojt", "wfr", "wfi",
                                "w_in", "w_out", "w_nl")):
            wbf[wn] = dscr(wn + "_bf", WSHAPES[wn], BF16)
            castm[wn] = r.dma("gpsimd", wbf[wn], dr[wn], CAST_SEMS[i])
        INIT = [r.mark("gp"), r.mark("ini"), r.mark("inig")]


        def emit_ln(t0, tw, xl_base, xl_stride, keep_x=True):
            if not hasattr(emit_ln, "dump_this"):
                emit_ln.dump_this = False
            st = sb["stat"]
            ST_TMP, ST_MEAN, ST_VAR, ST_RSTD, ST_NM = 0, 512, 1024, 1536, 1024
            xv = bufres["xt"].load(
                "sync", sb["xt"][:, 0:8 * tw].rearrange("p (c w) -> p c w", c=8),
                x_cur.rearrange("(c p) t -> p c t", p=128)[:, :, t0:t0 + tw],
                wait=INIT + list(xcur_ready))
            for c in range(8):
                r.emit("tensor", lambda h, c=c, tw=tw: h.matmul(
                    psum[6][0:1, 0:tw], sb["ones_colf"][:, 0:1],
                    sb["xt"][:, c * tw:(c + 1) * tw],
                    start=(c == 0), stop=(c == 7)),
                    wait=[xv, bankres[6].wait()] if c == 0 else ())
            sqmm = {}
            for c in range(8):
                w = [xv] if c == 0 else []
                if c >= 2:
                    w.append(sqmm[c - 2])
                if c == 0:
                    w.append(bufres["xsq"].wait())
                sqm = r.emit("scalar", lambda h, c=c, tw=tw: h.activation(
                    out=sb["xsq"][:, (c % 2) * tw:(c % 2) * tw + tw],
                    in_=sb["xt"][:, c * tw:(c + 1) * tw], func=AF.Square),
                    wait=w)
                smm = r.emit("tensor", lambda h, c=c, tw=tw: h.matmul(
                    psum[7][0:1, 0:tw], sb["ones_colf"][:, 0:1],
                    sb["xsq"][:, (c % 2) * tw:(c % 2) * tw + tw],
                    start=(c == 0), stop=(c == 7)),
                    wait=[sqm] + ([bankres[7].wait()] if c == 0 else []),
                    inc=("mm", 1))
                sqmm[c] = smm
            bufres["xsq"].set(smm)
            r.emit("vector", lambda h, tw=tw: h.tensor_scalar_mul(
                st[0:1, ST_MEAN:ST_MEAN + tw], psum[6][0:1, 0:tw], 1.0 / 1024.0),
                wait=[smm])
            r.emit("vector", lambda h, tw=tw: h.tensor_scalar_mul(
                st[0:1, ST_VAR:ST_VAR + tw], psum[7][0:1, 0:tw], 1.0 / 1024.0))
            r.emit("vector", lambda h, tw=tw: h.tensor_mul(
                st[0:1, ST_TMP:ST_TMP + tw], st[0:1, ST_MEAN:ST_MEAN + tw],
                st[0:1, ST_MEAN:ST_MEAN + tw]))
            r.emit("vector", lambda h, tw=tw: h.tensor_sub(
                st[0:1, ST_VAR:ST_VAR + tw], st[0:1, ST_VAR:ST_VAR + tw],
                st[0:1, ST_TMP:ST_TMP + tw]))
            dvm = r.mark("dv")
            r.emit("scalar", lambda h, tw=tw: h.activation(
                out=st[0:1, ST_TMP:ST_TMP + tw], in_=st[0:1, ST_VAR:ST_VAR + tw],
                func=AF.Ln, bias=1e-5), wait=[dvm])
            r.emit("scalar", lambda h, tw=tw: h.activation(
                out=st[0:1, ST_RSTD:ST_RSTD + tw], in_=st[0:1, ST_TMP:ST_TMP + tw],
                func=AF.Exp, scale=-0.5))
            acm = r.mark("ac")
            r.emit("vector", lambda h, tw=tw: h.tensor_mul(
                st[0:1, ST_NM:ST_NM + tw], st[0:1, ST_MEAN:ST_MEAN + tw],
                st[0:1, ST_RSTD:ST_RSTD + tw]), wait=[acm])
            r.emit("vector", lambda h, tw=tw: h.tensor_scalar_mul(
                st[0:1, ST_NM:ST_NM + tw], st[0:1, ST_NM:ST_NM + tw], -1.0))
            dvm2 = r.mark("dv")
            r.emit("tensor", lambda h, tw=tw: h.matmul(
                psum[6][:, 0:tw], sb["ones_rowf"][0:1, :],
                st[0:1, ST_RSTD:ST_RSTD + tw], start=True, stop=True),
                wait=[dvm2])
            bmm = r.emit("tensor", lambda h, tw=tw: h.matmul(
                psum[7][:, 0:tw], sb["ones_rowf"][0:1, :],
                st[0:1, ST_NM:ST_NM + tw], start=True, stop=True),
                inc=("mm", 1))
            lastdv = None
            for c in range(8):
                r.emit("vector", lambda h, c=c, tw=tw: h.tensor_mul(
                    sb["xsq"][:, 0:tw], sb["xt"][:, c * tw:(c + 1) * tw],
                    psum[6][:, 0:tw]),
                    wait=[bmm, bufres["xlt"].wait()] if c == 0 else ())
                lastdv = r.emit("vector", lambda h, c=c, tw=tw, xl_base=xl_base,
                                xl_stride=xl_stride: h.tensor_add(
                    sb["xlt"][:, xl_base + c * xl_stride:xl_base + c * xl_stride + tw],
                    sb["xsq"][:, 0:tw], psum[7][:, 0:tw]))
            if "ln_ps" in dbg and emit_ln.dump_this:
                emit_ln.dump_this = False
                pe6 = r.emit("scalar", lambda h, tw=tw: h.activation(
                    out=sb["xsq"][:, 0:tw], in_=psum[6][:, 0:tw],
                    func=AF.Identity), wait=[lastdv])
                pe7 = r.emit("scalar", lambda h, tw=tw: h.activation(
                    out=sb["xsq"][:, tw:2 * tw], in_=psum[7][:, 0:tw],
                    func=AF.Identity))
                pdm = r.dma("sync", dbg["ln_ps"], sb["xsq"][:, 0:2 * tw], "s4",
                            wait=[pe7])
                lastdv = r.emit("vector", lambda h: h.tensor_copy(
                    sb["xsq"][:, 0:1], sb["xsq"][:, 0:1]), wait=[pdm, pe6])
                bufres["xsq"].set(lastdv)
            bankres[6].set(lastdv)
            bankres[7].set(lastdv)
            bufres["xt"].set(lastdv)
            return lastdv, xv

        def phase_ffn0():
            ptb = psum[5][:, :].bitcast(BF16)
            brm = bufres["brow"].load("gpsimd", sb["brow"][0:1, :], dr["b2_row"])

            def lora_prep(lnm):
                tkm = None
                for c in range(8):
                    tm = r.emit("tensor", lambda h, c=c: h.transpose(
                        ptb[0:16, 0:128], sb["xlt"][:, c * 1024 + 1:c * 1024 + 17],
                        sb["idb"][:, :]),
                        wait=[lnm, bankres[5].wait()] if c == 0 else [bankres[5].wait()],
                        inc=("mm", 1))
                    tkm = r.emit("scalar", lambda h, c=c: h.activation(
                        out=sb["tok"][0:16, c * 128:(c + 1) * 128],
                        in_=ptb[0:16, 0:128], func=AF.Identity),
                        wait=[tm, bufres["tok"].wait()])
                    bankres[5].set(tkm)
                som = r.dma("sync", eo_hbm, sb["tok"][0:16, :], "eo",
                            wait=[tkm, r.mark("evs"), r.mark("e2t")])
                bufres["tok"].set(som)
                erm = bufres["evs"].load(
                    "sync", sb["evs"][0:1, :].rearrange("o (v d) -> o v d", v=4),
                    eo_hbm[0:8:2, :].rearrange("v d -> () v d"), wait=[som])
                e2m = None
                for ip in range(4):
                    e2m = bufres["e2t"].load(
                        "sync", sb["e2t"][0:4, ip * 256:(ip + 1) * 256],
                        eo_hbm[8 + 2 * ip:9 + 2 * ip, :].rearrange(
                            "o (q j) -> (o j) q", j=4),
                        wait=[som], allow_slow_non_contiguous=True)
                evm_ = r.emit("vector", lambda h: h.tensor_scalar_mul(
                    sb["evs"][0:1, :], sb["evs"][0:1, :],
                    sb["scale_pp"][0:1, 0:1]), wait=[erm])
                _d = dbg_dump(r, "l_evs", sb["evs"][0:1, :], [evm_])
                evm_ = _d or evm_
                _d = dbg_dump(r, "l_e2t", sb["e2t"][0:4, :], [e2m])
                e2m = _d or e2m

                om = None
                for c in range(8):
                    om = r.emit("vector", lambda h, c=c: h.tensor_copy(
                        sb["odd"][:, c * 8:(c + 1) * 8],
                        sb["xlt"][:, c * 1024 + 2:c * 1024 + 18:2]),
                        wait=[bufres["odd"].wait()] if c == 0 else ())
                _d = dbg_dump(r, "l_odd", sb["odd"][:, :], [om])
                om = _d or om
                return evm_, om, e2m

            lora = [None, None, None]
            for tb in range(4):
                t0 = tb * 1024
                if tb == 1:
                    emit_ln.dump_this = True
                ln0, _ = emit_ln(t0, 512, 0, 1024)
                ln1, _ = emit_ln(t0 + 512, 512, 512, 1024)
                if tb == 0:
                    lora[0], lora[1], lora[2] = lora_prep(ln0)
                if tb == 1:
                    _d = dbg_dump(r, "l_xlt1", sb["xlt"][:, :], [ln0, ln1])
                    if _d is not None:
                        ln0 = ln1 = _d
                    _d = dbg_dump(r, "l_xt1", sb["xt"][:, :], [ln1])
                    _d = dbg_dump(r, "l_stat1", sb["stat"][0:1, :], [ln1])
                um = None
                for half in range(2):
                    for c in range(8):
                        r.emit("tensor", lambda h, c=c, half=half: h.matmul(
                            psum[4][0:4, 0:512],
                            sb["odd"][:, c * 8:c * 8 + 4],
                            sb["xlt"][:, c * 1024 + half * 512:c * 1024 + half * 512 + 512],
                            start=(c == 0), stop=(c == 7)),
                            wait=[lora[1], ln0, ln1, bankres[4].wait()] if c == 0 else (),
                            inc=("mm", 1) if c == 7 else None)
                    um = r.emit("scalar", lambda h, half=half: h.activation(
                        out=sb["usb"][0:4, half * 512:half * 512 + 512],
                        in_=psum[4][0:4, 0:512], func=AF.Identity),
                        wait=[r.mark("mm"), bufres["usb"].wait()])
                    bankres[4].set(um)
                usm = r.dma("sync", ul_hbm, sb["usb"][0:4, :], "uh",
                            wait=[um, bufres["uh"].wait()])
                bufres["usb"].set(usm)
                urm = bufres["urow"].load(
                    "sync", sb["urow"][0:1, :],
                    ul_hbm.rearrange("i t -> () (i t)"), wait=[usm])
                if tb == 1:
                    _d = dbg_dump(r, "l_urow", sb["urow"][0:1, 0:4096], [urm])
                    urm = _d or urm
                bufres["uh"].set(urm)
                hm = None
                w1last = None
                for ht in range(32):
                    wbuf = "wa" if ht % 2 == 0 else "wb"
                    wm = bufres[wbuf].load(
                        weng(wbuf), sb[wbuf][:, 0:1024].rearrange("p (k m) -> p k m", k=8),
                        wbf["w1t"].rearrange("(k p) m -> p k m", p=128)
                        [:, :, ht * 128:(ht + 1) * 128], wait=INIT + [castm["w1t"]])
                    for half in range(2):
                        bk = (ht * 2 + half) % 4
                        for kch in range(8):
                            r.emit("tensor", lambda h, kch=kch, half=half, bk=bk,
                                   wbuf=wbuf: h.matmul(
                                psum[bk][:, :],
                                sb[wbuf][:, kch * 128:(kch + 1) * 128],
                                sb["xlt"][:, kch * 1024 + half * 512:
                                          kch * 1024 + half * 512 + 512],
                                start=(kch == 0), stop=False),
                                wait=([wm, bankres[bk].wait()] if kch == 0 else []) +
                                     ([ln0, ln1] if (kch == 0 and ht == 0) else []))
                        gl = r.emit("tensor", lambda h, ht=ht, half=half, bk=bk: h.matmul(
                            psum[bk][:, :],
                            sb["evs"][0:1, (ht // 8) * 1024 + (ht % 8) * 128:
                                      (ht // 8) * 1024 + (ht % 8) * 128 + 128],
                            sb["urow"][0:1, (ht // 8) * 1024 + half * 512:
                                       (ht // 8) * 1024 + half * 512 + 512],
                            start=False, stop=True),
                            wait=[urm, lora[0]] if ht == 0 and half == 0 else (),
                            inc=("mm", 1))
                        w1last = gl
                        hm = r.emit("scalar", lambda h, ht=ht, half=half, bk=bk:
                                    h.activation(
                            out=sb["hsb"][:, ht * 1024 + half * 512:
                                          ht * 1024 + half * 512 + 512],
                            in_=psum[bk][:, :], func=AF.Gelu,
                            bias=sb["b1_pp"][:, ht:ht + 1]),
                            wait=[gl] + ([bufres["hsb"].wait()] if ht == 0 and half == 0 else []))
                        bankres[bk].set(hm)
                    bufres[wbuf].set(w1last)
                bufres["xlt"].set(w1last)
                if tb == 1:
                    _d = dbg_dump(r, "l_h", sb["hsb"][:, 0:4096], [hm])
                    hm = _d or hm
                vm = None
                for j in range(4):
                    for half in range(2):
                        for c in range(8):
                            r.emit("tensor", lambda h, c=c, j=j, half=half: h.matmul(
                                psum[4][0:4, 0:512],
                                sb["odd"][:, c * 8 + 4:c * 8 + 8],
                                sb["hsb"][:, (j * 8 + c) * 1024 + half * 512:
                                          (j * 8 + c) * 1024 + half * 512 + 512],
                                start=(c == 0), stop=(c == 7)),
                                wait=[hm, bankres[4].wait()] if c == 0 else (),
                                inc=("mm", 1) if c == 7 else None)
                        vm = r.emit("scalar", lambda h, j=j, half=half: h.activation(
                            out=sb["vsb"][0:4, j * 1024 + half * 512:
                                          j * 1024 + half * 512 + 512],
                            in_=psum[4][0:4, 0:512], func=AF.Identity,
                            scale=sb["scale_pp"][0:4, 0:1]),
                            wait=[r.mark("mm")] +
                                 ([bufres["vsb"].wait()] if j == 0 and half == 0 else []))
                        bankres[4].set(vm)
                vsm = r.dma("sync", v_hbm, sb["vsb"][0:4, :], "vh",
                            wait=[vm, bufres["vh"].wait()])
                bufres["vsb"].set(vsm)
                vtm = None
                for ip in range(4):
                    vtm = bufres["vsb"].load(
                        "sync", sb["vsb"][0:4, ip * 1024:(ip + 1) * 1024],
                        v_hbm[ip:ip + 1, :].rearrange("o (j t) -> (o j) t", j=4),
                        wait=[vsm])
                if tb == 1:
                    _d = dbg_dump(r, "l_vsb", sb["vsb"][0:4, :], [vtm])
                    vtm = _d or vtm
                bufres["vh"].set(vtm)
                w2last = None
                for ot in range(8):
                    wbuf = ("wa", "wc") if ot % 2 == 0 else ("wb", "wd")
                    wms = []
                    for gi, wb_ in enumerate(wbuf):
                        wms.append(bufres[wb_].load(
                            weng(wb_),
                            sb[wb_][:, 0:2048].rearrange("p (k m) -> p k m", k=16),
                            wbf["w2t"].rearrange("(k p) m -> p k m", p=128)
                            [:, gi * 16:(gi + 1) * 16, ot * 128:(ot + 1) * 128],
                            wait=INIT + [castm["w2t"]]))
                    ip = ot // 2
                    qh = ot % 2
                    for half in range(2):
                        bk = (ot * 2 + half) % 4
                        for kch in range(32):
                            wb_ = wbuf[kch // 16]
                            r.emit("tensor", lambda h, kch=kch, half=half, bk=bk,
                                   wb_=wb_: h.matmul(
                                psum[bk][:, :],
                                sb[wb_][:, (kch % 16) * 128:(kch % 16 + 1) * 128],
                                sb["hsb"][:, kch * 1024 + half * 512:
                                          kch * 1024 + half * 512 + 512],
                                start=(kch == 0), stop=False),
                                wait=(wms + [bankres[bk].wait()]) if kch == 0 else ())
                        r.emit("tensor", lambda h, ip=ip, qh=qh, half=half, bk=bk: h.matmul(
                            psum[bk][:, :],
                            sb["e2t"][0:4, ip * 256 + qh * 128:ip * 256 + qh * 128 + 128],
                            sb["vsb"][0:4, ip * 1024 + half * 512:
                                      ip * 1024 + half * 512 + 512],
                            start=False, stop=False),
                            wait=[vtm, lora[2]] if ot == 0 and half == 0 else ())
                        gl = r.emit("tensor", lambda h, ot=ot, bk=bk: h.matmul(
                            psum[bk][:, :],
                            sb["brow"][0:1, ot * 128:(ot + 1) * 128],
                            sb["ones_rowb"][0:1, 0:512],
                            start=False, stop=True),
                            wait=[brm] if ot == 0 else (), inc=("mm", 1))
                        w2last = gl
                        xrb = "xr0" if half == 0 else "xr1"
                        xv = bufres[xrb].load(
                            "sync", sb["xr"][:, half * 512:half * 512 + 512],
                            x_cur.rearrange("(c p) t -> p c t", p=128)
                            [:, ot:ot + 1, t0 + half * 512:t0 + half * 512 + 512]
                            .rearrange("p o t -> p (o t)"),
                            wait=list(xcur_ready))
                        sm = r.emit("vector", lambda h, half=half, bk=bk:
                                    h.scalar_tensor_tensor(
                            sb["xr"][:, half * 512:half * 512 + 512],
                            psum[bk][:, :], 0.5,
                            sb["xr"][:, half * 512:half * 512 + 512],
                            op0=ALU.mult, op1=ALU.add), wait=[gl, xv])
                        bankres[bk].set(sm)
                        stm = r.dma("sync",
                                    x_cur.rearrange("(c p) t -> p c t", p=128)
                                    [:, ot:ot + 1, t0 + half * 512:t0 + half * 512 + 512]
                                    .rearrange("p o t -> p (o t)"),
                                    sb["xr"][:, half * 512:half * 512 + 512],
                                    xrb, wait=[sm])
                        bufres[xrb].set(stm)
                    for wb_ in wbuf:
                        bufres[wb_].set(w2last)
                bufres["hsb"].set(w2last)
                bufres["vsb"].set(w2last)
                bufres["urow"].set(w1last)
            xcur_ready[:] = [r.mark("xr0"), r.mark("xr1")]
            bufres["brow"].set(r.mark("mm"))
            bufres["odd"].set(r.mark("mm"))
            bufres["e2t"].set(r.mark("mm"))
            bufres["evs"].set(r.mark("mm"))

        def phase_attn():
            XS0, XS1 = 0, 512
            for tb in range(4 if "A" in ATTN_PARTS else 0):
                t0 = tb * 1024
                ln0, _ = emit_ln(t0, 512, 0, 1024)
                ln1, _ = emit_ln(t0 + 512, 512, 512, 1024)

                def qkv_group(mt, bk, wm, half, wbuf, extra_wait=()):
                    for kch in range(8):
                        r.emit("tensor", lambda h, kch=kch, bk=bk, half=half,
                               wbuf=wbuf: h.matmul(
                            psum[bk][:, :],
                            sb[wbuf][:, kch * 128:(kch + 1) * 128],
                            sb["xlt"][:, kch * 1024 + half * 512:
                                      kch * 1024 + half * 512 + 512],
                            start=(kch == 0), stop=False),
                            wait=([wm, bankres[bk].wait(), ln0, ln1] +
                                  list(extra_wait)) if kch == 0 else ())
                    return r.emit("tensor", lambda h, mt=mt, bk=bk: h.matmul(
                        psum[bk][:, :],
                        sb["bqkv_row"][0:1, mt * 128:(mt + 1) * 128],
                        sb["ones_rowb"][0:1, 0:512],
                        start=False, stop=True), wait=INIT, inc=("mm", 1))

                def wload(mt, wbuf):
                    return bufres[wbuf].load(
                        weng(wbuf),
                        sb[wbuf][:, 0:1024].rearrange("p (k m) -> p k m", k=8),
                        wbf["wqkvt"].rearrange("(k p) m -> p k m", p=128)
                        [:, :, mt * 128:(mt + 1) * 128], wait=INIT + [castm["wqkvt"]])

                QK = (("q", 0, qf_hbm), ("k", 8, kf_hbm))
                iters = [(qi, i, half) for qi in range(2)
                         for i in range(4) for half in range(2)]
                csres = [None, None]

                def cs_load(n, t0=t0):
                    qi, i, half = iters[n]
                    slot = n % 2
                    m = r.dma(
                        "sync",
                        sb["big1"][:, slot * 1024:(slot + 1) * 1024]
                        .rearrange("p (u w) -> p u w", u=2),
                        dr["rope_csp"][:, i:i + 1, :,
                                       t0 + half * 512:t0 + half * 512 + 512]
                        .rearrange("u o p t -> p (u o) t"),
                        ("cs", "qk")[slot], wait=[csres[slot]])
                    return m

                csm_pref = {0: cs_load(0)}
                wlo = whi = None
                glo = ghi = None
                for n, (qi, i, half) in enumerate(iters):
                    qk, base, dsthbm = QK[qi]
                    slot = n % 2
                    cs_b = sb["big1"][:, slot * 1024:(slot + 1) * 1024]
                    if half == 0:
                        wblo = "wa" if i % 2 == 0 else "wc"
                        wbhi = "wb" if i % 2 == 0 else "wd"
                        wlo = wload(base + i, wblo)
                        whi = wload(base + i + 4, wbhi)
                    blo, bhi = (0, 1) if (i + half) % 2 == 0 else (2, 3)
                    glo = qkv_group(base + i, blo, wlo, half, wblo)
                    ghi = qkv_group(base + i + 4, bhi, whi, half, wbhi)
                    csm = csm_pref[n]
                    if n + 1 < len(iters):
                        csm_pref[n + 1] = cs_load(n + 1)
                    r.emit("vector", lambda h, blo=blo, cs_b=cs_b: h.tensor_mul(
                        sb["xsq"][:, XS0:XS0 + 512], psum[blo][:, :],
                        cs_b[:, 0:512]),
                        wait=[glo, ghi, csm, bufres["xsq"].wait()])
                    r.emit("vector", lambda h, bhi=bhi, cs_b=cs_b: h.tensor_mul(
                        sb["xsq"][:, XS1:XS1 + 512], psum[bhi][:, :],
                        cs_b[:, 512:1024]))
                    r.emit("vector", lambda h: h.tensor_sub(
                        sb["wc"][:, 1024:1536],
                        sb["xsq"][:, XS0:XS0 + 512],
                        sb["xsq"][:, XS1:XS1 + 512]),
                        wait=[bufres["wc"].wait()])
                    r.emit("vector", lambda h, bhi=bhi, cs_b=cs_b: h.tensor_mul(
                        sb["xsq"][:, XS0:XS0 + 512], psum[bhi][:, :],
                        cs_b[:, 0:512]))
                    rlast = r.emit("vector", lambda h, blo=blo, cs_b=cs_b: h.tensor_mul(
                        sb["xsq"][:, XS1:XS1 + 512], psum[blo][:, :],
                        cs_b[:, 512:1024]))
                    bankres[blo].set(rlast)
                    bankres[bhi].set(rlast)
                    csres[slot] = rlast
                    rlast = r.emit("vector", lambda h: h.tensor_add(
                        sb["wc"][:, 1536:2048],
                        sb["xsq"][:, XS0:XS0 + 512],
                        sb["xsq"][:, XS1:XS1 + 512]))
                    bufres["xsq"].set(rlast)
                    sm = r.dma("sync",
                               dsthbm.rearrange("(g r p) t -> p g r t",
                                                g=2, p=128)
                               [:, :, i, t0 + half * 512:t0 + half * 512 + 512],
                               sb["wc"][:, 1024:2048].rearrange(
                                   "p (g w) -> p g w", g=2),
                               "wc", wait=[rlast])
                    bufres["wc"].set(sm)
                    if half == 1:
                        if wblo != "wc":
                            bufres[wblo].set(glo)
                        bufres[wbhi].set(ghi)
                bufres["big1"].set(r.mark("dv"))
                wcres = [None, None]
                pend = None

                def vflush(pend, tb=tb):
                    em, bk, dv_, half_, slot = pend
                    ptv = psum[bk][:, :].bitcast(BF16)
                    trm = None
                    for s4 in range(4):
                        trm = r.emit("tensor", lambda h, s4=s4, ptv=ptv, slot=slot:
                               h.transpose(
                            ptv[:, s4 * 128:(s4 + 1) * 128],
                            sb["wc"][:, slot * 512 + s4 * 128:
                                     slot * 512 + (s4 + 1) * 128],
                            sb["idb"][:, :]),
                            wait=[em] if s4 == 0 else (),
                            inc=("mm", 1) if s4 == 3 else None)
                    wcres[slot] = trm
                    evm2 = r.emit("scalar", lambda h, ptv=ptv, slot=slot: h.activation(
                        out=sb["wd"][:, slot * 512:slot * 512 + 512],
                        in_=ptv[:, 0:512], func=AF.Identity),
                        wait=[trm, bufres["wd"].wait()])
                    bankres[bk].set(evm2)
                    sm = r.dma("sync",
                               vt_hbm.rearrange("(g p) d -> p g d", p=128)
                               [:, tb * 8 + half_ * 4:tb * 8 + half_ * 4 + 4,
                                dv_ * 128:(dv_ + 1) * 128],
                               sb["wd"][:, slot * 512:slot * 512 + 512]
                               .rearrange("p (s w) -> p s w", s=4),
                               "wd", wait=[evm2])
                    bufres["wd"].set(sm)

                wm = None
                vwb = None
                for n in range(16):
                    dv, half = n // 2, n % 2
                    if half == 0:
                        vwb = "wa" if dv % 2 == 0 else "wb"
                        wm = wload(16 + dv, vwb)
                    bk = 4 if n % 2 == 0 else 5
                    gm = qkv_group(16 + dv, bk, wm, half, vwb)
                    if half == 1:
                        bufres[vwb].set(gm)
                    slot = n % 2
                    em = r.emit("scalar", lambda h, bk=bk, slot=slot: h.activation(
                        out=sb["wc"][:, slot * 512:slot * 512 + 512],
                        in_=psum[bk][:, :], func=AF.Identity),
                        wait=[gm, wcres[slot], bufres["wc"].wait()
                              if n == 0 else None])
                    pend_new = (em, bk, dv, half, slot)
                    if pend is not None:
                        vflush(pend)
                    pend = pend_new
                vflush(pend)
                bufres["wc"].set(max(w for w in wcres if w is not None))
            qkms = [r.mark("wc"), r.mark("wd")]
            dbg_dump(r, "a_qf", qf_hbm, qkms)
            dbg_dump(r, "a_kf", kf_hbm, qkms)
            dbg_dump(r, "a_vt", vt_hbm, qkms)
            if "B" not in ATTN_PARTS:
                vlm = None
            else:
                vlm = bufres["hsb"].load(
                "sync", sb["hsb"][:, :].rearrange("p (tt d) -> p tt d", tt=32),
                vt_hbm.rearrange("(tt p) d -> p tt d", p=128), wait=qkms)
            M_, L_, NGM, RS = 0, 1, 2, 3
            KMX, KSM = 8, 16
            prev_mask_use = None
            prev_stdm = None
            prev_pem = None
            prev_oev = None
            for tq in range(32 if "B" in ATTN_PARTS else 0):
                qm = bufres["wc"].load(
                    "sync", sb["wc"][:, 0:1024].rearrange("p (c m) -> p c m", c=8),
                    qf_hbm.rearrange("(c p) t -> p c t", p=128)
                    [:, :, tq * 128:(tq + 1) * 128], wait=qkms)
                dkb = tq // 4
                r.emit("gpsimd", lambda h: h.memset(sb["mks"][:, :], 0.0),
                       wait=[prev_mask_use])
                mkm = r.emit("gpsimd", lambda h, tq=tq: h.affine_select(
                    sb["mks"][:, :], sb["mks"][:, :], pattern=[[-1, 512]],
                    compare_op=ALU.is_ge, fill=-1e9,
                    base=(tq % 4) * 128, channel_multiplier=1))

                def emit_S(kb, qm=qm, dkb=dkb, mkm=mkm):
                    kbuf = "kb0" if kb % 2 == 0 else "kb1"
                    koff = 0 if kb % 2 == 0 else 4608
                    km = bufres[kbuf].load(
                        "sync",
                        sb["big0"][:, koff:koff + 4096].rearrange(
                            "p (c w) -> p c w", c=8),
                        kf_hbm.rearrange("(c p) t -> p c t", p=128)
                        [:, :, kb * 512:(kb + 1) * 512],
                        wait=qkms + [bufres["big0"].wait()])
                    bs = 0 if kb % 2 == 0 else 1
                    for kch in range(8):
                        r.emit("tensor", lambda h, kch=kch, koff=koff, bs=bs: h.matmul(
                            psum[bs][:, :],
                            sb["wc"][:, kch * 128:(kch + 1) * 128],
                            sb["big0"][:, koff + kch * 512:koff + (kch + 1) * 512],
                            start=(kch == 0), stop=(kch == 7)),
                            wait=[km, qm, bankres[bs].wait()] if kch == 0 else (),
                            inc=("mm", 1) if kch == 7 else None)
                    bufres[kbuf].set(r.mark("mm"))
                    smm = r.mark("mm")
                    if kb == dkb:
                        smm = r.emit("vector", lambda h, bs=bs: h.tensor_add(
                            psum[bs][:, :], psum[bs][:, :], sb["mks"][:, :]),
                            wait=[smm, mkm])
                    return smm

                for kb in range(dkb + 1):
                    smm = emit_S(kb)
                    if kb == dkb:
                        prev_mask_use = smm
                    bs = 0 if kb % 2 == 0 else 1
                    tmx = r.emit("vector", lambda h, bs=bs, kb=kb: h.reduce_max(
                        sb["xsq"][:, KMX + kb:KMX + kb + 1], psum[bs][:, :],
                        axis=AX.X),
                        wait=[smm, prev_stdm] +
                             ([prev_pem] if kb == 0 else []))
                    sev = r.emit("scalar", lambda h, bs=bs, kb=kb: h.activation(
                        out=sb["big1"][:, kb * 512:(kb + 1) * 512],
                        in_=psum[bs][:, :], func=AF.Identity),
                        wait=[smm, tmx] +
                             ([bufres["big1"].wait()] if kb == 0 else []))
                    bankres[bs].set(sev)
                gmx = r.emit("vector", lambda h, dkb=dkb: h.reduce_max(
                    sb["xsq"][:, M_:M_ + 1], sb["xsq"][:, KMX:KMX + dkb + 1],
                    axis=AX.X))
                ngm = r.emit("vector", lambda h: h.tensor_scalar_mul(
                    sb["xsq"][:, NGM:NGM + 1], sb["xsq"][:, M_:M_ + 1], -1.0))
                pem = None
                pvm = {}
                for kb in range(dkb + 1):
                    pem = r.emit("scalar", lambda h, kb=kb: h.activation(
                        out=sb["wd"][:, 0:512],
                        in_=sb["big1"][:, kb * 512:(kb + 1) * 512],
                        func=AF.Exp, bias=sb["xsq"][:, NGM:NGM + 1],
                        accum_out=sb["xsq"][:, KSM + kb:KSM + kb + 1]),
                        wait=[ngm] + ([bufres["wd"].wait()] if kb == 0 else []))
                    ptp = psum[4][:, :].bitcast(BF16)
                    trm = None
                    for s4 in range(4):
                        trm = r.emit("tensor", lambda h, s4=s4, ptp=ptp: h.transpose(
                            ptp[:, s4 * 128:(s4 + 1) * 128],
                            sb["wd"][:, s4 * 128:(s4 + 1) * 128],
                            sb["idb"][:, :]),
                            wait=[pem, bankres[4].wait()] if s4 == 0 else (),
                            inc=("mm", 1) if s4 == 3 else None)
                    ptev = r.emit("scalar", lambda h, ptp=ptp: h.activation(
                        out=sb["wd"][:, 512:1024], in_=ptp[:, 0:512],
                        func=AF.Identity), wait=[trm])
                    bankres[4].set(ptev)
                    for dh in range(2):
                        for kc in range(4):
                            pvm[dh] = r.emit("tensor",
                                   lambda h, dh=dh, kc=kc, kb=kb, dkb=dkb: h.matmul(
                                psum[2 + dh][:, :],
                                sb["wd"][:, 512 + kc * 128:512 + (kc + 1) * 128],
                                sb["hsb"][:, (kb * 4 + kc) * 1024 + dh * 512:
                                          (kb * 4 + kc) * 1024 + dh * 512 + 512],
                                start=(kb == 0 and kc == 0),
                                stop=(kb == dkb and kc == 3)),
                                wait=([ptev, vlm] +
                                      ([bankres[2 + dh].wait()] if kb == 0 else []))
                                     if kc == 0 else (),
                                inc=("mm", 1) if kc == 3 else None)
                prev_pem = pem
                bufres["wd"].set(pvm[1])
                lsum = r.emit("vector", lambda h, dkb=dkb: h.reduce_sum(
                    sb["xsq"][:, L_:L_ + 1], sb["xsq"][:, KSM:KSM + dkb + 1],
                    axis=AX.X), wait=[pem])
                stdm = None
                if "fl_st" in dbg:
                    stdm = r.dma("sync", dbg["fl_st"][tq], sb["xsq"][:, 0:8],
                                 ["s1", "s2", "s3"][tq % 3],
                                 wait=[r.mark("dv"), r.mark("ac")])
                    prev_stdm = stdm
                r.emit("vector", lambda h: h.reciprocal(
                    sb["xsq"][:, RS:RS + 1], sb["xsq"][:, L_:L_ + 1]),
                    wait=[lsum, stdm, prev_oev])
                rm = r.emit("vector", lambda h: h.tensor_scalar_mul(
                    sb["xsq"][:, RS:RS + 1], sb["xsq"][:, RS:RS + 1], 1.0 / 32.0))
                oev = None
                for dh in range(2):
                    oev = r.emit("scalar", lambda h, dh=dh: h.activation(
                        out=sb["wc"][:, dh * 512:dh * 512 + 512],
                        in_=psum[2 + dh][:, :], func=AF.Identity,
                        scale=sb["xsq"][:, RS:RS + 1]),
                        wait=[pvm[dh], rm])
                    bankres[2 + dh].set(oev)
                prev_oev = oev
                pto = psum[4][:, :].bitcast(BF16)
                trm = None
                for s8 in range(8):
                    trm = r.emit("tensor", lambda h, s8=s8, pto=pto: h.transpose(
                        pto[:, s8 * 128:(s8 + 1) * 128],
                        sb["wc"][:, s8 * 128:(s8 + 1) * 128],
                        sb["idb"][:, :]),
                        wait=[oev, bankres[4].wait()] if s8 == 0 else (),
                        inc=("mm", 1) if s8 == 7 else None)
                bufres["wc"].set(trm)
                otev = r.emit("scalar", lambda h, pto=pto: h.activation(
                    out=sb["wd"][:, 1024:2048], in_=pto[:, 0:1024],
                    func=AF.Identity), wait=[trm, bufres["wd"].wait()])
                bankres[4].set(otev)
                sm = r.dma("sync",
                           of_hbm.rearrange("(c p) t -> p c t", p=128)
                           [:, :, tq * 128:(tq + 1) * 128],
                           sb["wd"][:, 1024:2048].rearrange("p (c w) -> p c w", c=8),
                           "oo", wait=[otev])
                bufres["wd"].set(sm)
                bufres["big1"].set(pem)
            oom = r.mark("oo")
            dbg_dump(r, "a_of", of_hbm, [oom])
            bufres["hsb"].set(r.mark("mm"))
            bufres["big0"].set(r.mark("mm"))
            brm = bufres["brow"].load("gpsimd", sb["brow"][0:1, :], dr["bproj_row"])
            for tb in range(4 if "C" in ATTN_PARTS else 0):
                t0 = tb * 1024
                ofm = bufres["xlt"].load(
                    "sync", sb["xlt"][:, :].rearrange("p (c w) -> p c w", c=8),
                    of_hbm.rearrange("(c p) t -> p c t", p=128)
                    [:, :, t0:t0 + 1024], wait=[oom])
                for ot in range(8):
                    wbuf = "wa" if ot % 2 == 0 else "wb"
                    wm = bufres[wbuf].load(
                        weng(wbuf),
                        sb[wbuf][:, 0:1024].rearrange("p (k m) -> p k m", k=8),
                        wbf["wprojt"].rearrange("(k p) m -> p k m", p=128)
                        [:, :, ot * 128:(ot + 1) * 128], wait=INIT + [castm["wprojt"]])
                    glast = None
                    for half in range(2):
                        bk = (ot * 2 + half) % 4
                        for kch in range(8):
                            r.emit("tensor", lambda h, kch=kch, half=half, bk=bk,
                                   wbuf=wbuf: h.matmul(
                                psum[bk][:, :],
                                sb[wbuf][:, kch * 128:(kch + 1) * 128],
                                sb["xlt"][:, kch * 1024 + half * 512:
                                          kch * 1024 + half * 512 + 512],
                                start=(kch == 0), stop=False),
                                wait=[wm, ofm, bankres[bk].wait()] if kch == 0 else ())
                        gl = r.emit("tensor", lambda h, ot=ot, bk=bk: h.matmul(
                            psum[bk][:, :],
                            sb["brow"][0:1, ot * 128:(ot + 1) * 128],
                            sb["ones_rowb"][0:1, 0:512],
                            start=False, stop=True),
                            wait=[brm] if ot == 0 and half == 0 else (),
                            inc=("mm", 1))
                        glast = gl
                        xrb = "xr0" if half == 0 else "xr1"
                        xv = bufres[xrb].load(
                            "sync", sb["xr"][:, half * 512:half * 512 + 512],
                            x_cur.rearrange("(c p) t -> p c t", p=128)
                            [:, ot:ot + 1, t0 + half * 512:t0 + half * 512 + 512]
                            .rearrange("p o t -> p (o t)"),
                            wait=list(xcur_ready))
                        sm = r.emit("vector", lambda h, half=half, bk=bk: h.tensor_add(
                            sb["xr"][:, half * 512:half * 512 + 512],
                            sb["xr"][:, half * 512:half * 512 + 512],
                            psum[bk][:, :]), wait=[gl, xv])
                        bankres[bk].set(sm)
                        stm = r.dma("sync",
                                    x_cur.rearrange("(c p) t -> p c t", p=128)
                                    [:, ot:ot + 1, t0 + half * 512:t0 + half * 512 + 512]
                                    .rearrange("p o t -> p (o t)"),
                                    sb["xr"][:, half * 512:half * 512 + 512],
                                    xrb, wait=[sm])
                        bufres[xrb].set(stm)
                    bufres[wbuf].set(glast)
                bufres["xlt"].set(r.mark("mm"))
            xcur_ready[:] = [r.mark("xr0"), r.mark("xr1")]
            bufres["brow"].set(r.mark("mm"))

        def phase_mkM():
            wrm = bufres["big0"].load(
                "gpsimd", sb["big0"][:, 0:8192].rearrange("p (k n) -> p k n", k=8),
                wbf["wfr"].rearrange("(k p) n -> p k n", p=128), wait=INIT + [castm["wfr"]])
            wim = bufres["big1"].load(
                "gpsimd", sb["big1"][:, 0:8192].rearrange("p (k n) -> p k n", k=8),
                wbf["wfi"].rearrange("(k p) n -> p k n", p=128), wait=INIT + [castm["wfi"]])
            for (uname, terms) in (
                    ("u_hbm", (("big0", "c_fr", wrm), ("big1", "c_fin", wim))),
                    ("u2_hbm", (("big1", "c_fr", wim), ("big0", "c_fi", wrm)))):
                for nh in range(2):
                    first = True
                    for (wsb, fname, wmark) in terms:
                        for pch in range(8):
                            buf = "wa" if pch % 2 == 0 else "wb"
                            fv = bufres[buf].load(
                                "sync", sb[buf][:, 0:512],
                                dr[fname][pch * 128:(pch + 1) * 128,
                                          nh * 512:(nh + 1) * 512], wait=INIT)
                            for m in range(8):
                                w = [fv, wmark] if m == 0 else []
                                if first and pch == 0 and m == 0:
                                    w += [bankres[i].wait() for i in range(8)]
                                gm = r.emit("tensor",
                                       lambda h, pch=pch, m=m, buf=buf, wsb=wsb,
                                              first=first: h.matmul(
                                           psum[m][:, :],
                                           sb[wsb][:, pch * 1024 + m * 128:
                                                   pch * 1024 + (m + 1) * 128],
                                           sb[buf][:, 0:512],
                                           start=(first and pch == 0),
                                           stop=((not first) and pch == 7)),
                                       wait=w,
                                       inc=("mm", 1) if m == 7 else None)
                            bufres[buf].set(gm)
                        first = False
                    mmv = r.mark("mm")
                    evm = None
                    for m in range(8):
                        evm = r.emit("scalar", lambda h, m=m: h.activation(
                            out=sb["xt"][:, m * 512:(m + 1) * 512],
                            in_=psum[m][:, :], func=AF.Identity),
                            wait=[mmv, bufres["xt"].wait()] if m == 0 else ())
                    for m in range(8):
                        bankres[m].set(evm)
                    sv = r.dma("scalar",
                               dr[uname].rearrange("(mc p) n -> p mc n", p=128)
                               [:, :, nh * 512:(nh + 1) * 512],
                               sb["xt"][:, 0:8 * 512].rearrange(
                                   "p (mc n) -> p mc n", mc=8),
                               "us", wait=[evm])
                    bufres["xt"].set(sv)
            uq = r.mark("us")
            bufres["big0"].set(r.mark("mm"))
            bufres["big1"].set(r.mark("mm"))
            for nh in range(2):
                first = True
                for (uname, gname) in (("u_hbm", "c_gr"), ("u2_hbm", "c_gin")):
                    for kch in range(8):
                        ubuf = "wc" if kch % 2 == 0 else "wd"
                        gbuf = "wa" if kch % 2 == 0 else "wb"
                        uv = bufres[ubuf].load(
                            "gpsimd", sb[ubuf][:, 0:1024],
                            dr[uname][kch * 128:(kch + 1) * 128, :], wait=[uq])
                        gv = bufres[gbuf].load(
                            "sync", sb[gbuf][:, 0:512],
                            dr[gname][kch * 128:(kch + 1) * 128,
                                      nh * 512:(nh + 1) * 512], wait=INIT)
                        for m in range(8):
                            w = [uv, gv] if m == 0 else []
                            if first and kch == 0 and m == 0:
                                w += [bankres[i].wait() for i in range(8)]
                            gm = r.emit("tensor",
                                   lambda h, m=m, ubuf=ubuf, gbuf=gbuf, kch=kch,
                                          first=first: h.matmul(
                                       psum[m][:, :],
                                       sb[ubuf][:, m * 128:(m + 1) * 128],
                                       sb[gbuf][:, 0:512],
                                       start=(first and kch == 0),
                                       stop=((not first) and kch == 7)),
                                   wait=w,
                                   inc=("mm", 1) if m == 7 else None)
                        bufres[ubuf].set(gm)
                        bufres[gbuf].set(gm)
                    first = False
                mmv = r.mark("mm")
                evm = None
                for m in range(8):
                    evm = r.emit("scalar", lambda h, m=m, nh=nh: h.activation(
                        out=sb["big0"][:, m * 1024 + nh * 512:
                                       m * 1024 + (nh + 1) * 512],
                        in_=psum[m][:, :], func=AF.Identity),
                        wait=[mmv, bufres["big0"].wait()] if m == 0 else ())
                for m in range(8):
                    bankres[m].set(evm)
                dm = dbg_dump(r, f"mkm_big0_{nh}", sb["big0"][:, :], [r.mark("ac")])
                if dm is not None:
                    bufres["big0"].set(dm)
            acm = r.mark("ac")
            dbg_dump(r, "mkm_big0", sb["big0"][:, :], [acm])
            sv = r.dma("scalar", m_hbm.rearrange("(mc p) n -> p mc n", p=128),
                       sb["big0"][:, 0:8192].rearrange("p (mc n) -> p mc n", mc=8),
                       "ms", wait=[acm])
            bufres["big0"].set(sv)
            return r.mark("ms")

        def phase_mamba():
            ZB, YB = 0, 16384
            def xbc(ct):
                if ct < 8:
                    return sb["big0"], ct * 1040
                if ct < 16:
                    return sb["big1"], (ct - 8) * 1040
                return sb["cs"], 0
            R0, R1, R2, R3 = 0, 1024, 2048, 3072
            WCOL, CCOL, CA64, XS0 = 896, 904, 912, 0
            ptb1 = psum[1][:, :].bitcast(BF16)

            brm = bufres["brow"].load("gpsimd", sb["brow"][0:1, :], dr["bnl_row"])
            h0m = bufres["vsb"].load("sync", sb["hts"][0:64, :], dr["h0"])
            on64 = r.emit("gpsimd", lambda h: h.memset(
                sb["scr8"][64:65, 1024:1152], 1.0), wait=[bufres["vsb"].wait()])

            statem = [h0m]
            dbgm = {}
            dmh0 = dbg_dump(r, "mb_h0", sb["hts"][0:64, :], [h0m])
            if dmh0 is not None:
                statem[0] = dmh0
            bufres["lcr"].set(bufres["urow"].wait())
            bufres["car"].set(bufres["urow"].wait())
            am = r.emit("scalar", lambda h: h.activation(
                out=sb["acol"][0:8, 0:1], in_=sb["acol"][0:8, 1:2], func=AF.Exp),
                wait=INIT)
            am = r.emit("vector", lambda h: h.tensor_scalar_mul(
                sb["acol"][0:8, 0:1], sb["acol"][0:8, 0:1], -1.0), wait=[am])
            import os
            MDBG = os.environ.get("MAMBA_DBG", "")
            cvh = None
            for ct in range(0 if "nodiag" in MDBG else 17):
                dm = None
                for k in range(16):
                    dm = r.emit("vector", lambda h, ct=ct, k=k: h.tensor_scalar_mul(
                        sb["wc"][:, k * 128:(k + 1) * 128], sb["idb"][:, :],
                        sb["cvw"][:, ct * 16 + k:ct * 16 + k + 1]),
                        wait=[INIT[0], bufres["wc"].wait()] if k == 0 else ())
                cvh = r.dma("sync",
                            conv_hbm.rearrange("(q p) m -> p q m", p=128)
                            [:, ct * 16:(ct + 1) * 16, :],
                            sb["wc"][:, 0:2048].rearrange("p (k m) -> p k m", k=16),
                            "cvh", wait=[dm])
                bufres["wc"].set(cvh)
            hm0 = None
            for ct in range(0 if "nohist" in MDBG else 17):
                tb_, base = xbc(ct)
                hm0 = r.emit("gpsimd", lambda h, tb_=tb_, base=base: h.memset(
                    tb_[:, base:base + 15], 0.0),
                    wait=[bufres["big0"].wait(), bufres["big1"].wait(),
                          bufres["cs"].wait()] if ct == 0 else ())

            for tc in range(mamba_tc):
                t0 = tc * 1024
                lnm = {}
                for half in range(2):
                    lnm[half], _ = emit_ln(t0 + half * 512, 512, half * 512, 1024)
                    r.emit("vector", lambda h, half=half: h.tensor_copy(
                        sb["nmb"][0:1, half * 512:half * 512 + 512],
                        sb["stat"][0:1, 1024:1536]))
                    r.emit("vector", lambda h, half=half: h.tensor_scalar_add(
                        sb["scr8"][64:65, half * 512:half * 512 + 512],
                        sb["stat"][0:1, 1536:2048], 1.0),
                        wait=[on64, statem[0]] if tc == 0 and half == 0 else ())
                rowm = r.mark("dv")
                ipm = {}
                for mt in range(34):
                    mw = 128 if mt < 33 else 8
                    wbuf = "wa" if mt % 2 == 0 else "wb"
                    wm = bufres[wbuf].load(
                        weng(wbuf),
                        sb[wbuf][:, 0:8 * mw].rearrange("p (k m) -> p k m", k=8),
                        wbf["w_in"].rearrange("(k p) m -> p k m", p=128)
                        [:, :, mt * 128:mt * 128 + mw], wait=INIT + [castm["w_in"]])
                    glast = None
                    for half in range(2):
                        bk = (mt * 2 + half) % 4
                        for kch in range(8):
                            glast = r.emit("tensor", lambda h, kch=kch, half=half,
                                   bk=bk, wbuf=wbuf, mw=mw: h.matmul(
                                psum[bk][0:mw, :],
                                sb[wbuf][:, kch * mw:(kch + 1) * mw],
                                sb["xlt"][:, kch * 1024 + half * 512:
                                          kch * 1024 + half * 512 + 512],
                                start=(kch == 0), stop=(kch == 7)),
                                wait=[wm, bankres[bk].wait(), lnm[0], lnm[1]]
                                     if kch == 0 else (),
                                inc=("mm", 1) if kch == 7 else None)
                        if mt < 16:
                            em = r.emit("scalar", lambda h, mt=mt, half=half, bk=bk:
                                        h.activation(
                                out=sb["hsb"][:, ZB + mt * 1024 + half * 512:
                                              ZB + mt * 1024 + half * 512 + 512],
                                in_=psum[bk][:, :], func=AF.Identity),
                                wait=[glast] + ([bufres["hsb"].wait()]
                                                if mt == 0 and half == 0 else []))
                        elif mt < 33:
                            tb_, base = xbc(mt - 16)
                            em = r.emit("scalar", lambda h, tb_=tb_, base=base,
                                        half=half, bk=bk: h.activation(
                                out=tb_[:, base + 15 + half * 512:
                                        base + 15 + half * 512 + 512],
                                in_=psum[bk][:, :], func=AF.Identity),
                                wait=[glast, hm0])
                        else:
                            em = r.emit("scalar", lambda h, half=half, bk=bk:
                                        h.activation(
                                out=sb["xt"][0:8, R0 + half * 512:
                                             R0 + half * 512 + 512],
                                in_=psum[bk][0:8, :], func=AF.Identity,
                                bias=sb["dtb8"][0:8, 0:1]),
                                wait=[glast, bufres["xt"].wait()])
                        bankres[bk].set(em)
                        ipm[(mt, half)] = em
                    bufres[wbuf].set(glast)
                bufres["xlt"].set(r.mark("mm"))
                zdone = r.mark("ac")
                if tc == 0:
                    dbgm["zx"] = dbg_dump(r, "mb_zx", sb["hsb"][:, 0:16384], [zdone])
                for ct in range(17):
                    tb_, base = xbc(ct)
                    wbuf = "wa" if ct % 2 == 0 else "wb"
                    dm = bufres[wbuf].load(
                        "sync", sb[wbuf][:, 0:2048].rearrange("p (k m) -> p k m", k=16),
                        conv_hbm.rearrange("(q p) m -> p q m", p=128)
                        [:, ct * 16:(ct + 1) * 16, :], wait=[cvh])
                    gms = []
                    for th in range(2):
                        bk = 4 + th
                        for k in range(16):
                            gm = r.emit("tensor", lambda h, k=k, th=th, bk=bk,
                                   tb_=tb_, base=base, wbuf=wbuf: h.matmul(
                                psum[bk][:, :],
                                sb[wbuf][:, k * 128:(k + 1) * 128],
                                tb_[:, base + th * 512 + k:
                                    base + th * 512 + k + 512],
                                start=(k == 0), stop=(k == 15)),
                                wait=[dm, bankres[bk].wait(),
                                      ipm[(16 + ct, 0)], ipm[(16 + ct, 1)]]
                                     if k == 0 else (),
                                inc=("mm", 1) if k == 15 else None)
                        gms.append(gm)
                    tlm = r.emit("vector", lambda h, tb_=tb_, base=base: h.tensor_copy(
                        tb_[:, base:base + 15], tb_[:, base + 1024:base + 1039]),
                        wait=gms)
                    for th in range(2):
                        bk = 4 + th
                        em = r.emit("scalar", lambda h, th=th, bk=bk, tb_=tb_,
                                    base=base, ct=ct: h.activation(
                            out=tb_[:, base + 15 + th * 512:base + 15 + th * 512 + 512],
                            in_=psum[bk][:, :], func=AF.Silu,
                            bias=sb["cvb"][:, ct:ct + 1]),
                            wait=[tlm])
                        bankres[bk].set(em)
                    bufres[wbuf].set(em)
                    ipm[("c", ct)] = em
                if tc == 0:
                    dbg_dump(r, "mb_xbc0",
                             sb["big0"][:, 0:8 * 1040].rearrange(
                                 "p (s w) -> p s w", s=8)[:, :, 0:1039],
                             [ipm[("c", ct_)] for ct_ in range(8)])
                xr_ = sb["xt"][0:8, :]
                dt0 = [ipm[(33, 0)], ipm[(33, 1)], am]
                r.emit("vector", lambda h: h.tensor_scalar_max(
                    xr_[:, R1:R1 + 1024], xr_[:, R0:R0 + 1024], 0.0), wait=dt0)
                r.emit("scalar", lambda h: h.activation(
                    out=xr_[:, R2:R2 + 1024], in_=xr_[:, R0:R0 + 1024],
                    func=AF.Abs), wait=dt0)
                acm = r.mark("ac")
                r.emit("scalar", lambda h: h.activation(
                    out=xr_[:, R3:R3 + 1024], in_=xr_[:, R2:R2 + 1024],
                    func=AF.Exp, scale=-1.0))
                r.emit("scalar", lambda h: h.activation(
                    out=xr_[:, R2:R2 + 1024], in_=xr_[:, R3:R3 + 1024],
                    func=AF.Ln, bias=1.0))
                acm = r.mark("ac")
                r.emit("vector", lambda h: h.tensor_add(
                    xr_[:, R0:R0 + 1024], xr_[:, R1:R1 + 1024],
                    xr_[:, R2:R2 + 1024]), wait=[acm])
                r.emit("vector", lambda h: h.tensor_scalar_mul(
                    xr_[:, R1:R1 + 1024], xr_[:, R0:R0 + 1024],
                    sb["acol"][0:8, 0:1]))
                dvm = r.mark("dv")
                ldtm = r.emit("scalar", lambda h: h.activation(
                    out=xr_[:, R2:R2 + 1024], in_=xr_[:, R0:R0 + 1024],
                    func=AF.Ln), wait=[dvm])
                scm = None
                for sc in range(8):
                    scm = r.emit("vector", lambda h, sc=sc: h.tensor_tensor_scan(
                        xr_[:, R3 + sc * 128:R3 + (sc + 1) * 128],
                        xr_[:, R1 + sc * 128:R1 + (sc + 1) * 128],
                        sb["z8"][0:8, :], 0.0, op0=ALU.add, op1=ALU.add),
                        wait=[INIT[0]] if sc == 0 else ())
                cmbm = r.emit("vector", lambda h: h.tensor_sub(
                    xr_[:, R1:R1 + 1024], xr_[:, R2:R2 + 1024],
                    xr_[:, R3:R3 + 1024]), wait=[ldtm])
                cam = r.emit("scalar", lambda h: h.activation(
                    out=xr_[:, R2:R2 + 1024], in_=xr_[:, R3:R3 + 1024],
                    func=AF.Exp), wait=[cmbm])
                lcs = r.dma("sync", lc_hbm, xr_[:, R3:R3 + 1024], "uh",
                            wait=[cmbm, r.mark("lcr")])
                cas = r.dma("sync", ca_hbm, xr_[:, R2:R2 + 1024], "vh",
                            wait=[cam, r.mark("carg"), r.mark("hts")])
                wlm = None
                for sc in range(8):
                    wlm = r.emit("vector", lambda h, sc=sc: h.tensor_scalar_add(
                        xr_[:, R0 + sc * 128:R0 + (sc + 1) * 128],
                        xr_[:, R1 + sc * 128:R1 + (sc + 1) * 128],
                        xr_[:, R3 + sc * 128 + 127:R3 + sc * 128 + 128]),
                        wait=[ldtm] if sc == 0 else ())
                wm_ = r.emit("scalar", lambda h: h.activation(
                    out=xr_[:, R3:R3 + 1024], in_=xr_[:, R0:R0 + 1024],
                    func=AF.Exp), wait=[wlm, lcs])
                if tc == 0:
                    dbg_dump(r, "mb_dt", sb["xt"][0:8, :], [wm_, r.mark("dv")])
                WC_BW, WC_MT, WC_EE, WC_BT, WC_C64 = 0, 512, 640, 768, 832
                bct, cbase = xbc(16)
                c64m = r.emit("vector", lambda h, bct=bct, cbase=cbase: h.tensor_copy(
                    sb["wd"][0:64, WC_C64:WC_C64 + 1024],
                    bct[64:128, cbase + 15:cbase + 15 + 1024]),
                    wait=[ipm[("c", 16)], bufres["wd"].wait()])
                yevs = []
                for sc in range(8):
                    lm = bufres["lcr"].load(
                        "sync", sb["lcr"][0:1, :].rearrange("o (g w) -> o g w", g=8),
                        lc_hbm[:, sc * 128:(sc + 1) * 128]
                        .rearrange("g w -> () g w"), wait=[lcs])
                    cm = bufres["car"].load(
                        "gpsimd", sb["car"][0:1, :].rearrange("o (g w) -> o g w", g=8),
                        ca_hbm[:, sc * 128:(sc + 1) * 128]
                        .rearrange("g w -> () g w"), wait=[cas])
                    cem = bufres["hts"].load(
                        "sync", sb["urow"][0:1, 3072:3088].bitcast(F32),
                        ca_hbm[:, sc * 128 + 127:sc * 128 + 128]
                        .rearrange("g o -> () (g o)"), wait=[cas],
                        allow_slow_non_contiguous=True)
                    xtm = None
                    for rnd in range(2):
                        tms = []
                        for j8 in range(8):
                            j = rnd * 8 + j8
                            tb_, base = xbc(j)
                            tms.append(r.emit("tensor",
                                   lambda h, j8=j8, tb_=tb_, base=base, sc=sc:
                                   h.transpose(
                                ptb1[:, j8 * 128:(j8 + 1) * 128],
                                tb_[:, base + 15 + sc * 128:
                                    base + 15 + sc * 128 + 128],
                                sb["idb"][:, :]),
                                wait=[bankres[1].wait(), ipm[("c", j)]]
                                     if j8 == 0 else (),
                                inc=("mm", 1) if j8 == 7 else None))
                        xtm = r.emit("scalar", lambda h, rnd=rnd: h.activation(
                            out=sb["wc"][:, rnd * 1024:(rnd + 1) * 1024],
                            in_=ptb1[:, 0:1024], func=AF.Identity),
                            wait=[tms[-1]] +
                                 ([bufres["wc"].wait()] if rnd == 0 else []))
                        bankres[1].set(xtm)
                    btm = r.emit("tensor", lambda h, bct=bct, cbase=cbase, sc=sc:
                                 h.transpose(
                        ptb1[0:128, 0:64],
                        bct[0:64, cbase + 15 + sc * 128:cbase + 15 + sc * 128 + 128],
                        sb["idb"][0:64, 0:64]),
                        wait=[bankres[1].wait(), ipm[("c", 16)]], inc=("mm", 1))
                    btm = r.emit("scalar", lambda h: h.activation(
                        out=sb["wd"][:, WC_BT:WC_BT + 64], in_=ptb1[0:128, 0:64],
                        func=AF.Identity), wait=[btm, bufres["wd"].wait(), c64m])
                    bankres[1].set(btm)
                    wtm = r.emit("tensor", lambda h, sc=sc: h.transpose(
                        psum[1][0:128, 0:8],
                        sb["xt"][0:8, R3 + sc * 128:R3 + (sc + 1) * 128],
                        sb["idf"][0:8, 0:8]),
                        wait=[bankres[1].wait(), wm_], inc=("mm", 1))
                    wtm = r.emit("vector", lambda h: h.tensor_copy(
                        sb["xsq"][:, WCOL:WCOL + 8], psum[1][0:128, 0:8]),
                        wait=[wtm])
                    bankres[1].set(wtm)
                    ctm = r.emit("tensor", lambda h, sc=sc: h.transpose(
                        psum[1][0:128, 0:8],
                        sb["xt"][0:8, R1 + sc * 128:R1 + (sc + 1) * 128],
                        sb["idf"][0:8, 0:8]),
                        wait=[bankres[1].wait(), cmbm], inc=("mm", 1))
                    ctm = r.emit("vector", lambda h: h.tensor_copy(
                        sb["xsq"][:, CCOL:CCOL + 8], psum[1][0:128, 0:8]),
                        wait=[ctm])
                    bankres[1].set(ctm)
                    bwm = None
                    for hd in range(8):
                        bwm = r.emit("vector", lambda h, hd=hd: h.tensor_scalar_mul(
                            sb["wd"][:, WC_BW + hd * 64:WC_BW + (hd + 1) * 64],
                            sb["wd"][:, WC_BT:WC_BT + 64],
                            sb["xsq"][:, WCOL + hd:WCOL + hd + 1]),
                            wait=[btm, wtm] if hd == 0 else ())
                    gtm = r.emit("tensor", lambda h, bct=bct, cbase=cbase, sc=sc:
                                 h.matmul(
                        psum[0][:, 0:128],
                        bct[0:64, cbase + 15 + sc * 128:cbase + 15 + sc * 128 + 128],
                        sb["wd"][0:64, WC_C64 + sc * 128:WC_C64 + (sc + 1) * 128],
                        start=True, stop=True),
                        wait=[bankres[0].wait(), c64m, ipm[("c", 16)]],
                        inc=("mm", 1))
                    ym = {}
                    for hd in range(8):
                        ebm = r.emit("tensor", lambda h, hd=hd: h.matmul(
                            psum[1][:, 0:128], sb["ones_rowf"][0:1, :],
                            sb["lcr"][0:1, hd * 128:(hd + 1) * 128],
                            start=True, stop=True),
                            wait=[bankres[1].wait(), lm], inc=("mm", 1))
                        ebm = r.emit("vector", lambda h: h.tensor_add(
                            psum[1][:, 0:128], psum[1][:, 0:128], sb["ngt"][:, :]),
                            wait=[ebm, INIT[0]])
                        eem = r.emit("scalar", lambda h, hd=hd: h.activation(
                            out=sb["wd"][:, WC_EE:WC_EE + 128], in_=psum[1][:, 0:128],
                            func=AF.Exp, bias=sb["xsq"][:, CCOL + hd:CCOL + hd + 1]),
                            wait=[ebm, ctm])
                        bankres[1].set(eem)
                        mtm = r.emit("vector", lambda h: h.tensor_mul(
                            sb["wd"][:, WC_MT:WC_MT + 128],
                            sb["wd"][:, WC_EE:WC_EE + 128], psum[0][:, 0:128]),
                            wait=[eem, gtm])
                        if tc == 0 and sc == 0 and hd == 7:
                            dm1 = dbg_dump(r, "mb_ee",
                                           sb["wd"][:, WC_EE:WC_EE + 128], [eem])
                            dm2 = dbg_dump(r, "mb_mt",
                                           sb["wd"][:, WC_MT:WC_MT + 128], [mtm])
                            if dm2 is not None:
                                bankres[1].set(dm1)
                                mtm = dm2
                        csb = r.emit("tensor", lambda h, hd=hd: h.matmul(
                            psum[1][0:64, 0:128], sb["ones_rowb"][0:1, 0:64],
                            sb["car"][0:1, hd * 128:(hd + 1) * 128],
                            start=True, stop=True),
                            wait=[bankres[1].wait(), cm], inc=("mm", 1))
                        cscm = r.emit("vector", lambda h, sc=sc:
                                      h.tensor_mul(
                            sb["xsq"][0:64, 0:128],
                            sb["wd"][0:64, WC_C64 + sc * 128:WC_C64 + (sc + 1) * 128],
                            psum[1][0:64, 0:128]),
                            wait=[csb, ym.get(hd * 2 - 1)])
                        bankres[1].set(cscm)
                        for pc in range(2):
                            j = hd * 2 + pc
                            bk = 2 + j // 4
                            cl = (j % 4) * 128
                            r.emit("tensor", lambda h, j=j, bk=bk, cl=cl: h.matmul(
                                psum[bk][:, cl:cl + 128],
                                sb["wc"][:, j * 128:(j + 1) * 128],
                                sb["wd"][:, WC_MT:WC_MT + 128],
                                start=True, stop=False),
                                wait=[mtm, xtm] + ([bankres[bk].wait()]
                                                   if j % 4 == 0 else []))
                            ym[j] = r.emit("tensor", lambda h, hd=hd, pc=pc, bk=bk,
                                   cl=cl: h.matmul(
                                psum[bk][:, cl:cl + 128],
                                sb["hts"][0:64, hd * 256 + pc * 128:
                                          hd * 256 + pc * 128 + 128],
                                sb["xsq"][0:64, 0:128],
                                start=False, stop=True),
                                wait=[cscm, statem[0]],
                                inc=("mm", 1))
                    for bk in range(2, 6):
                        yev = r.emit("scalar", lambda h, bk=bk, sc=sc: h.activation(
                            out=sb["hsb"][:, YB:YB + 16384]
                            .rearrange("p (j w) -> p j w", j=16)
                            [:, (bk - 2) * 4:(bk - 2) * 4 + 4, sc * 128:sc * 128 + 128],
                            in_=psum[bk][:, :].rearrange("p (j w) -> p j w", j=4),
                            func=AF.Identity),
                            wait=[ym[(bk - 2) * 4 + 3]] +
                                 ([bufres["hsb"].wait()] if bk == 2 else []))
                        yevs.append(yev)
                        bankres[bk].set(yev)
                    yevm = r.mark("ac")
                    dl = {}
                    for hd in range(8):
                        bk = 2 + hd // 2
                        cl = (hd % 2) * 256
                        dl[hd] = r.emit("tensor", lambda h, hd=hd, bk=bk, cl=cl:
                               h.matmul(
                            psum[bk][0:64, cl:cl + 256],
                            sb["wd"][:, WC_BW + hd * 64:WC_BW + (hd + 1) * 64],
                            sb["wc"][:, hd * 256:(hd + 1) * 256],
                            start=True, stop=True),
                            wait=[bwm, xtm, yevm] + ([bankres[bk].wait()]
                                                     if hd % 2 == 0 else []),
                            inc=("mm", 1))
                    cab = r.emit("tensor", lambda h: h.matmul(
                        psum[1][0:64, 0:8], sb["ones_rowf"][0:1, 0:64],
                        sb["urow"][0:1, 3072:3088].bitcast(F32),
                        start=True, stop=True),
                        wait=[bankres[1].wait(), cem], inc=("mm", 1))
                    cab = r.emit("vector", lambda h: h.tensor_copy(
                        sb["xsq"][0:64, CA64:CA64 + 8], psum[1][0:64, 0:8]),
                        wait=[cab])
                    bankres[1].set(cab)
                    stm_ = None
                    for hd in range(8):
                        bk = 2 + hd // 2
                        cl = (hd % 2) * 256
                        stm_ = r.emit("vector", lambda h, hd=hd, bk=bk, cl=cl:
                               h.scalar_tensor_tensor(
                            sb["hts"][0:64, hd * 256:(hd + 1) * 256],
                            sb["hts"][0:64, hd * 256:(hd + 1) * 256],
                            sb["xsq"][0:64, CA64 + hd:CA64 + hd + 1],
                            psum[bk][0:64, cl:cl + 256],
                            op0=ALU.mult, op1=ALU.add),
                            wait=[dl[hd], cab] + ([r.mark("mm")] if hd == 0 else []))
                        bankres[bk].set(stm_)
                    statem[0] = stm_
                    if tc == 0 and sc == 0:
                        dmh = dbg_dump(r, "mb_h1", sb["hts"][0:64, :], [stm_])
                        if dmh is not None:
                            statem[0] = dmh
                    if tc == 0 and sc == 1:
                        dmy = dbg_dump(r, "mb_y1",
                                       sb["hsb"][:, 16384:32768].rearrange(
                                           "p (j w) -> p j w", j=16)[:, :, 0:128],
                                       list(yevs))
                        if dmy is not None:
                            bufres["hsb"].set(dmy)
                    if tc == 0 and sc == 0:
                        dmy = dbg_dump(r, "mb_y0",
                                       sb["hsb"][:, 16384:32768].rearrange(
                                           "p (j w) -> p j w", j=16)[:, :, 0:128],
                                       list(yevs))
                        if dmy is not None:
                            bufres["hsb"].set(dmy)
                    if tc == 0 and sc == 0:
                        dmx = dbg_dump(r, "mb_xtm", sb["wc"][:, 0:2048],
                                       [r.mark("mm")])
                        if dmx is not None:
                            bufres["wc"].set(dmx)
                        else:
                            bufres["wc"].set(r.mark("mm"))
                    else:
                        bufres["wc"].set(r.mark("mm"))
                    bufres["wd"].set(r.mark("mm"))
                    bufres["lcr"].set(r.mark("mm"))
                    bufres["car"].set(r.mark("mm"))
                    bufres["hts"].set(r.mark("mm"))
                if tc == 0:
                    dbgm["y"] = dbg_dump(r, "mb_y", sb["hsb"][:, 16384:32768],
                                         yevs + [r.mark("dv")])
                dxm = None
                for j in range(16):
                    tb_, base = xbc(j)
                    dxm = r.emit("vector", lambda h, j=j, tb_=tb_, base=base:
                           h.scalar_tensor_tensor(
                        sb["hsb"][:, YB + j * 1024:YB + (j + 1) * 1024],
                        tb_[:, base + 15:base + 15 + 1024],
                        sb["dpp"][:, j:j + 1],
                        sb["hsb"][:, YB + j * 1024:YB + (j + 1) * 1024],
                        op0=ALU.mult, op1=ALU.add),
                        wait=(yevs + [INIT[0], dbgm.get("y")]) if j == 0 else ())
                sim_ = None
                for hf in range(2):
                    sim_ = r.emit("scalar", lambda h, hf=hf: h.activation(
                        out=sb["xlt"][:, 0:8192],
                        in_=sb["hsb"][:, ZB + hf * 8192:ZB + (hf + 1) * 8192],
                        func=AF.Silu),
                        wait=[zdone, bufres["xlt"].wait(), dxm])
                    gm_ = r.emit("vector", lambda h, hf=hf: h.tensor_mul(
                        sb["hsb"][:, YB + hf * 8192:YB + (hf + 1) * 8192],
                        sb["hsb"][:, YB + hf * 8192:YB + (hf + 1) * 8192],
                        sb["xlt"][:, 0:8192]), wait=[sim_])
                    bufres["xlt"].set(gm_)
                dxm = gm_
                sqm_ = r.emit("scalar", lambda h: h.activation(
                    out=sb["hsb"][:, ZB:ZB + 16384],
                    in_=sb["hsb"][:, YB:YB + 16384], func=AF.Square),
                    wait=[dxm, dbgm.get("zx")])
                for half in range(2):
                    rsm = None
                    for j in range(16):
                        rsm = r.emit("tensor", lambda h, j=j, half=half: h.matmul(
                            psum[6][0:1, 0:512], sb["ones_colb"][:, 0:1],
                            sb["hsb"][:, ZB + j * 1024 + half * 512:
                                      ZB + j * 1024 + half * 512 + 512],
                            start=(j == 0), stop=(j == 15)),
                            wait=[sqm_, bankres[6].wait()] if j == 0 else (),
                            inc=("mm", 1) if j == 15 else None)
                    mvm = r.emit("vector", lambda h, half=half: h.tensor_scalar_mul(
                        sb["xsq"][0:1, XS0:XS0 + 512], psum[6][0:1, 0:512],
                        1.0 / 2048.0), wait=[rsm])
                    bankres[6].set(mvm)
                    l1 = r.emit("scalar", lambda h: h.activation(
                        out=sb["xsq"][0:1, XS0:XS0 + 512],
                        in_=sb["xsq"][0:1, XS0:XS0 + 512], func=AF.Ln, bias=1e-5),
                        wait=[mvm])
                    l2 = r.emit("scalar", lambda h, half=half: h.activation(
                        out=sb["lcr"][0:1, half * 512:half * 512 + 512],
                        in_=sb["xsq"][0:1, XS0:XS0 + 512], func=AF.Exp, scale=-0.5),
                        wait=[l1, bufres["lcr"].wait() if half == 0 else None])
                    bufres["lcr"].set(l2)
                    bcm = r.emit("tensor", lambda h, half=half: h.matmul(
                        psum[6][:, 0:512], sb["ones_rowf"][0:1, :],
                        sb["lcr"][0:1, half * 512:half * 512 + 512],
                        start=True, stop=True),
                        wait=[l2, bankres[6].wait()], inc=("mm", 1))
                    nym = None
                    for j in range(16):
                        nym = r.emit("vector", lambda h, j=j, half=half:
                               h.scalar_tensor_tensor(
                            sb["hsb"][:, ZB + j * 1024 + half * 512:
                                      ZB + j * 1024 + half * 512 + 512],
                            sb["hsb"][:, YB + j * 1024 + half * 512:
                                      YB + j * 1024 + half * 512 + 512],
                            sb["nw"][:, j:j + 1], psum[6][:, 0:512],
                            op0=ALU.mult, op1=ALU.mult),
                            wait=[bcm] if j == 0 else ())
                    bankres[6].set(nym)
                ynm = r.mark("dv")
                if tc == 0:
                    dy2 = dbg_dump(r, "mb_yn", sb["hsb"][:, 0:16384], [ynm])
                    if dy2 is not None:
                        ynm = dy2
                for ot in range(8):
                    wbuf = "wa" if ot % 2 == 0 else "wb"
                    wm = bufres[wbuf].load(
                        weng(wbuf),
                        sb[wbuf][:, 0:2048].rearrange("p (k m) -> p k m", k=16),
                        wbf["w_out"].rearrange("(k p) m -> p k m", p=128)
                        [:, :, ot * 128:(ot + 1) * 128], wait=INIT + [castm["w_out"]])
                    for half in range(2):
                        bk = (ot * 2 + half) % 4
                        for kch in range(16):
                            gm = r.emit("tensor", lambda h, kch=kch, half=half,
                                   bk=bk, wbuf=wbuf: h.matmul(
                                psum[bk][:, :],
                                sb[wbuf][:, kch * 128:(kch + 1) * 128],
                                sb["hsb"][:, ZB + kch * 1024 + half * 512:
                                          ZB + kch * 1024 + half * 512 + 512],
                                start=(kch == 0), stop=(kch == 15)),
                                wait=[wm, bankres[bk].wait(), ynm]
                                     if kch == 0 else (),
                                inc=("mm", 1) if kch == 15 else None)
                        s1 = r.emit("scalar", lambda h, bk=bk: h.activation(
                            out=sb["xsq"][:, XS0:XS0 + 512], in_=psum[bk][:, :],
                            func=AF.Square), wait=[gm, bufres["xsq"].wait()])
                        s2 = r.emit("scalar", lambda h: h.activation(
                            out=sb["xsq"][:, XS0:XS0 + 512],
                            in_=sb["xsq"][:, XS0:XS0 + 512], func=AF.Ln, bias=1.0))
                        s3 = r.emit("scalar", lambda h: h.activation(
                            out=sb["xsq"][:, XS0:XS0 + 512],
                            in_=sb["xsq"][:, XS0:XS0 + 512], func=AF.Exp,
                            scale=-0.5))
                        tm_ = r.emit("vector", lambda h, ot=ot, half=half, bk=bk:
                               h.tensor_mul(
                            sb["xlt"][:, ot * 1024 + half * 512:
                                      ot * 1024 + half * 512 + 512],
                            psum[bk][:, :], sb["xsq"][:, XS0:XS0 + 512]),
                            wait=[s3, bufres["xlt"].wait()]
                                 if ot == 0 and half == 0 else [s3])
                        bankres[bk].set(tm_)
                        bufres["xsq"].set(tm_)
                    bufres[wbuf].set(gm)
                tdone = r.mark("dv")
                if tc == 0:
                    dy3 = dbg_dump(r, "mb_t", sb["xlt"][:, 0:8192], [tdone])
                    if dy3 is not None:
                        tdone = dy3
                rb = {}
                for half in range(2):
                    rb[half] = r.emit("tensor", lambda h, half=half: h.matmul(
                        psum[6 + half][:, 0:512], sb["scr8"][64:65, 1024:1152],
                        sb["scr8"][64:65, half * 512:half * 512 + 512],
                        start=True, stop=True),
                        wait=[rowm, bankres[6 + half].wait()], inc=("mm", 1))
                for ot in range(8):
                    wbuf = "wc" if ot % 2 == 0 else "wd"
                    wm = bufres[wbuf].load(
                        weng(wbuf),
                        sb[wbuf][:, 0:1024].rearrange("p (k m) -> p k m", k=8),
                        wbf["w_nl"].rearrange("(k p) m -> p k m", p=128)
                        [:, :, ot * 128:(ot + 1) * 128], wait=INIT + [castm["w_nl"]])
                    glast = None
                    for half in range(2):
                        bk = (ot * 2 + half) % 4
                        for kch in range(8):
                            r.emit("tensor", lambda h, kch=kch, half=half, bk=bk,
                                   wbuf=wbuf: h.matmul(
                                psum[bk][:, :],
                                sb[wbuf][:, kch * 128:(kch + 1) * 128],
                                sb["xlt"][:, kch * 1024 + half * 512:
                                          kch * 1024 + half * 512 + 512],
                                start=(kch == 0), stop=False),
                                wait=[wm, bankres[bk].wait(), tdone]
                                     if kch == 0 else ())
                        r.emit("tensor", lambda h, ot=ot, bk=bk: h.matmul(
                            psum[bk][:, :], sb["brow"][0:1, ot * 128:(ot + 1) * 128],
                            sb["ones_rowb"][0:1, 0:512], start=False, stop=False),
                            wait=[brm] if ot == 0 and half == 0 else ())
                        gl = r.emit("tensor", lambda h, half=half, bk=bk: h.matmul(
                            psum[bk][:, :], sb["ones_rowb"][0:1, 0:128],
                            sb["nmb"][0:1, half * 512:half * 512 + 512],
                            start=False, stop=True),
                            wait=[rowm] if ot == 0 and half == 0 else (),
                            inc=("mm", 1))
                        glast = gl
                        xrb = "xr0" if half == 0 else "xr1"
                        xv = bufres[xrb].load(
                            "sync", sb["xr"][:, half * 512:half * 512 + 512],
                            x_cur.rearrange("(c p) t -> p c t", p=128)
                            [:, ot:ot + 1, t0 + half * 512:t0 + half * 512 + 512]
                            .rearrange("p o t -> p (o t)"),
                            wait=list(xcur_ready))
                        m1_ = r.emit("vector", lambda h, half=half: h.tensor_mul(
                            sb["xsq"][:, XS0:XS0 + 512],
                            sb["xr"][:, half * 512:half * 512 + 512],
                            psum[6 + half][:, 0:512]),
                            wait=[xv, rb[half], bufres["xsq"].wait()])
                        bufres["xsq"].set(m1_)
                        m2_ = r.emit("vector", lambda h, half=half, bk=bk:
                               h.tensor_add(
                            sb["xr"][:, half * 512:half * 512 + 512],
                            sb["xsq"][:, XS0:XS0 + 512], psum[bk][:, :]),
                            wait=[gl])
                        bankres[bk].set(m2_)
                        stm = r.dma("sync",
                                    x_cur.rearrange("(c p) t -> p c t", p=128)
                                    [:, ot:ot + 1, t0 + half * 512:t0 + half * 512 + 512]
                                    .rearrange("p o t -> p (o t)"),
                                    sb["xr"][:, half * 512:half * 512 + 512],
                                    xrb, wait=[m2_])
                        bufres[xrb].set(stm)
                    bufres[wbuf].set(glast)
                bankres[6].set(r.mark("dv"))
                bankres[7].set(r.mark("dv"))
                bufres["xlt"].set(r.mark("mm"))
                bufres["xt"].set(r.mark("mm"))
                bufres["hsb"].set(r.mark("mm"))
                bufres["big0"].set(r.mark("mm"))
                bufres["big1"].set(r.mark("mm"))
                bufres["cs"].set(r.mark("mm"))
            xcur_ready[:] = [r.mark("xr0"), r.mark("xr1")]
            bufres["lcr"].set(r.mark("mm"))
            bufres["car"].set(r.mark("mm"))
            bufres["vsb"].set(r.mark("dv"))
            bufres["brow"].set(r.mark("mm"))

        def phase_ffn1(m_ready):
            brm = bufres["brow"].load("gpsimd", sb["brow"][0:1, :], dr["bsp_row"])
            mld = bufres["hsb"].load(
                "sync", sb["hsb"][:, 0:8192].rearrange("p (k n) -> p k n", k=8),
                m_hbm.rearrange("(k p) n -> p k n", p=128), wait=[m_ready])
            xlres = [None, None]
            for tb in range(8):
                t0 = tb * 512
                xl_base = (tb % 2) * 4096
                bufres["xlt"].set(xlres[tb % 2])
                lnm, xv = emit_ln(t0, 512, xl_base, 512)
                for mt in range(8):
                    bk = mt % 4
                    for kch in range(8):
                        gm = r.emit("tensor", lambda h, bk=bk, kch=kch, mt=mt,
                                    xl_base=xl_base: h.matmul(
                            psum[bk][:, :],
                            sb["hsb"][:, kch * 1024 + mt * 128:
                                      kch * 1024 + (mt + 1) * 128],
                            sb["xlt"][:, xl_base + kch * 512:
                                      xl_base + (kch + 1) * 512],
                            start=(kch == 0), stop=False),
                            wait=[mld, lnm, bankres[bk].wait()] if kch == 0 else ())
                    gm = r.emit("tensor", lambda h, bk=bk, mt=mt: h.matmul(
                        psum[bk][:, :], sb["brow"][0:1, mt * 128:(mt + 1) * 128],
                        sb["ones_rowb"][0:1, :], start=False, stop=True),
                        wait=[brm] if (tb == 0 and mt == 0) else (), inc=("mm", 1))
                    ev = r.emit("vector", lambda h, bk=bk, mt=mt: h.scalar_tensor_tensor(
                        sb["big0"][:, :].bitcast(F32)[:, mt * 512:(mt + 1) * 512],
                        psum[bk][:, :], 0.5,
                        sb["xt"][:, mt * 512:(mt + 1) * 512],
                        op0=ALU.mult, op1=ALU.add),
                        wait=[gm] + ([bufres["big0"].wait()]
                                     if mt == 0 else []))
                    bankres[bk].set(ev)
                dvm = r.mark("dv")
                sv = r.dma("sync",
                           x_cur.rearrange("(c p) t -> p c t", p=128)[:, :, t0:t0 + 512],
                           sb["big0"][:, :].bitcast(F32)[:, 0:8 * 512]
                           .rearrange("p (c w) -> p c w", c=8),
                           "xs", wait=[dvm])
                bufres["big0"].set(sv)
                bufres["xt"].set(dvm)
                xlres[tb % 2] = r.mark("mm")
            bufres["xlt"].set(r.mark("mm"))
            bufres["hsb"].set(r.mark("mm"))
            xcur_ready[:] = [r.mark("xs")]
            bufres["brow"].set(r.mark("mm"))

        m_ready = None
        if "ffn0_a" in phases:
            phase_ffn0()
        if "attn" in phases:
            phase_attn()
        if "ffn0_b" in phases:
            phase_ffn0()
        if "mkM" in phases:
            m_ready = phase_mkM()
        if "ffn1_a" in phases:
            phase_ffn1(m_ready)
            _dm = dbg_dump(r, "xmid", x_cur, list(xcur_ready))
            if _dm is not None:
                xcur_ready.append(_dm)
        if "mamba" in phases:
            phase_mamba()
        if "ffn1_b" in phases:
            phase_ffn1(m_ready)

        for dn, src in (("e_consts", sb["consts"][:, :]),
                        ("e_onesrf", sb["ones_rowf"][0:1, :]),
                        ("e_onesrb", sb["ones_rowb"][0:1, :]),
                        ("e_idb", sb["idb"][:, :]),
                        ("e_stat", sb["stat"][0:1, :])):
            if dn in dbg:
                ew = [(s, v) for s, v in sorted(r.cnt.items())]
                r.dma("sync", dbg[dn], src, "s5", wait=ew)

        fv = r.dma("sync", out_t, x_cur, "s0", wait=list(xcur_ready))
        finals = [(s, v) for s, v in sorted(r.cnt.items())]
        r.emit("sync", lambda h: h.nop(), wait=finals)

        sems = {}
        for sn in SEM_NAMES:
            sems[sn] = es.enter_context(nc.semaphore(sn))
        block = es.enter_context(nc.Block())
        r.replay(block, sems)
    return nc


def kernel(**inputs):
    sh, per_core = host_prep(inputs)
    nc = build_program()
    in_maps = []
    for i in range(8):
        m = dict(sh)
        m.update(per_core[i])
        in_maps.append(m)
    res = run_bass_kernel_spmd(nc, in_maps, core_ids=list(range(8)))
    outs = [np.ascontiguousarray(np.asarray(rr["out"]).T) for rr in res.results]
    return np.stack(outs).astype(np.float32)


# version 11 (final)
# speedup vs baseline: 1.0505x; 1.0092x over previous
from contextlib import ExitStack

import numpy as np
import ml_dtypes

import concourse.bass as bass
import concourse.mybir as mybir
from concourse.bass_utils import run_bass_kernel_spmd

F32 = mybir.dt.float32
BF16 = mybir.dt.bfloat16
AF = mybir.ActivationFunctionType
ALU = mybir.AluOpType
AX = mybir.AxisListType
BF = ml_dtypes.bfloat16

D = 1024
T = 4096
ATTN_PARTS = "ABC"


def _tile_pp(v, dtype=np.float32):
    v = np.asarray(v)
    n = v.shape[0] // 128
    return np.ascontiguousarray(v.reshape(n, 128).T).astype(dtype)


def _dft_consts():
    j = np.arange(D, dtype=np.float64)
    ang = 2.0 * np.pi * np.outer(j, j) / D
    c = lambda a: np.ascontiguousarray(a).astype(BF)
    Fr, Fi = np.cos(ang), -np.sin(ang)
    Gr, Gi = np.cos(ang) / D, np.sin(ang) / D
    return c(Fr), c(Fi), c(-Fi), c(Gr), c(-Gi)


def _rope_consts():
    inv_freq = 1.0 / (10000.0 ** (np.arange(0, D, 2, dtype=np.float32) / D))
    t = np.arange(T, dtype=np.float32)
    freqs = np.outer(t, inv_freq).astype(np.float32)
    cs = np.stack([np.cos(freqs).T, np.sin(freqs).T])
    return np.ascontiguousarray(cs.reshape(2, 4, 128, T)).astype(BF)


def host_prep(inp):
    sh = {}
    f32 = lambda a: np.ascontiguousarray(np.asarray(a)).astype(np.float32)

    sh["w1t"] = f32(np.asarray(inp["L0_ffn_W1"]).T)
    sh["w2t"] = f32(np.asarray(inp["L0_ffn_W2"]).T)
    sh["b1_pp"] = _tile_pp(inp["L0_ffn_b1"])
    sh["b2_row"] = f32(inp["L0_ffn_b2"]).reshape(1, 1024)
    sh["scale_pp"] = np.ascontiguousarray(
        np.broadcast_to(np.asarray(inp["L0_ffn_scale"], np.float32), (128, 1)))

    sh["wqkvt"] = f32(np.asarray(inp["L0_attn_c_attn_w"]).T)
    sh["bqkv_row"] = f32(inp["L0_attn_c_attn_b"]).reshape(1, 3072)
    sh["wprojt"] = f32(np.asarray(inp["L0_attn_c_proj_w"]).T)
    sh["bproj_row"] = f32(inp["L0_attn_c_proj_b"]).reshape(1, 1024)
    sh["rope_csp"] = _rope_consts()

    sh["wfr"] = f32(inp["L1_ffn_w_freq_real"])
    sh["wfi"] = f32(inp["L1_ffn_w_freq_imag"])
    sh["c_fr"], sh["c_fi"], sh["c_fin"], sh["c_gr"], sh["c_gin"] = _dft_consts()
    sh["bsp_row"] = f32(inp["L1_ffn_bias"]).reshape(1, 1024)

    sh["w_in"] = f32(inp["L1_attn_W_in"])
    cw = np.asarray(inp["L1_attn_conv_w"], np.float32)
    sh["convw_pp"] = np.ascontiguousarray(
        np.transpose(cw.reshape(17, 128, 16), (1, 0, 2)).reshape(128, 272))
    sh["convb_pp"] = _tile_pp(inp["L1_attn_conv_b"])
    sh["dtb_col"] = f32(inp["L1_attn_dt_bias"]).reshape(8, 1)
    sh["alog_col"] = f32(inp["L1_attn_A_log"]).reshape(8, 1)
    Dv = np.asarray(inp["L1_attn_D"], np.float32)
    sh["d_pp"] = np.ascontiguousarray(
        np.broadcast_to(np.repeat(Dv, 2)[None, :], (128, 16)))
    ist = np.asarray(inp["L1_attn_init_states"], np.float32)
    sh["h0"] = np.ascontiguousarray(np.transpose(ist, (2, 0, 1)).reshape(64, 2048))
    sh["normw_pp"] = _tile_pp(inp["L1_attn_norm_w"])
    sh["w_out"] = f32(inp["L1_attn_W_out"])
    sh["w_nl"] = f32(inp["L1_attn_W_nl"])
    sh["bnl_row"] = f32(inp["L1_attn_b_nl"]).reshape(1, 1024)

    x = np.asarray(inp["L0_attn_x"], np.float32)
    per_core = [{"x0": np.ascontiguousarray(x[i].T)} for i in range(8)]
    return sh, per_core



ENGS = ("sync", "scalar", "vector", "gpsimd", "tensor")


AUTO_SEM = {"vector": "dv", "scalar": "ac", "gpsimd": "gp"}


class Rec:

    def __init__(self):
        self.items = []
        self.cnt = {}

    def emit(self, eng, fn, wait=(), inc=None, dma=False):
        wd_ = {}
        for x in wait:
            if x is not None:
                wd_[x[0]] = max(wd_.get(x[0], 0), x[1])
        w = list(wd_.items())
        if not dma and eng in AUTO_SEM:
            s = AUTO_SEM[eng]
            prev = self.cnt.get(s, 0)
            if prev > 0:
                w.append((s, prev))
            assert inc is None
            inc = (s, 1)
        if inc is not None:
            self.cnt[inc[0]] = self.cnt.get(inc[0], 0) + inc[1]
        self.items.append((eng, fn, tuple(w), inc))
        return None if inc is None else (inc[0], self.cnt[inc[0]])

    def val(self, sem):
        return self.cnt.get(sem, 0)

    def mark(self, sem):
        return (sem, self.cnt.get(sem, 0))

    def dma(self, eng, out, in_, sem, wait=(), **kw):
        return self.emit(eng, lambda h: h.dma_start(out=out, in_=in_, **kw),
                         wait=wait, inc=(sem, 16), dma=True)

    def replay(self, block, sems):
        def mk(engname):
            def run(h):
                for (eng, fn, waits, inc) in self.items:
                    if eng != engname:
                        continue
                    for (s, v) in waits:
                        h.wait_ge(sems[s], v)
                    ins = fn(h)
                    if inc is not None:
                        ins.then_inc(sems[inc[0]], inc[1])
            return run
        for engname in ENGS:
            getattr(block, engname)(mk(engname))


class Res:
    def __init__(self):
        self.m = None

    def wait(self):
        return self.m

    def set(self, mark):
        self.m = mark


class Buf(Res):

    def __init__(self, rec, sem):
        super().__init__()
        self.r = rec
        self.sem = sem

    def load(self, eng, out, in_, wait=(), **kw):
        sem = self.sem + ("g" if eng == "gpsimd" else "")
        return self.r.dma(eng, out, in_, sem,
                          wait=list(wait) + [self.m], **kw)


BUF_NAMES = ["wa", "wb", "wc", "wd", "xt", "xsq", "xlt", "big0", "big1",
             "ini", "xs", "us", "ms", "s0", "s1", "s2", "s3", "s4", "s5",
             "hsb", "xr0", "xr1", "odd", "tok", "evs", "e2t", "usb",
             "urow", "vsb", "eo", "uh", "vh", "qk", "oo", "brow",
             "kb0", "kb1", "cs", "lcr", "car", "hts", "cvh",
             "s6", "s7", "s8", "s9"]
BUF_SEMS = [n for b in BUF_NAMES for n in (b, b + "g")]
SEM_NAMES = ["mm", "dv", "ac", "gp"] + BUF_SEMS
ALL_PHASES = ("ffn0_a", "attn", "ffn0_b", "mkM", "ffn1_a", "mamba", "ffn1_b")


SCRATCH_SIZE = 16384
BIG1_W = 9216


def build_program(phases=ALL_PHASES, debug_out=(), mamba_tc=4, dram_out=()):
    nc = bass.Bass("TRN2", target_bir_lowering=False,
                   dynamic_dma_scratch_size=SCRATCH_SIZE)
    r = Rec()

    dr = {}
    def din(name, shape, dt=F32):
        dr[name] = nc.dram_tensor(name, list(shape), dt, kind="ExternalInput").ap()
        return dr[name]

    def dscr(name, shape, dt=F32):
        kind = "ExternalOutput" if name in dram_out else "Internal"
        dr[name] = nc.dram_tensor(name, list(shape), dt, kind=kind).ap()
        return dr[name]

    x0 = din("x0", [D, T])
    out_t = nc.dram_tensor("out", [D, T], F32, kind="ExternalOutput").ap()

    din("w1t", [1024, 4096]); din("w2t", [4096, 1024])
    din("b1_pp", [128, 32]); din("b2_row", [1, 1024]); din("scale_pp", [128, 1])
    din("wqkvt", [1024, 3072]); din("bqkv_row", [1, 3072])
    din("wprojt", [1024, 1024]); din("bproj_row", [1, 1024])
    din("rope_csp", [2, 4, 128, T], BF16)
    din("wfr", [1024, 1024]); din("wfi", [1024, 1024])
    for n in ("c_fr", "c_fi", "c_fin", "c_gr", "c_gin"):
        din(n, [1024, 1024], BF16)
    din("bsp_row", [1, 1024])
    din("w_in", [1024, 4232]); din("convw_pp", [128, 272]); din("convb_pp", [128, 17])
    din("dtb_col", [8, 1]); din("alog_col", [8, 1]); din("d_pp", [128, 16])
    din("h0", [64, 2048]); din("normw_pp", [128, 16])
    din("w_out", [2048, 1024]); din("w_nl", [1024, 1024]); din("bnl_row", [1, 1024])

    x_cur = dscr("x_cur", [D, T])
    m_hbm = dscr("m_hbm", [1024, 1024], BF16)
    eo_hbm = dscr("eo_hbm", [16, 1024], BF16)
    ul_hbm = dscr("ul_hbm", [4, 1024], BF16)
    v_hbm = dscr("v_hbm", [4, 4096], BF16)
    conv_hbm = dscr("conv_hbm", [34816, 128], BF16)
    lc_hbm = dscr("lc_hbm", [8, 1024])
    ca_hbm = dscr("ca_hbm", [8, 1024])
    qf_hbm = dscr("qf_hbm", [1024, T], BF16)
    kf_hbm = dscr("kf_hbm", [1024, T], BF16)
    vt_hbm = dscr("vt_hbm", [T, 1024], BF16)
    of_hbm = dscr("of_hbm", [1024, T], BF16)
    u_hbm = dscr("u_hbm", [1024, 1024])
    u2_hbm = dscr("u2_hbm", [1024, 1024])

    dbg = {}
    for (name, shape, dt) in debug_out:
        dbg[name] = nc.dram_tensor(
            "dbg_" + name, list(shape), {"f32": F32, "bf16": BF16}[dt],
            kind="ExternalOutput").ap()

    _dbg_ctr = [0]
    def dbg_dump(r, name, src, wait):
        if name in dbg:
            sem = ["s1", "s2", "s3", "s4", "s5", "s6", "s7", "s8",
                   "s9"][_dbg_ctr[0] % 9]
            _dbg_ctr[0] += 1
            return r.dma("sync", dbg[name], src, sem, wait=wait)
        return None

    es = ExitStack()
    with es:
        sb = {}
        def S(name, shape, dt=F32):
            sb[name] = es.enter_context(nc.sbuf_tensor("sb_" + name, list(shape), dt))
            return sb[name]

        S("consts", [128, 8])
        S("ones_rowb", [1, 512], BF16)
        S("ones_rowf", [1, 128])
        S("ones_colf", [128, 1])
        S("ones_colb", [128, 1], BF16)
        S("idb", [128, 128], BF16)
        S("cs", [128, 1040], BF16)
        S("brow", [1, 1024], BF16)
        S("b1_pp", [128, 32])
        S("bqkv_row", [1, 3072], BF16)
        S("scale_pp", [128, 1])
        S("xt", [128, 8 * 512])
        S("xsq", [128, 2 * 512])
        S("stat", [1, 4 * 512])

        S("xlt", [128, 8 * 1024], BF16)
        S("big0", [128, 9216], BF16)
        S("big1", [128, BIG1_W], BF16)
        sb["mks"] = sb["big0"][:, 4096:4608]
        S("wa", [128, 2048], BF16)
        S("wb", [128, 2048], BF16)
        S("wc", [128, 2048], BF16)
        S("wd", [128, 2048], BF16)
        S("hsb", [128, 32 * 1024], BF16)
        S("xr", [128, 1024])
        S("odd", [128, 64], BF16)
        sb["tok"] = sb["wd"][0:16, 0:1024]
        S("evs", [1, 4096], BF16)
        S("e2t", [4, 1024], BF16)
        S("usb", [4, 1024], BF16)
        S("urow", [1, 4096], BF16)
        S("scr8", [128, 2048])
        sb["vsb"] = sb["scr8"][:, :].bitcast(BF16)[0:4, :]
        sb["hts"] = sb["scr8"][0:64, :]
        sb["nmb"] = sb["usb"][0:1, 0:1024]
        sb["lcr"] = sb["urow"][0:1, 0:2048].bitcast(F32)
        sb["car"] = sb["urow"][0:1, 2048:3072]
        S("sm128", [128, 768])
        sb["idf"] = sb["sm128"][:, 0:128]
        sb["ngt"] = sb["sm128"][:, 128:256]
        sb["cvw"] = sb["sm128"][:, 256:528]
        sb["cvb"] = sb["sm128"][:, 528:545]
        sb["dpp"] = sb["sm128"][:, 545:561]
        sb["nw"] = sb["sm128"][:, 561:577]
        sb["z8"] = sb["sm128"][0:8, 577:705]
        sb["acol"] = sb["sm128"][0:8, 705:707]
        sb["dtb8"] = sb["sm128"][0:8, 707:708]

        psum = {}
        for i in range(8):
            psum[i] = es.enter_context(nc.psum_tensor(f"p{i}", [128, 512], F32))

        def weng(b):
            return "gpsimd"

        bankres = {i: Res() for i in range(8)}
        bufres = {n: Buf(r, n) for n in (
            "wa", "wb", "wc", "wd", "xt", "xlt", "big0", "big1", "xsq",
            "hsb", "xr0", "xr1", "odd", "tok", "eor", "evs", "e2t", "usb",
            "urow", "vsb", "eo", "uh", "vh", "qk", "oo", "brow",
            "kb0", "kb1", "cs", "lcr", "car", "hts", "cvh")}
        ini = Buf(r, "ini")
        xcur_ready = []

        cvals = [1e-5, -0.5, 0.5, -1.0, 2.0, 1.0, 30.0, 0.0]
        for i, v in enumerate(cvals):
            r.emit("gpsimd", lambda h, i=i, v=v: h.memset(sb["consts"][:, i:i + 1], v))
            nc.const_aps.register(F32, v, sb["consts"][:, i:i + 1])
        r.emit("gpsimd", lambda h: h.memset(sb["ones_rowb"][:, :], 1.0))
        r.emit("gpsimd", lambda h: h.memset(sb["ones_rowf"][:, :], 1.0))
        r.emit("gpsimd", lambda h: h.memset(sb["ones_colf"][:, :], 1.0))
        r.emit("gpsimd", lambda h: h.memset(sb["ones_colb"][:, :], 1.0))
        r.emit("gpsimd", lambda h: h.memset(sb["idb"][:, :], 1.0))
        r.emit("gpsimd", lambda h: h.affine_select(
            sb["idb"][:, :], sb["idb"][:, :], pattern=[[1, 128]],
            compare_op=ALU.is_equal, fill=0.0, base=0, channel_multiplier=-1))

        ini.load("sync", sb["b1_pp"][:, :], dr["b1_pp"])
        ini.load("gpsimd", sb["bqkv_row"][0:1, :], dr["bqkv_row"])
        ini.load("sync", sb["scale_pp"][:, :], dr["scale_pp"])
        ini.load("sync", sb["cvw"][:, :], dr["convw_pp"])
        ini.load("sync", sb["cvb"][:, :], dr["convb_pp"])
        ini.load("sync", sb["dtb8"][0:8, :], dr["dtb_col"])
        ini.load("sync", sb["dpp"][:, :], dr["d_pp"])
        ini.load("sync", sb["nw"][:, :], dr["normw_pp"])
        ini.load("sync", sb["acol"][0:8, 1:2], dr["alog_col"])
        r.emit("gpsimd", lambda h: h.memset(sb["idf"][:, :], 1.0))
        r.emit("gpsimd", lambda h: h.affine_select(
            sb["idf"][:, :], sb["idf"][:, :], pattern=[[1, 128]],
            compare_op=ALU.is_equal, fill=0.0, base=0, channel_multiplier=-1))
        r.emit("gpsimd", lambda h: h.memset(sb["ngt"][:, :], 0.0))
        r.emit("gpsimd", lambda h: h.affine_select(
            sb["ngt"][:, :], sb["ngt"][:, :], pattern=[[1, 128]],
            compare_op=ALU.is_ge, fill=-1e9, base=0, channel_multiplier=-1))
        r.emit("gpsimd", lambda h: h.memset(sb["z8"][0:8, :], 0.0))
        xcur_ready[:] = [r.dma("sync", x_cur, x0, "xs")]
        WSHAPES = {"w1t": [1024, 4096], "w2t": [4096, 1024],
                   "wqkvt": [1024, 3072], "wprojt": [1024, 1024],
                   "wfr": [1024, 1024], "wfi": [1024, 1024],
                   "w_in": [1024, 4232], "w_out": [2048, 1024],
                   "w_nl": [1024, 1024]}
        wbf = {}
        castm = {}
        CAST_SEMS = ["s6g", "s7g", "s8g", "s9g", "usg", "msg", "qkg",
                     "uhg", "vhg"]
        for i, wn in enumerate(("w1t", "w2t", "wqkvt", "wprojt", "wfr", "wfi",
                                "w_in", "w_out", "w_nl")):
            wbf[wn] = dscr(wn + "_bf", WSHAPES[wn], BF16)
            castm[wn] = r.dma("gpsimd", wbf[wn], dr[wn], CAST_SEMS[i])
        INIT = [r.mark("gp"), r.mark("ini"), r.mark("inig")]


        def emit_ln(t0, tw, xl_base, xl_stride, keep_x=True):
            if not hasattr(emit_ln, "dump_this"):
                emit_ln.dump_this = False
            st = sb["stat"]
            ST_TMP, ST_MEAN, ST_VAR, ST_RSTD, ST_NM = 0, 512, 1024, 1536, 1024
            xv = bufres["xt"].load(
                "sync", sb["xt"][:, 0:8 * tw].rearrange("p (c w) -> p c w", c=8),
                x_cur.rearrange("(c p) t -> p c t", p=128)[:, :, t0:t0 + tw],
                wait=INIT + list(xcur_ready))
            for c in range(8):
                r.emit("tensor", lambda h, c=c, tw=tw: h.matmul(
                    psum[6][0:1, 0:tw], sb["ones_colf"][:, 0:1],
                    sb["xt"][:, c * tw:(c + 1) * tw],
                    start=(c == 0), stop=(c == 7)),
                    wait=[xv, bankres[6].wait()] if c == 0 else ())
            sqmm = {}
            for c in range(8):
                w = [xv] if c == 0 else []
                if c >= 2:
                    w.append(sqmm[c - 2])
                if c == 0:
                    w.append(bufres["xsq"].wait())
                sqm = r.emit("scalar", lambda h, c=c, tw=tw: h.activation(
                    out=sb["xsq"][:, (c % 2) * tw:(c % 2) * tw + tw],
                    in_=sb["xt"][:, c * tw:(c + 1) * tw], func=AF.Square),
                    wait=w)
                smm = r.emit("tensor", lambda h, c=c, tw=tw: h.matmul(
                    psum[7][0:1, 0:tw], sb["ones_colf"][:, 0:1],
                    sb["xsq"][:, (c % 2) * tw:(c % 2) * tw + tw],
                    start=(c == 0), stop=(c == 7)),
                    wait=[sqm] + ([bankres[7].wait()] if c == 0 else []),
                    inc=("mm", 1))
                sqmm[c] = smm
            bufres["xsq"].set(smm)
            r.emit("vector", lambda h, tw=tw: h.tensor_scalar_mul(
                st[0:1, ST_MEAN:ST_MEAN + tw], psum[6][0:1, 0:tw], 1.0 / 1024.0),
                wait=[smm])
            r.emit("vector", lambda h, tw=tw: h.tensor_scalar_mul(
                st[0:1, ST_VAR:ST_VAR + tw], psum[7][0:1, 0:tw], 1.0 / 1024.0))
            r.emit("vector", lambda h, tw=tw: h.tensor_mul(
                st[0:1, ST_TMP:ST_TMP + tw], st[0:1, ST_MEAN:ST_MEAN + tw],
                st[0:1, ST_MEAN:ST_MEAN + tw]))
            r.emit("vector", lambda h, tw=tw: h.tensor_sub(
                st[0:1, ST_VAR:ST_VAR + tw], st[0:1, ST_VAR:ST_VAR + tw],
                st[0:1, ST_TMP:ST_TMP + tw]))
            dvm = r.mark("dv")
            r.emit("scalar", lambda h, tw=tw: h.activation(
                out=st[0:1, ST_TMP:ST_TMP + tw], in_=st[0:1, ST_VAR:ST_VAR + tw],
                func=AF.Ln, bias=1e-5), wait=[dvm])
            r.emit("scalar", lambda h, tw=tw: h.activation(
                out=st[0:1, ST_RSTD:ST_RSTD + tw], in_=st[0:1, ST_TMP:ST_TMP + tw],
                func=AF.Exp, scale=-0.5))
            acm = r.mark("ac")
            r.emit("vector", lambda h, tw=tw: h.tensor_mul(
                st[0:1, ST_NM:ST_NM + tw], st[0:1, ST_MEAN:ST_MEAN + tw],
                st[0:1, ST_RSTD:ST_RSTD + tw]), wait=[acm])
            r.emit("vector", lambda h, tw=tw: h.tensor_scalar_mul(
                st[0:1, ST_NM:ST_NM + tw], st[0:1, ST_NM:ST_NM + tw], -1.0))
            dvm2 = r.mark("dv")
            r.emit("tensor", lambda h, tw=tw: h.matmul(
                psum[6][:, 0:tw], sb["ones_rowf"][0:1, :],
                st[0:1, ST_RSTD:ST_RSTD + tw], start=True, stop=True),
                wait=[dvm2])
            bmm = r.emit("tensor", lambda h, tw=tw: h.matmul(
                psum[7][:, 0:tw], sb["ones_rowf"][0:1, :],
                st[0:1, ST_NM:ST_NM + tw], start=True, stop=True),
                inc=("mm", 1))
            lastdv = None
            for c in range(8):
                r.emit("vector", lambda h, c=c, tw=tw: h.tensor_mul(
                    sb["xsq"][:, 0:tw], sb["xt"][:, c * tw:(c + 1) * tw],
                    psum[6][:, 0:tw]),
                    wait=[bmm, bufres["xlt"].wait()] if c == 0 else ())
                lastdv = r.emit("vector", lambda h, c=c, tw=tw, xl_base=xl_base,
                                xl_stride=xl_stride: h.tensor_add(
                    sb["xlt"][:, xl_base + c * xl_stride:xl_base + c * xl_stride + tw],
                    sb["xsq"][:, 0:tw], psum[7][:, 0:tw]))
            if "ln_ps" in dbg and emit_ln.dump_this:
                emit_ln.dump_this = False
                pe6 = r.emit("scalar", lambda h, tw=tw: h.activation(
                    out=sb["xsq"][:, 0:tw], in_=psum[6][:, 0:tw],
                    func=AF.Identity), wait=[lastdv])
                pe7 = r.emit("scalar", lambda h, tw=tw: h.activation(
                    out=sb["xsq"][:, tw:2 * tw], in_=psum[7][:, 0:tw],
                    func=AF.Identity))
                pdm = r.dma("sync", dbg["ln_ps"], sb["xsq"][:, 0:2 * tw], "s4",
                            wait=[pe7])
                lastdv = r.emit("vector", lambda h: h.tensor_copy(
                    sb["xsq"][:, 0:1], sb["xsq"][:, 0:1]), wait=[pdm, pe6])
                bufres["xsq"].set(lastdv)
            bankres[6].set(lastdv)
            bankres[7].set(lastdv)
            bufres["xt"].set(lastdv)
            return lastdv, xv

        def phase_ffn0():
            ptb = psum[5][:, :].bitcast(BF16)
            brm = bufres["brow"].load("gpsimd", sb["brow"][0:1, :], dr["b2_row"])

            def lora_prep(lnm):
                tkm = None
                for c in range(8):
                    tm = r.emit("tensor", lambda h, c=c: h.transpose(
                        ptb[0:16, 0:128], sb["xlt"][:, c * 1024 + 1:c * 1024 + 17],
                        sb["idb"][:, :]),
                        wait=[lnm, bankres[5].wait()] if c == 0 else [bankres[5].wait()],
                        inc=("mm", 1))
                    tkm = r.emit("scalar", lambda h, c=c: h.activation(
                        out=sb["tok"][0:16, c * 128:(c + 1) * 128],
                        in_=ptb[0:16, 0:128], func=AF.Identity),
                        wait=[tm, bufres["tok"].wait()])
                    bankres[5].set(tkm)
                som = r.dma("sync", eo_hbm, sb["tok"][0:16, :], "eo",
                            wait=[tkm, r.mark("evs"), r.mark("e2t")])
                bufres["tok"].set(som)
                erm = bufres["evs"].load(
                    "sync", sb["evs"][0:1, :].rearrange("o (v d) -> o v d", v=4),
                    eo_hbm[0:8:2, :].rearrange("v d -> () v d"), wait=[som])
                e2m = None
                for ip in range(4):
                    e2m = bufres["e2t"].load(
                        "sync", sb["e2t"][0:4, ip * 256:(ip + 1) * 256],
                        eo_hbm[8 + 2 * ip:9 + 2 * ip, :].rearrange(
                            "o (q j) -> (o j) q", j=4),
                        wait=[som], allow_slow_non_contiguous=True)
                evm_ = r.emit("vector", lambda h: h.tensor_scalar_mul(
                    sb["evs"][0:1, :], sb["evs"][0:1, :],
                    sb["scale_pp"][0:1, 0:1]), wait=[erm])
                _d = dbg_dump(r, "l_evs", sb["evs"][0:1, :], [evm_])
                evm_ = _d or evm_
                _d = dbg_dump(r, "l_e2t", sb["e2t"][0:4, :], [e2m])
                e2m = _d or e2m

                om = None
                for c in range(8):
                    om = r.emit("vector", lambda h, c=c: h.tensor_copy(
                        sb["odd"][:, c * 8:(c + 1) * 8],
                        sb["xlt"][:, c * 1024 + 2:c * 1024 + 18:2]),
                        wait=[bufres["odd"].wait()] if c == 0 else ())
                _d = dbg_dump(r, "l_odd", sb["odd"][:, :], [om])
                om = _d or om
                return evm_, om, e2m

            lora = [None, None, None]
            for tb in range(4):
                t0 = tb * 1024
                if tb == 1:
                    emit_ln.dump_this = True
                ln0, _ = emit_ln(t0, 512, 0, 1024)
                ln1, _ = emit_ln(t0 + 512, 512, 512, 1024)
                if tb == 0:
                    lora[0], lora[1], lora[2] = lora_prep(ln0)
                if tb == 1:
                    _d = dbg_dump(r, "l_xlt1", sb["xlt"][:, :], [ln0, ln1])
                    if _d is not None:
                        ln0 = ln1 = _d
                    _d = dbg_dump(r, "l_xt1", sb["xt"][:, :], [ln1])
                    _d = dbg_dump(r, "l_stat1", sb["stat"][0:1, :], [ln1])
                um = None
                for half in range(2):
                    for c in range(8):
                        r.emit("tensor", lambda h, c=c, half=half: h.matmul(
                            psum[4][0:4, 0:512],
                            sb["odd"][:, c * 8:c * 8 + 4],
                            sb["xlt"][:, c * 1024 + half * 512:c * 1024 + half * 512 + 512],
                            start=(c == 0), stop=(c == 7)),
                            wait=[lora[1], ln0, ln1, bankres[4].wait()] if c == 0 else (),
                            inc=("mm", 1) if c == 7 else None)
                    um = r.emit("scalar", lambda h, half=half: h.activation(
                        out=sb["usb"][0:4, half * 512:half * 512 + 512],
                        in_=psum[4][0:4, 0:512], func=AF.Identity),
                        wait=[r.mark("mm"), bufres["usb"].wait()])
                    bankres[4].set(um)
                usm = r.dma("sync", ul_hbm, sb["usb"][0:4, :], "uh",
                            wait=[um, bufres["uh"].wait()])
                bufres["usb"].set(usm)
                urm = bufres["urow"].load(
                    "sync", sb["urow"][0:1, :],
                    ul_hbm.rearrange("i t -> () (i t)"), wait=[usm])
                if tb == 1:
                    _d = dbg_dump(r, "l_urow", sb["urow"][0:1, 0:4096], [urm])
                    urm = _d or urm
                bufres["uh"].set(urm)
                hm = None
                w1last = None
                for ht in range(32):
                    wbuf = "wa" if ht % 2 == 0 else "wb"
                    wm = bufres[wbuf].load(
                        weng(wbuf), sb[wbuf][:, 0:1024].rearrange("p (k m) -> p k m", k=8),
                        wbf["w1t"].rearrange("(k p) m -> p k m", p=128)
                        [:, :, ht * 128:(ht + 1) * 128], wait=INIT + [castm["w1t"]])
                    for half in range(2):
                        bk = (ht * 2 + half) % 4
                        for kch in range(8):
                            r.emit("tensor", lambda h, kch=kch, half=half, bk=bk,
                                   wbuf=wbuf: h.matmul(
                                psum[bk][:, :],
                                sb[wbuf][:, kch * 128:(kch + 1) * 128],
                                sb["xlt"][:, kch * 1024 + half * 512:
                                          kch * 1024 + half * 512 + 512],
                                start=(kch == 0), stop=False),
                                wait=([wm, bankres[bk].wait()] if kch == 0 else []) +
                                     ([ln0, ln1] if (kch == 0 and ht == 0) else []))
                        gl = r.emit("tensor", lambda h, ht=ht, half=half, bk=bk: h.matmul(
                            psum[bk][:, :],
                            sb["evs"][0:1, (ht // 8) * 1024 + (ht % 8) * 128:
                                      (ht // 8) * 1024 + (ht % 8) * 128 + 128],
                            sb["urow"][0:1, (ht // 8) * 1024 + half * 512:
                                       (ht // 8) * 1024 + half * 512 + 512],
                            start=False, stop=True),
                            wait=[urm, lora[0]] if ht == 0 and half == 0 else (),
                            inc=("mm", 1))
                        w1last = gl
                        hm = r.emit("scalar", lambda h, ht=ht, half=half, bk=bk:
                                    h.activation(
                            out=sb["hsb"][:, ht * 1024 + half * 512:
                                          ht * 1024 + half * 512 + 512],
                            in_=psum[bk][:, :], func=AF.Gelu,
                            bias=sb["b1_pp"][:, ht:ht + 1]),
                            wait=[gl] + ([bufres["hsb"].wait()] if ht == 0 and half == 0 else []))
                        bankres[bk].set(hm)
                    bufres[wbuf].set(w1last)
                bufres["xlt"].set(w1last)
                if tb == 1:
                    _d = dbg_dump(r, "l_h", sb["hsb"][:, 0:4096], [hm])
                    hm = _d or hm
                vm = None
                for j in range(4):
                    for half in range(2):
                        for c in range(8):
                            r.emit("tensor", lambda h, c=c, j=j, half=half: h.matmul(
                                psum[4][0:4, 0:512],
                                sb["odd"][:, c * 8 + 4:c * 8 + 8],
                                sb["hsb"][:, (j * 8 + c) * 1024 + half * 512:
                                          (j * 8 + c) * 1024 + half * 512 + 512],
                                start=(c == 0), stop=(c == 7)),
                                wait=[hm, bankres[4].wait()] if c == 0 else (),
                                inc=("mm", 1) if c == 7 else None)
                        vm = r.emit("scalar", lambda h, j=j, half=half: h.activation(
                            out=sb["vsb"][0:4, j * 1024 + half * 512:
                                          j * 1024 + half * 512 + 512],
                            in_=psum[4][0:4, 0:512], func=AF.Identity,
                            scale=sb["scale_pp"][0:4, 0:1]),
                            wait=[r.mark("mm")] +
                                 ([bufres["vsb"].wait()] if j == 0 and half == 0 else []))
                        bankres[4].set(vm)
                vsm = r.dma("sync", v_hbm, sb["vsb"][0:4, :], "vh",
                            wait=[vm, bufres["vh"].wait()])
                bufres["vsb"].set(vsm)
                vtm = None
                for ip in range(4):
                    vtm = bufres["vsb"].load(
                        "sync", sb["vsb"][0:4, ip * 1024:(ip + 1) * 1024],
                        v_hbm[ip:ip + 1, :].rearrange("o (j t) -> (o j) t", j=4),
                        wait=[vsm])
                if tb == 1:
                    _d = dbg_dump(r, "l_vsb", sb["vsb"][0:4, :], [vtm])
                    vtm = _d or vtm
                bufres["vh"].set(vtm)
                w2last = None
                for ot in range(8):
                    wbuf = ("wa", "wc") if ot % 2 == 0 else ("wb", "wd")
                    wms = []
                    for gi, wb_ in enumerate(wbuf):
                        wms.append(bufres[wb_].load(
                            weng(wb_),
                            sb[wb_][:, 0:2048].rearrange("p (k m) -> p k m", k=16),
                            wbf["w2t"].rearrange("(k p) m -> p k m", p=128)
                            [:, gi * 16:(gi + 1) * 16, ot * 128:(ot + 1) * 128],
                            wait=INIT + [castm["w2t"]]))
                    ip = ot // 2
                    qh = ot % 2
                    for half in range(2):
                        bk = (ot * 2 + half) % 4
                        for kch in range(32):
                            wb_ = wbuf[kch // 16]
                            r.emit("tensor", lambda h, kch=kch, half=half, bk=bk,
                                   wb_=wb_: h.matmul(
                                psum[bk][:, :],
                                sb[wb_][:, (kch % 16) * 128:(kch % 16 + 1) * 128],
                                sb["hsb"][:, kch * 1024 + half * 512:
                                          kch * 1024 + half * 512 + 512],
                                start=(kch == 0), stop=False),
                                wait=(wms + [bankres[bk].wait()]) if kch == 0 else ())
                        r.emit("tensor", lambda h, ip=ip, qh=qh, half=half, bk=bk: h.matmul(
                            psum[bk][:, :],
                            sb["e2t"][0:4, ip * 256 + qh * 128:ip * 256 + qh * 128 + 128],
                            sb["vsb"][0:4, ip * 1024 + half * 512:
                                      ip * 1024 + half * 512 + 512],
                            start=False, stop=False),
                            wait=[vtm, lora[2]] if ot == 0 and half == 0 else ())
                        gl = r.emit("tensor", lambda h, ot=ot, bk=bk: h.matmul(
                            psum[bk][:, :],
                            sb["brow"][0:1, ot * 128:(ot + 1) * 128],
                            sb["ones_rowb"][0:1, 0:512],
                            start=False, stop=True),
                            wait=[brm] if ot == 0 else (), inc=("mm", 1))
                        w2last = gl
                        xrb = "xr0" if half == 0 else "xr1"
                        xv = bufres[xrb].load(
                            "sync", sb["xr"][:, half * 512:half * 512 + 512],
                            x_cur.rearrange("(c p) t -> p c t", p=128)
                            [:, ot:ot + 1, t0 + half * 512:t0 + half * 512 + 512]
                            .rearrange("p o t -> p (o t)"),
                            wait=list(xcur_ready))
                        sm = r.emit("vector", lambda h, half=half, bk=bk:
                                    h.scalar_tensor_tensor(
                            sb["xr"][:, half * 512:half * 512 + 512],
                            psum[bk][:, :], 0.5,
                            sb["xr"][:, half * 512:half * 512 + 512],
                            op0=ALU.mult, op1=ALU.add), wait=[gl, xv])
                        bankres[bk].set(sm)
                        stm = r.dma("sync",
                                    x_cur.rearrange("(c p) t -> p c t", p=128)
                                    [:, ot:ot + 1, t0 + half * 512:t0 + half * 512 + 512]
                                    .rearrange("p o t -> p (o t)"),
                                    sb["xr"][:, half * 512:half * 512 + 512],
                                    xrb, wait=[sm])
                        bufres[xrb].set(stm)
                    for wb_ in wbuf:
                        bufres[wb_].set(w2last)
                bufres["hsb"].set(w2last)
                bufres["vsb"].set(w2last)
                bufres["urow"].set(w1last)
            xcur_ready[:] = [r.mark("xr0"), r.mark("xr1")]
            bufres["brow"].set(r.mark("mm"))
            bufres["odd"].set(r.mark("mm"))
            bufres["e2t"].set(r.mark("mm"))
            bufres["evs"].set(r.mark("mm"))

        def phase_attn():
            XS0, XS1 = 0, 512
            for tb in range(4 if "A" in ATTN_PARTS else 0):
                t0 = tb * 1024
                ln0, _ = emit_ln(t0, 512, 0, 1024)
                ln1, _ = emit_ln(t0 + 512, 512, 512, 1024)

                def qkv_group(mt, bk, wm, half, wbuf, extra_wait=()):
                    for kch in range(8):
                        r.emit("tensor", lambda h, kch=kch, bk=bk, half=half,
                               wbuf=wbuf: h.matmul(
                            psum[bk][:, :],
                            sb[wbuf][:, kch * 128:(kch + 1) * 128],
                            sb["xlt"][:, kch * 1024 + half * 512:
                                      kch * 1024 + half * 512 + 512],
                            start=(kch == 0), stop=False),
                            wait=([wm, bankres[bk].wait(), ln0, ln1] +
                                  list(extra_wait)) if kch == 0 else ())
                    return r.emit("tensor", lambda h, mt=mt, bk=bk: h.matmul(
                        psum[bk][:, :],
                        sb["bqkv_row"][0:1, mt * 128:(mt + 1) * 128],
                        sb["ones_rowb"][0:1, 0:512],
                        start=False, stop=True), wait=INIT, inc=("mm", 1))

                def wload(mt, wbuf):
                    return bufres[wbuf].load(
                        weng(wbuf),
                        sb[wbuf][:, 0:1024].rearrange("p (k m) -> p k m", k=8),
                        wbf["wqkvt"].rearrange("(k p) m -> p k m", p=128)
                        [:, :, mt * 128:(mt + 1) * 128], wait=INIT + [castm["wqkvt"]])

                QK = (("q", 0, qf_hbm), ("k", 8, kf_hbm))
                iters = [(qi, i, half) for qi in range(2)
                         for i in range(4) for half in range(2)]
                csres = [None, None]

                def cs_load(n, t0=t0):
                    qi, i, half = iters[n]
                    slot = n % 2
                    m = r.dma(
                        "sync",
                        sb["big1"][:, slot * 1024:(slot + 1) * 1024]
                        .rearrange("p (u w) -> p u w", u=2),
                        dr["rope_csp"][:, i:i + 1, :,
                                       t0 + half * 512:t0 + half * 512 + 512]
                        .rearrange("u o p t -> p (u o) t"),
                        ("cs", "qk")[slot], wait=[csres[slot]])
                    return m

                csm_pref = {0: cs_load(0)}
                wlo = whi = None
                glo = ghi = None
                for n, (qi, i, half) in enumerate(iters):
                    qk, base, dsthbm = QK[qi]
                    slot = n % 2
                    cs_b = sb["big1"][:, slot * 1024:(slot + 1) * 1024]
                    if half == 0:
                        wblo = "wa" if i % 2 == 0 else "wc"
                        wbhi = "wb" if i % 2 == 0 else "wd"
                        wlo = wload(base + i, wblo)
                        whi = wload(base + i + 4, wbhi)
                    blo, bhi = (0, 1) if (i + half) % 2 == 0 else (2, 3)
                    glo = qkv_group(base + i, blo, wlo, half, wblo)
                    ghi = qkv_group(base + i + 4, bhi, whi, half, wbhi)
                    csm = csm_pref[n]
                    if n + 1 < len(iters):
                        csm_pref[n + 1] = cs_load(n + 1)
                    r.emit("vector", lambda h, blo=blo, cs_b=cs_b: h.tensor_mul(
                        sb["xsq"][:, XS0:XS0 + 512], psum[blo][:, :],
                        cs_b[:, 0:512]),
                        wait=[glo, ghi, csm, bufres["xsq"].wait()])
                    r.emit("vector", lambda h, bhi=bhi, cs_b=cs_b: h.tensor_mul(
                        sb["xsq"][:, XS1:XS1 + 512], psum[bhi][:, :],
                        cs_b[:, 512:1024]))
                    r.emit("vector", lambda h: h.tensor_sub(
                        sb["wc"][:, 1024:1536],
                        sb["xsq"][:, XS0:XS0 + 512],
                        sb["xsq"][:, XS1:XS1 + 512]),
                        wait=[bufres["wc"].wait()])
                    r.emit("vector", lambda h, bhi=bhi, cs_b=cs_b: h.tensor_mul(
                        sb["xsq"][:, XS0:XS0 + 512], psum[bhi][:, :],
                        cs_b[:, 0:512]))
                    rlast = r.emit("vector", lambda h, blo=blo, cs_b=cs_b: h.tensor_mul(
                        sb["xsq"][:, XS1:XS1 + 512], psum[blo][:, :],
                        cs_b[:, 512:1024]))
                    bankres[blo].set(rlast)
                    bankres[bhi].set(rlast)
                    csres[slot] = rlast
                    rlast = r.emit("vector", lambda h: h.tensor_add(
                        sb["wc"][:, 1536:2048],
                        sb["xsq"][:, XS0:XS0 + 512],
                        sb["xsq"][:, XS1:XS1 + 512]))
                    bufres["xsq"].set(rlast)
                    sm = r.dma("sync",
                               dsthbm.rearrange("(g r p) t -> p g r t",
                                                g=2, p=128)
                               [:, :, i, t0 + half * 512:t0 + half * 512 + 512],
                               sb["wc"][:, 1024:2048].rearrange(
                                   "p (g w) -> p g w", g=2),
                               "wc", wait=[rlast])
                    bufres["wc"].set(sm)
                    if half == 1:
                        if wblo != "wc":
                            bufres[wblo].set(glo)
                        bufres[wbhi].set(ghi)
                bufres["big1"].set(r.mark("dv"))
                wcres = [None, None]
                pend = None

                def vflush(pend, tb=tb):
                    em, bk, dv_, half_, slot = pend
                    ptv = psum[bk][:, :].bitcast(BF16)
                    trm = None
                    for s4 in range(4):
                        trm = r.emit("tensor", lambda h, s4=s4, ptv=ptv, slot=slot:
                               h.transpose(
                            ptv[:, s4 * 128:(s4 + 1) * 128],
                            sb["wc"][:, slot * 512 + s4 * 128:
                                     slot * 512 + (s4 + 1) * 128],
                            sb["idb"][:, :]),
                            wait=[em] if s4 == 0 else (),
                            inc=("mm", 1) if s4 == 3 else None)
                    wcres[slot] = trm
                    evm2 = r.emit("scalar", lambda h, ptv=ptv, slot=slot: h.activation(
                        out=sb["wd"][:, slot * 512:slot * 512 + 512],
                        in_=ptv[:, 0:512], func=AF.Identity),
                        wait=[trm, bufres["wd"].wait()])
                    bankres[bk].set(evm2)
                    sm = r.dma("sync",
                               vt_hbm.rearrange("(g p) d -> p g d", p=128)
                               [:, tb * 8 + half_ * 4:tb * 8 + half_ * 4 + 4,
                                dv_ * 128:(dv_ + 1) * 128],
                               sb["wd"][:, slot * 512:slot * 512 + 512]
                               .rearrange("p (s w) -> p s w", s=4),
                               "wd", wait=[evm2])
                    bufres["wd"].set(sm)

                wm = None
                vwb = None
                for n in range(16):
                    dv, half = n // 2, n % 2
                    if half == 0:
                        vwb = "wa" if dv % 2 == 0 else "wb"
                        wm = wload(16 + dv, vwb)
                    bk = 4 if n % 2 == 0 else 5
                    gm = qkv_group(16 + dv, bk, wm, half, vwb)
                    if half == 1:
                        bufres[vwb].set(gm)
                    slot = n % 2
                    em = r.emit("scalar", lambda h, bk=bk, slot=slot: h.activation(
                        out=sb["wc"][:, slot * 512:slot * 512 + 512],
                        in_=psum[bk][:, :], func=AF.Identity),
                        wait=[gm, wcres[slot], bufres["wc"].wait()
                              if n == 0 else None])
                    pend_new = (em, bk, dv, half, slot)
                    if pend is not None:
                        vflush(pend)
                    pend = pend_new
                vflush(pend)
                bufres["wc"].set(max(w for w in wcres if w is not None))
            qkms = [r.mark("wc"), r.mark("wd")]
            dbg_dump(r, "a_qf", qf_hbm, qkms)
            dbg_dump(r, "a_kf", kf_hbm, qkms)
            dbg_dump(r, "a_vt", vt_hbm, qkms)
            if "B" not in ATTN_PARTS:
                vlm = None
            else:
                vlm = bufres["hsb"].load(
                "sync", sb["hsb"][:, :].rearrange("p (tt d) -> p tt d", tt=32),
                vt_hbm.rearrange("(tt p) d -> p tt d", p=128), wait=qkms)
            M_, L_, NGM, RS = 0, 1, 2, 3
            KMX, KSM = 8, 16
            prev_mask_use = None
            prev_stdm = None
            prev_pem = None
            prev_oev = None
            for tq in range(32 if "B" in ATTN_PARTS else 0):
                qm = bufres["wc"].load(
                    "sync", sb["wc"][:, 0:1024].rearrange("p (c m) -> p c m", c=8),
                    qf_hbm.rearrange("(c p) t -> p c t", p=128)
                    [:, :, tq * 128:(tq + 1) * 128], wait=qkms)
                dkb = tq // 4
                r.emit("gpsimd", lambda h: h.memset(sb["mks"][:, :], 0.0),
                       wait=[prev_mask_use])
                mkm = r.emit("gpsimd", lambda h, tq=tq: h.affine_select(
                    sb["mks"][:, :], sb["mks"][:, :], pattern=[[-1, 512]],
                    compare_op=ALU.is_ge, fill=-1e9,
                    base=(tq % 4) * 128, channel_multiplier=1))

                def emit_S(kb, qm=qm, dkb=dkb, mkm=mkm):
                    kbuf = "kb0" if kb % 2 == 0 else "kb1"
                    koff = 0 if kb % 2 == 0 else 4608
                    km = bufres[kbuf].load(
                        "sync",
                        sb["big0"][:, koff:koff + 4096].rearrange(
                            "p (c w) -> p c w", c=8),
                        kf_hbm.rearrange("(c p) t -> p c t", p=128)
                        [:, :, kb * 512:(kb + 1) * 512],
                        wait=qkms + [bufres["big0"].wait()])
                    bs = 0 if kb % 2 == 0 else 1
                    for kch in range(8):
                        r.emit("tensor", lambda h, kch=kch, koff=koff, bs=bs: h.matmul(
                            psum[bs][:, :],
                            sb["wc"][:, kch * 128:(kch + 1) * 128],
                            sb["big0"][:, koff + kch * 512:koff + (kch + 1) * 512],
                            start=(kch == 0), stop=(kch == 7)),
                            wait=[km, qm, bankres[bs].wait()] if kch == 0 else (),
                            inc=("mm", 1) if kch == 7 else None)
                    bufres[kbuf].set(r.mark("mm"))
                    smm = r.mark("mm")
                    if kb == dkb:
                        smm = r.emit("vector", lambda h, bs=bs: h.tensor_add(
                            psum[bs][:, :], psum[bs][:, :], sb["mks"][:, :]),
                            wait=[smm, mkm])
                    return smm

                for kb in range(dkb + 1):
                    smm = emit_S(kb)
                    if kb == dkb:
                        prev_mask_use = smm
                    bs = 0 if kb % 2 == 0 else 1
                    tmx = r.emit("vector", lambda h, bs=bs, kb=kb: h.reduce_max(
                        sb["xsq"][:, KMX + kb:KMX + kb + 1], psum[bs][:, :],
                        axis=AX.X),
                        wait=[smm, prev_stdm] +
                             ([prev_pem] if kb == 0 else []))
                    sev = r.emit("scalar", lambda h, bs=bs, kb=kb: h.activation(
                        out=sb["big1"][:, kb * 512:(kb + 1) * 512],
                        in_=psum[bs][:, :], func=AF.Identity),
                        wait=[smm, tmx] +
                             ([bufres["big1"].wait()] if kb == 0 else []))
                    bankres[bs].set(sev)
                gmx = r.emit("vector", lambda h, dkb=dkb: h.reduce_max(
                    sb["xsq"][:, M_:M_ + 1], sb["xsq"][:, KMX:KMX + dkb + 1],
                    axis=AX.X))
                ngm = r.emit("vector", lambda h: h.tensor_scalar_mul(
                    sb["xsq"][:, NGM:NGM + 1], sb["xsq"][:, M_:M_ + 1], -1.0))
                pem = None
                pvm = {}
                prev_trm = None
                for kb in range(dkb + 1):
                    pem = r.emit("scalar", lambda h, kb=kb: h.activation(
                        out=sb["wd"][:, 0:512],
                        in_=sb["big1"][:, kb * 512:(kb + 1) * 512],
                        func=AF.Exp, bias=sb["xsq"][:, NGM:NGM + 1],
                        accum_out=sb["xsq"][:, KSM + kb:KSM + kb + 1]),
                        wait=[ngm, prev_trm] +
                             ([bufres["wd"].wait()] if kb == 0 else []))
                    ptp = psum[4][:, :].bitcast(BF16)
                    trm = None
                    for s4 in range(4):
                        trm = r.emit("tensor", lambda h, s4=s4, ptp=ptp: h.transpose(
                            ptp[:, s4 * 128:(s4 + 1) * 128],
                            sb["wd"][:, s4 * 128:(s4 + 1) * 128],
                            sb["idb"][:, :]),
                            wait=[pem, bankres[4].wait()] if s4 == 0 else (),
                            inc=("mm", 1) if s4 == 3 else None)
                    prev_trm = trm
                    ptev = r.emit("vector", lambda h, ptp=ptp: h.tensor_copy(
                        sb["wd"][:, 512:1024], ptp[:, 0:512]), wait=[trm])
                    bankres[4].set(ptev)
                    for dh in range(2):
                        for kc in range(4):
                            pvm[dh] = r.emit("tensor",
                                   lambda h, dh=dh, kc=kc, kb=kb, dkb=dkb: h.matmul(
                                psum[2 + dh][:, :],
                                sb["wd"][:, 512 + kc * 128:512 + (kc + 1) * 128],
                                sb["hsb"][:, (kb * 4 + kc) * 1024 + dh * 512:
                                          (kb * 4 + kc) * 1024 + dh * 512 + 512],
                                start=(kb == 0 and kc == 0),
                                stop=(kb == dkb and kc == 3)),
                                wait=([ptev, vlm] +
                                      ([bankres[2 + dh].wait()] if kb == 0 else []))
                                     if kc == 0 else (),
                                inc=("mm", 1) if kc == 3 else None)
                prev_pem = pem
                bufres["wd"].set(pvm[1])
                lsum = r.emit("vector", lambda h, dkb=dkb: h.reduce_sum(
                    sb["xsq"][:, L_:L_ + 1], sb["xsq"][:, KSM:KSM + dkb + 1],
                    axis=AX.X), wait=[pem])
                stdm = None
                if "fl_st" in dbg:
                    stdm = r.dma("sync", dbg["fl_st"][tq], sb["xsq"][:, 0:8],
                                 ["s1", "s2", "s3"][tq % 3],
                                 wait=[r.mark("dv"), r.mark("ac")])
                    prev_stdm = stdm
                r.emit("vector", lambda h: h.reciprocal(
                    sb["xsq"][:, RS:RS + 1], sb["xsq"][:, L_:L_ + 1]),
                    wait=[lsum, stdm, prev_oev])
                rm = r.emit("vector", lambda h: h.tensor_scalar_mul(
                    sb["xsq"][:, RS:RS + 1], sb["xsq"][:, RS:RS + 1], 1.0 / 32.0))
                oev = None
                for dh in range(2):
                    oev = r.emit("scalar", lambda h, dh=dh: h.activation(
                        out=sb["wc"][:, dh * 512:dh * 512 + 512],
                        in_=psum[2 + dh][:, :], func=AF.Identity,
                        scale=sb["xsq"][:, RS:RS + 1]),
                        wait=[pvm[dh], rm])
                    bankres[2 + dh].set(oev)
                prev_oev = oev
                pto = psum[4][:, :].bitcast(BF16)
                trm = None
                for s8 in range(8):
                    trm = r.emit("tensor", lambda h, s8=s8, pto=pto: h.transpose(
                        pto[:, s8 * 128:(s8 + 1) * 128],
                        sb["wc"][:, s8 * 128:(s8 + 1) * 128],
                        sb["idb"][:, :]),
                        wait=[oev, bankres[4].wait()] if s8 == 0 else (),
                        inc=("mm", 1) if s8 == 7 else None)
                bufres["wc"].set(trm)
                otev = r.emit("scalar", lambda h, pto=pto: h.activation(
                    out=sb["wd"][:, 1024:2048], in_=pto[:, 0:1024],
                    func=AF.Identity), wait=[trm, bufres["wd"].wait()])
                bankres[4].set(otev)
                sm = r.dma("sync",
                           of_hbm.rearrange("(c p) t -> p c t", p=128)
                           [:, :, tq * 128:(tq + 1) * 128],
                           sb["wd"][:, 1024:2048].rearrange("p (c w) -> p c w", c=8),
                           "oo", wait=[otev])
                bufres["wd"].set(sm)
                bufres["big1"].set(pem)
            oom = r.mark("oo")
            dbg_dump(r, "a_of", of_hbm, [oom])
            bufres["hsb"].set(r.mark("mm"))
            bufres["big0"].set(r.mark("mm"))
            brm = bufres["brow"].load("gpsimd", sb["brow"][0:1, :], dr["bproj_row"])
            for tb in range(4 if "C" in ATTN_PARTS else 0):
                t0 = tb * 1024
                ofm = bufres["xlt"].load(
                    "sync", sb["xlt"][:, :].rearrange("p (c w) -> p c w", c=8),
                    of_hbm.rearrange("(c p) t -> p c t", p=128)
                    [:, :, t0:t0 + 1024], wait=[oom])
                for ot in range(8):
                    wbuf = "wa" if ot % 2 == 0 else "wb"
                    wm = bufres[wbuf].load(
                        weng(wbuf),
                        sb[wbuf][:, 0:1024].rearrange("p (k m) -> p k m", k=8),
                        wbf["wprojt"].rearrange("(k p) m -> p k m", p=128)
                        [:, :, ot * 128:(ot + 1) * 128], wait=INIT + [castm["wprojt"]])
                    glast = None
                    for half in range(2):
                        bk = (ot * 2 + half) % 4
                        for kch in range(8):
                            r.emit("tensor", lambda h, kch=kch, half=half, bk=bk,
                                   wbuf=wbuf: h.matmul(
                                psum[bk][:, :],
                                sb[wbuf][:, kch * 128:(kch + 1) * 128],
                                sb["xlt"][:, kch * 1024 + half * 512:
                                          kch * 1024 + half * 512 + 512],
                                start=(kch == 0), stop=False),
                                wait=[wm, ofm, bankres[bk].wait()] if kch == 0 else ())
                        gl = r.emit("tensor", lambda h, ot=ot, bk=bk: h.matmul(
                            psum[bk][:, :],
                            sb["brow"][0:1, ot * 128:(ot + 1) * 128],
                            sb["ones_rowb"][0:1, 0:512],
                            start=False, stop=True),
                            wait=[brm] if ot == 0 and half == 0 else (),
                            inc=("mm", 1))
                        glast = gl
                        xrb = "xr0" if half == 0 else "xr1"
                        xv = bufres[xrb].load(
                            "sync", sb["xr"][:, half * 512:half * 512 + 512],
                            x_cur.rearrange("(c p) t -> p c t", p=128)
                            [:, ot:ot + 1, t0 + half * 512:t0 + half * 512 + 512]
                            .rearrange("p o t -> p (o t)"),
                            wait=list(xcur_ready))
                        sm = r.emit("vector", lambda h, half=half, bk=bk: h.tensor_add(
                            sb["xr"][:, half * 512:half * 512 + 512],
                            sb["xr"][:, half * 512:half * 512 + 512],
                            psum[bk][:, :]), wait=[gl, xv])
                        bankres[bk].set(sm)
                        stm = r.dma("sync",
                                    x_cur.rearrange("(c p) t -> p c t", p=128)
                                    [:, ot:ot + 1, t0 + half * 512:t0 + half * 512 + 512]
                                    .rearrange("p o t -> p (o t)"),
                                    sb["xr"][:, half * 512:half * 512 + 512],
                                    xrb, wait=[sm])
                        bufres[xrb].set(stm)
                    bufres[wbuf].set(glast)
                bufres["xlt"].set(r.mark("mm"))
            xcur_ready[:] = [r.mark("xr0"), r.mark("xr1")]
            bufres["brow"].set(r.mark("mm"))

        def phase_mkM():
            wrm = bufres["big0"].load(
                "gpsimd", sb["big0"][:, 0:8192].rearrange("p (k n) -> p k n", k=8),
                wbf["wfr"].rearrange("(k p) n -> p k n", p=128), wait=INIT + [castm["wfr"]])
            wim = bufres["big1"].load(
                "gpsimd", sb["big1"][:, 0:8192].rearrange("p (k n) -> p k n", k=8),
                wbf["wfi"].rearrange("(k p) n -> p k n", p=128), wait=INIT + [castm["wfi"]])
            for (uname, terms) in (
                    ("u_hbm", (("big0", "c_fr", wrm), ("big1", "c_fin", wim))),
                    ("u2_hbm", (("big1", "c_fr", wim), ("big0", "c_fi", wrm)))):
                for nh in range(2):
                    first = True
                    for (wsb, fname, wmark) in terms:
                        for pch in range(8):
                            buf = "wa" if pch % 2 == 0 else "wb"
                            fv = bufres[buf].load(
                                "sync", sb[buf][:, 0:512],
                                dr[fname][pch * 128:(pch + 1) * 128,
                                          nh * 512:(nh + 1) * 512], wait=INIT)
                            for m in range(8):
                                w = [fv, wmark] if m == 0 else []
                                if first and pch == 0 and m == 0:
                                    w += [bankres[i].wait() for i in range(8)]
                                gm = r.emit("tensor",
                                       lambda h, pch=pch, m=m, buf=buf, wsb=wsb,
                                              first=first: h.matmul(
                                           psum[m][:, :],
                                           sb[wsb][:, pch * 1024 + m * 128:
                                                   pch * 1024 + (m + 1) * 128],
                                           sb[buf][:, 0:512],
                                           start=(first and pch == 0),
                                           stop=((not first) and pch == 7)),
                                       wait=w,
                                       inc=("mm", 1) if m == 7 else None)
                            bufres[buf].set(gm)
                        first = False
                    mmv = r.mark("mm")
                    evm = None
                    for m in range(8):
                        evm = r.emit("scalar", lambda h, m=m: h.activation(
                            out=sb["xt"][:, m * 512:(m + 1) * 512],
                            in_=psum[m][:, :], func=AF.Identity),
                            wait=[mmv, bufres["xt"].wait()] if m == 0 else ())
                    for m in range(8):
                        bankres[m].set(evm)
                    sv = r.dma("scalar",
                               dr[uname].rearrange("(mc p) n -> p mc n", p=128)
                               [:, :, nh * 512:(nh + 1) * 512],
                               sb["xt"][:, 0:8 * 512].rearrange(
                                   "p (mc n) -> p mc n", mc=8),
                               "us", wait=[evm])
                    bufres["xt"].set(sv)
            uq = r.mark("us")
            bufres["big0"].set(r.mark("mm"))
            bufres["big1"].set(r.mark("mm"))
            for nh in range(2):
                first = True
                for (uname, gname) in (("u_hbm", "c_gr"), ("u2_hbm", "c_gin")):
                    for kch in range(8):
                        ubuf = "wc" if kch % 2 == 0 else "wd"
                        gbuf = "wa" if kch % 2 == 0 else "wb"
                        uv = bufres[ubuf].load(
                            "gpsimd", sb[ubuf][:, 0:1024],
                            dr[uname][kch * 128:(kch + 1) * 128, :], wait=[uq])
                        gv = bufres[gbuf].load(
                            "sync", sb[gbuf][:, 0:512],
                            dr[gname][kch * 128:(kch + 1) * 128,
                                      nh * 512:(nh + 1) * 512], wait=INIT)
                        for m in range(8):
                            w = [uv, gv] if m == 0 else []
                            if first and kch == 0 and m == 0:
                                w += [bankres[i].wait() for i in range(8)]
                            gm = r.emit("tensor",
                                   lambda h, m=m, ubuf=ubuf, gbuf=gbuf, kch=kch,
                                          first=first: h.matmul(
                                       psum[m][:, :],
                                       sb[ubuf][:, m * 128:(m + 1) * 128],
                                       sb[gbuf][:, 0:512],
                                       start=(first and kch == 0),
                                       stop=((not first) and kch == 7)),
                                   wait=w,
                                   inc=("mm", 1) if m == 7 else None)
                        bufres[ubuf].set(gm)
                        bufres[gbuf].set(gm)
                    first = False
                mmv = r.mark("mm")
                evm = None
                for m in range(8):
                    evm = r.emit("scalar", lambda h, m=m, nh=nh: h.activation(
                        out=sb["big0"][:, m * 1024 + nh * 512:
                                       m * 1024 + (nh + 1) * 512],
                        in_=psum[m][:, :], func=AF.Identity),
                        wait=[mmv, bufres["big0"].wait()] if m == 0 else ())
                for m in range(8):
                    bankres[m].set(evm)
                dm = dbg_dump(r, f"mkm_big0_{nh}", sb["big0"][:, :], [r.mark("ac")])
                if dm is not None:
                    bufres["big0"].set(dm)
            acm = r.mark("ac")
            dbg_dump(r, "mkm_big0", sb["big0"][:, :], [acm])
            sv = r.dma("scalar", m_hbm.rearrange("(mc p) n -> p mc n", p=128),
                       sb["big0"][:, 0:8192].rearrange("p (mc n) -> p mc n", mc=8),
                       "ms", wait=[acm])
            bufres["big0"].set(sv)
            return r.mark("ms")

        def phase_mamba():
            ZB, YB = 0, 16384
            def xbc(ct):
                if ct < 8:
                    return sb["big0"], ct * 1040
                if ct < 16:
                    return sb["big1"], (ct - 8) * 1040
                return sb["cs"], 0
            R0, R1, R2, R3 = 0, 1024, 2048, 3072
            WCOL, CCOL, CA64, XS0 = 896, 904, 912, 0
            ptb1 = psum[1][:, :].bitcast(BF16)

            brm = bufres["brow"].load("gpsimd", sb["brow"][0:1, :], dr["bnl_row"])
            h0m = bufres["vsb"].load("sync", sb["hts"][0:64, :], dr["h0"])
            on64 = r.emit("gpsimd", lambda h: h.memset(
                sb["scr8"][64:65, 1024:1152], 1.0), wait=[bufres["vsb"].wait()])

            statem = [h0m]
            dbgm = {}
            dmh0 = dbg_dump(r, "mb_h0", sb["hts"][0:64, :], [h0m])
            if dmh0 is not None:
                statem[0] = dmh0
            bufres["lcr"].set(bufres["urow"].wait())
            bufres["car"].set(bufres["urow"].wait())
            am = r.emit("scalar", lambda h: h.activation(
                out=sb["acol"][0:8, 0:1], in_=sb["acol"][0:8, 1:2], func=AF.Exp),
                wait=INIT)
            am = r.emit("vector", lambda h: h.tensor_scalar_mul(
                sb["acol"][0:8, 0:1], sb["acol"][0:8, 0:1], -1.0), wait=[am])
            import os
            MDBG = os.environ.get("MAMBA_DBG", "")
            cvh = None
            for ct in range(0 if "nodiag" in MDBG else 17):
                dm = None
                for k in range(16):
                    dm = r.emit("vector", lambda h, ct=ct, k=k: h.tensor_scalar_mul(
                        sb["wc"][:, k * 128:(k + 1) * 128], sb["idb"][:, :],
                        sb["cvw"][:, ct * 16 + k:ct * 16 + k + 1]),
                        wait=[INIT[0], bufres["wc"].wait()] if k == 0 else ())
                cvh = r.dma("sync",
                            conv_hbm.rearrange("(q p) m -> p q m", p=128)
                            [:, ct * 16:(ct + 1) * 16, :],
                            sb["wc"][:, 0:2048].rearrange("p (k m) -> p k m", k=16),
                            "cvh", wait=[dm])
                bufres["wc"].set(cvh)
            hm0 = None
            for ct in range(0 if "nohist" in MDBG else 17):
                tb_, base = xbc(ct)
                hm0 = r.emit("gpsimd", lambda h, tb_=tb_, base=base: h.memset(
                    tb_[:, base:base + 15], 0.0),
                    wait=[bufres["big0"].wait(), bufres["big1"].wait(),
                          bufres["cs"].wait()] if ct == 0 else ())

            for tc in range(mamba_tc):
                t0 = tc * 1024
                lnm = {}
                for half in range(2):
                    lnm[half], _ = emit_ln(t0 + half * 512, 512, half * 512, 1024)
                    r.emit("vector", lambda h, half=half: h.tensor_copy(
                        sb["nmb"][0:1, half * 512:half * 512 + 512],
                        sb["stat"][0:1, 1024:1536]))
                    r.emit("vector", lambda h, half=half: h.tensor_scalar_add(
                        sb["scr8"][64:65, half * 512:half * 512 + 512],
                        sb["stat"][0:1, 1536:2048], 1.0),
                        wait=[on64, statem[0]] if tc == 0 and half == 0 else ())
                rowm = r.mark("dv")
                ipm = {}
                for mt in range(34):
                    mw = 128 if mt < 33 else 8
                    wbuf = "wa" if mt % 2 == 0 else "wb"
                    wm = bufres[wbuf].load(
                        weng(wbuf),
                        sb[wbuf][:, 0:8 * mw].rearrange("p (k m) -> p k m", k=8),
                        wbf["w_in"].rearrange("(k p) m -> p k m", p=128)
                        [:, :, mt * 128:mt * 128 + mw], wait=INIT + [castm["w_in"]])
                    glast = None
                    for half in range(2):
                        bk = (mt * 2 + half) % 4
                        for kch in range(8):
                            glast = r.emit("tensor", lambda h, kch=kch, half=half,
                                   bk=bk, wbuf=wbuf, mw=mw: h.matmul(
                                psum[bk][0:mw, :],
                                sb[wbuf][:, kch * mw:(kch + 1) * mw],
                                sb["xlt"][:, kch * 1024 + half * 512:
                                          kch * 1024 + half * 512 + 512],
                                start=(kch == 0), stop=(kch == 7)),
                                wait=[wm, bankres[bk].wait(), lnm[0], lnm[1]]
                                     if kch == 0 else (),
                                inc=("mm", 1) if kch == 7 else None)
                        if mt < 16:
                            em = r.emit("scalar", lambda h, mt=mt, half=half, bk=bk:
                                        h.activation(
                                out=sb["hsb"][:, ZB + mt * 1024 + half * 512:
                                              ZB + mt * 1024 + half * 512 + 512],
                                in_=psum[bk][:, :], func=AF.Identity),
                                wait=[glast] + ([bufres["hsb"].wait()]
                                                if mt == 0 and half == 0 else []))
                        elif mt < 33:
                            tb_, base = xbc(mt - 16)
                            em = r.emit("scalar", lambda h, tb_=tb_, base=base,
                                        half=half, bk=bk: h.activation(
                                out=tb_[:, base + 15 + half * 512:
                                        base + 15 + half * 512 + 512],
                                in_=psum[bk][:, :], func=AF.Identity),
                                wait=[glast, hm0])
                        else:
                            em = r.emit("scalar", lambda h, half=half, bk=bk:
                                        h.activation(
                                out=sb["xt"][0:8, R0 + half * 512:
                                             R0 + half * 512 + 512],
                                in_=psum[bk][0:8, :], func=AF.Identity,
                                bias=sb["dtb8"][0:8, 0:1]),
                                wait=[glast, bufres["xt"].wait()])
                        bankres[bk].set(em)
                        ipm[(mt, half)] = em
                    bufres[wbuf].set(glast)
                bufres["xlt"].set(r.mark("mm"))
                zdone = r.mark("ac")
                if tc == 0:
                    dbgm["zx"] = dbg_dump(r, "mb_zx", sb["hsb"][:, 0:16384], [zdone])
                for ct in range(17):
                    tb_, base = xbc(ct)
                    wbuf = "wa" if ct % 2 == 0 else "wb"
                    dm = bufres[wbuf].load(
                        "sync", sb[wbuf][:, 0:2048].rearrange("p (k m) -> p k m", k=16),
                        conv_hbm.rearrange("(q p) m -> p q m", p=128)
                        [:, ct * 16:(ct + 1) * 16, :], wait=[cvh])
                    gms = []
                    for th in range(2):
                        bk = 4 + th
                        for k in range(16):
                            gm = r.emit("tensor", lambda h, k=k, th=th, bk=bk,
                                   tb_=tb_, base=base, wbuf=wbuf: h.matmul(
                                psum[bk][:, :],
                                sb[wbuf][:, k * 128:(k + 1) * 128],
                                tb_[:, base + th * 512 + k:
                                    base + th * 512 + k + 512],
                                start=(k == 0), stop=(k == 15)),
                                wait=[dm, bankres[bk].wait(),
                                      ipm[(16 + ct, 0)], ipm[(16 + ct, 1)]]
                                     if k == 0 else (),
                                inc=("mm", 1) if k == 15 else None)
                        gms.append(gm)
                    tlm = r.emit("vector", lambda h, tb_=tb_, base=base: h.tensor_copy(
                        tb_[:, base:base + 15], tb_[:, base + 1024:base + 1039]),
                        wait=gms)
                    for th in range(2):
                        bk = 4 + th
                        em = r.emit("scalar", lambda h, th=th, bk=bk, tb_=tb_,
                                    base=base, ct=ct: h.activation(
                            out=tb_[:, base + 15 + th * 512:base + 15 + th * 512 + 512],
                            in_=psum[bk][:, :], func=AF.Silu,
                            bias=sb["cvb"][:, ct:ct + 1]),
                            wait=[tlm])
                        bankres[bk].set(em)
                    bufres[wbuf].set(em)
                    ipm[("c", ct)] = em
                if tc == 0:
                    dbg_dump(r, "mb_xbc0",
                             sb["big0"][:, 0:8 * 1040].rearrange(
                                 "p (s w) -> p s w", s=8)[:, :, 0:1039],
                             [ipm[("c", ct_)] for ct_ in range(8)])
                xr_ = sb["xt"][0:8, :]
                dt0 = [ipm[(33, 0)], ipm[(33, 1)], am]
                r.emit("vector", lambda h: h.tensor_scalar_max(
                    xr_[:, R1:R1 + 1024], xr_[:, R0:R0 + 1024], 0.0), wait=dt0)
                r.emit("scalar", lambda h: h.activation(
                    out=xr_[:, R2:R2 + 1024], in_=xr_[:, R0:R0 + 1024],
                    func=AF.Abs), wait=dt0)
                acm = r.mark("ac")
                r.emit("scalar", lambda h: h.activation(
                    out=xr_[:, R3:R3 + 1024], in_=xr_[:, R2:R2 + 1024],
                    func=AF.Exp, scale=-1.0))
                r.emit("scalar", lambda h: h.activation(
                    out=xr_[:, R2:R2 + 1024], in_=xr_[:, R3:R3 + 1024],
                    func=AF.Ln, bias=1.0))
                acm = r.mark("ac")
                r.emit("vector", lambda h: h.tensor_add(
                    xr_[:, R0:R0 + 1024], xr_[:, R1:R1 + 1024],
                    xr_[:, R2:R2 + 1024]), wait=[acm])
                r.emit("vector", lambda h: h.tensor_scalar_mul(
                    xr_[:, R1:R1 + 1024], xr_[:, R0:R0 + 1024],
                    sb["acol"][0:8, 0:1]))
                dvm = r.mark("dv")
                ldtm = r.emit("scalar", lambda h: h.activation(
                    out=xr_[:, R2:R2 + 1024], in_=xr_[:, R0:R0 + 1024],
                    func=AF.Ln), wait=[dvm])
                scm = None
                for sc in range(8):
                    scm = r.emit("vector", lambda h, sc=sc: h.tensor_tensor_scan(
                        xr_[:, R3 + sc * 128:R3 + (sc + 1) * 128],
                        xr_[:, R1 + sc * 128:R1 + (sc + 1) * 128],
                        sb["z8"][0:8, :], 0.0, op0=ALU.add, op1=ALU.add),
                        wait=[INIT[0]] if sc == 0 else ())
                cmbm = r.emit("vector", lambda h: h.tensor_sub(
                    xr_[:, R1:R1 + 1024], xr_[:, R2:R2 + 1024],
                    xr_[:, R3:R3 + 1024]), wait=[ldtm])
                cam = r.emit("scalar", lambda h: h.activation(
                    out=xr_[:, R2:R2 + 1024], in_=xr_[:, R3:R3 + 1024],
                    func=AF.Exp), wait=[cmbm])
                lcs = r.dma("sync", lc_hbm, xr_[:, R3:R3 + 1024], "uh",
                            wait=[cmbm, r.mark("lcr")])
                cas = r.dma("sync", ca_hbm, xr_[:, R2:R2 + 1024], "vh",
                            wait=[cam, r.mark("carg"), r.mark("hts")])
                wlm = None
                for sc in range(8):
                    wlm = r.emit("vector", lambda h, sc=sc: h.tensor_scalar_add(
                        xr_[:, R0 + sc * 128:R0 + (sc + 1) * 128],
                        xr_[:, R1 + sc * 128:R1 + (sc + 1) * 128],
                        xr_[:, R3 + sc * 128 + 127:R3 + sc * 128 + 128]),
                        wait=[ldtm] if sc == 0 else ())
                wm_ = r.emit("scalar", lambda h: h.activation(
                    out=xr_[:, R3:R3 + 1024], in_=xr_[:, R0:R0 + 1024],
                    func=AF.Exp), wait=[wlm, lcs])
                if tc == 0:
                    dbg_dump(r, "mb_dt", sb["xt"][0:8, :], [wm_, r.mark("dv")])
                WC_BW, WC_MT, WC_EE, WC_BT, WC_C64 = 0, 512, 640, 768, 832
                bct, cbase = xbc(16)
                c64m = r.emit("vector", lambda h, bct=bct, cbase=cbase: h.tensor_copy(
                    sb["wd"][0:64, WC_C64:WC_C64 + 1024],
                    bct[64:128, cbase + 15:cbase + 15 + 1024]),
                    wait=[ipm[("c", 16)], bufres["wd"].wait()])
                yevs = []
                for sc in range(8):
                    lm = bufres["lcr"].load(
                        "sync", sb["lcr"][0:1, :].rearrange("o (g w) -> o g w", g=8),
                        lc_hbm[:, sc * 128:(sc + 1) * 128]
                        .rearrange("g w -> () g w"), wait=[lcs])
                    cm = bufres["car"].load(
                        "gpsimd", sb["car"][0:1, :].rearrange("o (g w) -> o g w", g=8),
                        ca_hbm[:, sc * 128:(sc + 1) * 128]
                        .rearrange("g w -> () g w"), wait=[cas])
                    cem = bufres["hts"].load(
                        "sync", sb["urow"][0:1, 3072:3088].bitcast(F32),
                        ca_hbm[:, sc * 128 + 127:sc * 128 + 128]
                        .rearrange("g o -> () (g o)"), wait=[cas],
                        allow_slow_non_contiguous=True)
                    xtm = None
                    for rnd in range(2):
                        tms = []
                        for j8 in range(8):
                            j = rnd * 8 + j8
                            tb_, base = xbc(j)
                            tms.append(r.emit("tensor",
                                   lambda h, j8=j8, tb_=tb_, base=base, sc=sc:
                                   h.transpose(
                                ptb1[:, j8 * 128:(j8 + 1) * 128],
                                tb_[:, base + 15 + sc * 128:
                                    base + 15 + sc * 128 + 128],
                                sb["idb"][:, :]),
                                wait=[bankres[1].wait(), ipm[("c", j)]]
                                     if j8 == 0 else (),
                                inc=("mm", 1) if j8 == 7 else None))
                        xtm = r.emit("scalar", lambda h, rnd=rnd: h.activation(
                            out=sb["wc"][:, rnd * 1024:(rnd + 1) * 1024],
                            in_=ptb1[:, 0:1024], func=AF.Identity),
                            wait=[tms[-1]] +
                                 ([bufres["wc"].wait()] if rnd == 0 else []))
                        bankres[1].set(xtm)
                    btm = r.emit("tensor", lambda h, bct=bct, cbase=cbase, sc=sc:
                                 h.transpose(
                        ptb1[0:128, 0:64],
                        bct[0:64, cbase + 15 + sc * 128:cbase + 15 + sc * 128 + 128],
                        sb["idb"][0:64, 0:64]),
                        wait=[bankres[1].wait(), ipm[("c", 16)]], inc=("mm", 1))
                    btm = r.emit("scalar", lambda h: h.activation(
                        out=sb["wd"][:, WC_BT:WC_BT + 64], in_=ptb1[0:128, 0:64],
                        func=AF.Identity), wait=[btm, bufres["wd"].wait(), c64m])
                    bankres[1].set(btm)
                    wtm = r.emit("tensor", lambda h, sc=sc: h.transpose(
                        psum[1][0:128, 0:8],
                        sb["xt"][0:8, R3 + sc * 128:R3 + (sc + 1) * 128],
                        sb["idf"][0:8, 0:8]),
                        wait=[bankres[1].wait(), wm_], inc=("mm", 1))
                    wtm = r.emit("vector", lambda h: h.tensor_copy(
                        sb["xsq"][:, WCOL:WCOL + 8], psum[1][0:128, 0:8]),
                        wait=[wtm])
                    bankres[1].set(wtm)
                    ctm = r.emit("tensor", lambda h, sc=sc: h.transpose(
                        psum[1][0:128, 0:8],
                        sb["xt"][0:8, R1 + sc * 128:R1 + (sc + 1) * 128],
                        sb["idf"][0:8, 0:8]),
                        wait=[bankres[1].wait(), cmbm], inc=("mm", 1))
                    ctm = r.emit("vector", lambda h: h.tensor_copy(
                        sb["xsq"][:, CCOL:CCOL + 8], psum[1][0:128, 0:8]),
                        wait=[ctm])
                    bankres[1].set(ctm)
                    bwm = None
                    for hd in range(8):
                        bwm = r.emit("vector", lambda h, hd=hd: h.tensor_scalar_mul(
                            sb["wd"][:, WC_BW + hd * 64:WC_BW + (hd + 1) * 64],
                            sb["wd"][:, WC_BT:WC_BT + 64],
                            sb["xsq"][:, WCOL + hd:WCOL + hd + 1]),
                            wait=[btm, wtm] if hd == 0 else ())
                    gtm = r.emit("tensor", lambda h, bct=bct, cbase=cbase, sc=sc:
                                 h.matmul(
                        psum[0][:, 0:128],
                        bct[0:64, cbase + 15 + sc * 128:cbase + 15 + sc * 128 + 128],
                        sb["wd"][0:64, WC_C64 + sc * 128:WC_C64 + (sc + 1) * 128],
                        start=True, stop=True),
                        wait=[bankres[0].wait(), c64m, ipm[("c", 16)]],
                        inc=("mm", 1))
                    ym = {}
                    for hd in range(8):
                        ebm = r.emit("tensor", lambda h, hd=hd: h.matmul(
                            psum[1][:, 0:128], sb["ones_rowf"][0:1, :],
                            sb["lcr"][0:1, hd * 128:(hd + 1) * 128],
                            start=True, stop=True),
                            wait=[bankres[1].wait(), lm], inc=("mm", 1))
                        ebm = r.emit("vector", lambda h: h.tensor_add(
                            psum[1][:, 0:128], psum[1][:, 0:128], sb["ngt"][:, :]),
                            wait=[ebm, INIT[0]])
                        eem = r.emit("scalar", lambda h, hd=hd: h.activation(
                            out=sb["wd"][:, WC_EE:WC_EE + 128], in_=psum[1][:, 0:128],
                            func=AF.Exp, bias=sb["xsq"][:, CCOL + hd:CCOL + hd + 1]),
                            wait=[ebm, ctm])
                        bankres[1].set(eem)
                        mtm = r.emit("vector", lambda h: h.tensor_mul(
                            sb["wd"][:, WC_MT:WC_MT + 128],
                            sb["wd"][:, WC_EE:WC_EE + 128], psum[0][:, 0:128]),
                            wait=[eem, gtm])
                        if tc == 0 and sc == 0 and hd == 7:
                            dm1 = dbg_dump(r, "mb_ee",
                                           sb["wd"][:, WC_EE:WC_EE + 128], [eem])
                            dm2 = dbg_dump(r, "mb_mt",
                                           sb["wd"][:, WC_MT:WC_MT + 128], [mtm])
                            if dm2 is not None:
                                bankres[1].set(dm1)
                                mtm = dm2
                        csb = r.emit("tensor", lambda h, hd=hd: h.matmul(
                            psum[1][0:64, 0:128], sb["ones_rowb"][0:1, 0:64],
                            sb["car"][0:1, hd * 128:(hd + 1) * 128],
                            start=True, stop=True),
                            wait=[bankres[1].wait(), cm], inc=("mm", 1))
                        cscm = r.emit("vector", lambda h, sc=sc:
                                      h.tensor_mul(
                            sb["xsq"][0:64, 0:128],
                            sb["wd"][0:64, WC_C64 + sc * 128:WC_C64 + (sc + 1) * 128],
                            psum[1][0:64, 0:128]),
                            wait=[csb, ym.get(hd * 2 - 1)])
                        bankres[1].set(cscm)
                        for pc in range(2):
                            j = hd * 2 + pc
                            bk = 2 + j // 4
                            cl = (j % 4) * 128
                            r.emit("tensor", lambda h, j=j, bk=bk, cl=cl: h.matmul(
                                psum[bk][:, cl:cl + 128],
                                sb["wc"][:, j * 128:(j + 1) * 128],
                                sb["wd"][:, WC_MT:WC_MT + 128],
                                start=True, stop=False),
                                wait=[mtm, xtm] + ([bankres[bk].wait()]
                                                   if j % 4 == 0 else []))
                            ym[j] = r.emit("tensor", lambda h, hd=hd, pc=pc, bk=bk,
                                   cl=cl: h.matmul(
                                psum[bk][:, cl:cl + 128],
                                sb["hts"][0:64, hd * 256 + pc * 128:
                                          hd * 256 + pc * 128 + 128],
                                sb["xsq"][0:64, 0:128],
                                start=False, stop=True),
                                wait=[cscm, statem[0]],
                                inc=("mm", 1))
                    for bk in range(2, 6):
                        yev = r.emit("scalar", lambda h, bk=bk, sc=sc: h.activation(
                            out=sb["hsb"][:, YB:YB + 16384]
                            .rearrange("p (j w) -> p j w", j=16)
                            [:, (bk - 2) * 4:(bk - 2) * 4 + 4, sc * 128:sc * 128 + 128],
                            in_=psum[bk][:, :].rearrange("p (j w) -> p j w", j=4),
                            func=AF.Identity),
                            wait=[ym[(bk - 2) * 4 + 3]] +
                                 ([bufres["hsb"].wait()] if bk == 2 else []))
                        yevs.append(yev)
                        bankres[bk].set(yev)
                    yevm = r.mark("ac")
                    dl = {}
                    for hd in range(8):
                        bk = 2 + hd // 2
                        cl = (hd % 2) * 256
                        dl[hd] = r.emit("tensor", lambda h, hd=hd, bk=bk, cl=cl:
                               h.matmul(
                            psum[bk][0:64, cl:cl + 256],
                            sb["wd"][:, WC_BW + hd * 64:WC_BW + (hd + 1) * 64],
                            sb["wc"][:, hd * 256:(hd + 1) * 256],
                            start=True, stop=True),
                            wait=[bwm, xtm, yevm] + ([bankres[bk].wait()]
                                                     if hd % 2 == 0 else []),
                            inc=("mm", 1))
                    cab = r.emit("tensor", lambda h: h.matmul(
                        psum[1][0:64, 0:8], sb["ones_rowf"][0:1, 0:64],
                        sb["urow"][0:1, 3072:3088].bitcast(F32),
                        start=True, stop=True),
                        wait=[bankres[1].wait(), cem], inc=("mm", 1))
                    cab = r.emit("vector", lambda h: h.tensor_copy(
                        sb["xsq"][0:64, CA64:CA64 + 8], psum[1][0:64, 0:8]),
                        wait=[cab])
                    bankres[1].set(cab)
                    stm_ = None
                    for hd in range(8):
                        bk = 2 + hd // 2
                        cl = (hd % 2) * 256
                        stm_ = r.emit("vector", lambda h, hd=hd, bk=bk, cl=cl:
                               h.scalar_tensor_tensor(
                            sb["hts"][0:64, hd * 256:(hd + 1) * 256],
                            sb["hts"][0:64, hd * 256:(hd + 1) * 256],
                            sb["xsq"][0:64, CA64 + hd:CA64 + hd + 1],
                            psum[bk][0:64, cl:cl + 256],
                            op0=ALU.mult, op1=ALU.add),
                            wait=[dl[hd], cab] + ([r.mark("mm")] if hd == 0 else []))
                        bankres[bk].set(stm_)
                    statem[0] = stm_
                    if tc == 0 and sc == 0:
                        dmh = dbg_dump(r, "mb_h1", sb["hts"][0:64, :], [stm_])
                        if dmh is not None:
                            statem[0] = dmh
                    if tc == 0 and sc == 1:
                        dmy = dbg_dump(r, "mb_y1",
                                       sb["hsb"][:, 16384:32768].rearrange(
                                           "p (j w) -> p j w", j=16)[:, :, 0:128],
                                       list(yevs))
                        if dmy is not None:
                            bufres["hsb"].set(dmy)
                    if tc == 0 and sc == 0:
                        dmy = dbg_dump(r, "mb_y0",
                                       sb["hsb"][:, 16384:32768].rearrange(
                                           "p (j w) -> p j w", j=16)[:, :, 0:128],
                                       list(yevs))
                        if dmy is not None:
                            bufres["hsb"].set(dmy)
                    if tc == 0 and sc == 0:
                        dmx = dbg_dump(r, "mb_xtm", sb["wc"][:, 0:2048],
                                       [r.mark("mm")])
                        if dmx is not None:
                            bufres["wc"].set(dmx)
                        else:
                            bufres["wc"].set(r.mark("mm"))
                    else:
                        bufres["wc"].set(r.mark("mm"))
                    bufres["wd"].set(r.mark("mm"))
                    bufres["lcr"].set(r.mark("mm"))
                    bufres["car"].set(r.mark("mm"))
                    bufres["hts"].set(r.mark("mm"))
                if tc == 0:
                    dbgm["y"] = dbg_dump(r, "mb_y", sb["hsb"][:, 16384:32768],
                                         yevs + [r.mark("dv")])
                dxm = None
                for j in range(16):
                    tb_, base = xbc(j)
                    dxm = r.emit("vector", lambda h, j=j, tb_=tb_, base=base:
                           h.scalar_tensor_tensor(
                        sb["hsb"][:, YB + j * 1024:YB + (j + 1) * 1024],
                        tb_[:, base + 15:base + 15 + 1024],
                        sb["dpp"][:, j:j + 1],
                        sb["hsb"][:, YB + j * 1024:YB + (j + 1) * 1024],
                        op0=ALU.mult, op1=ALU.add),
                        wait=(yevs + [INIT[0], dbgm.get("y")]) if j == 0 else ())
                sim_ = None
                for hf in range(2):
                    sim_ = r.emit("scalar", lambda h, hf=hf: h.activation(
                        out=sb["xlt"][:, 0:8192],
                        in_=sb["hsb"][:, ZB + hf * 8192:ZB + (hf + 1) * 8192],
                        func=AF.Silu),
                        wait=[zdone, bufres["xlt"].wait(), dxm])
                    gm_ = r.emit("vector", lambda h, hf=hf: h.tensor_mul(
                        sb["hsb"][:, YB + hf * 8192:YB + (hf + 1) * 8192],
                        sb["hsb"][:, YB + hf * 8192:YB + (hf + 1) * 8192],
                        sb["xlt"][:, 0:8192]), wait=[sim_])
                    bufres["xlt"].set(gm_)
                dxm = gm_
                sqm_ = r.emit("scalar", lambda h: h.activation(
                    out=sb["hsb"][:, ZB:ZB + 16384],
                    in_=sb["hsb"][:, YB:YB + 16384], func=AF.Square),
                    wait=[dxm, dbgm.get("zx")])
                for half in range(2):
                    rsm = None
                    for j in range(16):
                        rsm = r.emit("tensor", lambda h, j=j, half=half: h.matmul(
                            psum[6][0:1, 0:512], sb["ones_colb"][:, 0:1],
                            sb["hsb"][:, ZB + j * 1024 + half * 512:
                                      ZB + j * 1024 + half * 512 + 512],
                            start=(j == 0), stop=(j == 15)),
                            wait=[sqm_, bankres[6].wait()] if j == 0 else (),
                            inc=("mm", 1) if j == 15 else None)
                    mvm = r.emit("vector", lambda h, half=half: h.tensor_scalar_mul(
                        sb["xsq"][0:1, XS0:XS0 + 512], psum[6][0:1, 0:512],
                        1.0 / 2048.0), wait=[rsm])
                    bankres[6].set(mvm)
                    l1 = r.emit("scalar", lambda h: h.activation(
                        out=sb["xsq"][0:1, XS0:XS0 + 512],
                        in_=sb["xsq"][0:1, XS0:XS0 + 512], func=AF.Ln, bias=1e-5),
                        wait=[mvm])
                    l2 = r.emit("scalar", lambda h, half=half: h.activation(
                        out=sb["lcr"][0:1, half * 512:half * 512 + 512],
                        in_=sb["xsq"][0:1, XS0:XS0 + 512], func=AF.Exp, scale=-0.5),
                        wait=[l1, bufres["lcr"].wait() if half == 0 else None])
                    bufres["lcr"].set(l2)
                    bcm = r.emit("tensor", lambda h, half=half: h.matmul(
                        psum[6][:, 0:512], sb["ones_rowf"][0:1, :],
                        sb["lcr"][0:1, half * 512:half * 512 + 512],
                        start=True, stop=True),
                        wait=[l2, bankres[6].wait()], inc=("mm", 1))
                    nym = None
                    for j in range(16):
                        nym = r.emit("vector", lambda h, j=j, half=half:
                               h.scalar_tensor_tensor(
                            sb["hsb"][:, ZB + j * 1024 + half * 512:
                                      ZB + j * 1024 + half * 512 + 512],
                            sb["hsb"][:, YB + j * 1024 + half * 512:
                                      YB + j * 1024 + half * 512 + 512],
                            sb["nw"][:, j:j + 1], psum[6][:, 0:512],
                            op0=ALU.mult, op1=ALU.mult),
                            wait=[bcm] if j == 0 else ())
                    bankres[6].set(nym)
                ynm = r.mark("dv")
                if tc == 0:
                    dy2 = dbg_dump(r, "mb_yn", sb["hsb"][:, 0:16384], [ynm])
                    if dy2 is not None:
                        ynm = dy2
                for ot in range(8):
                    wbuf = "wa" if ot % 2 == 0 else "wb"
                    wm = bufres[wbuf].load(
                        weng(wbuf),
                        sb[wbuf][:, 0:2048].rearrange("p (k m) -> p k m", k=16),
                        wbf["w_out"].rearrange("(k p) m -> p k m", p=128)
                        [:, :, ot * 128:(ot + 1) * 128], wait=INIT + [castm["w_out"]])
                    for half in range(2):
                        bk = (ot * 2 + half) % 4
                        for kch in range(16):
                            gm = r.emit("tensor", lambda h, kch=kch, half=half,
                                   bk=bk, wbuf=wbuf: h.matmul(
                                psum[bk][:, :],
                                sb[wbuf][:, kch * 128:(kch + 1) * 128],
                                sb["hsb"][:, ZB + kch * 1024 + half * 512:
                                          ZB + kch * 1024 + half * 512 + 512],
                                start=(kch == 0), stop=(kch == 15)),
                                wait=[wm, bankres[bk].wait(), ynm]
                                     if kch == 0 else (),
                                inc=("mm", 1) if kch == 15 else None)
                        s1 = r.emit("scalar", lambda h, bk=bk: h.activation(
                            out=sb["xsq"][:, XS0:XS0 + 512], in_=psum[bk][:, :],
                            func=AF.Square), wait=[gm, bufres["xsq"].wait()])
                        s2 = r.emit("scalar", lambda h: h.activation(
                            out=sb["xsq"][:, XS0:XS0 + 512],
                            in_=sb["xsq"][:, XS0:XS0 + 512], func=AF.Ln, bias=1.0))
                        s3 = r.emit("scalar", lambda h: h.activation(
                            out=sb["xsq"][:, XS0:XS0 + 512],
                            in_=sb["xsq"][:, XS0:XS0 + 512], func=AF.Exp,
                            scale=-0.5))
                        tm_ = r.emit("vector", lambda h, ot=ot, half=half, bk=bk:
                               h.tensor_mul(
                            sb["xlt"][:, ot * 1024 + half * 512:
                                      ot * 1024 + half * 512 + 512],
                            psum[bk][:, :], sb["xsq"][:, XS0:XS0 + 512]),
                            wait=[s3, bufres["xlt"].wait()]
                                 if ot == 0 and half == 0 else [s3])
                        bankres[bk].set(tm_)
                        bufres["xsq"].set(tm_)
                    bufres[wbuf].set(gm)
                tdone = r.mark("dv")
                if tc == 0:
                    dy3 = dbg_dump(r, "mb_t", sb["xlt"][:, 0:8192], [tdone])
                    if dy3 is not None:
                        tdone = dy3
                rb = {}
                for half in range(2):
                    rb[half] = r.emit("tensor", lambda h, half=half: h.matmul(
                        psum[6 + half][:, 0:512], sb["scr8"][64:65, 1024:1152],
                        sb["scr8"][64:65, half * 512:half * 512 + 512],
                        start=True, stop=True),
                        wait=[rowm, bankres[6 + half].wait()], inc=("mm", 1))
                for ot in range(8):
                    wbuf = "wc" if ot % 2 == 0 else "wd"
                    wm = bufres[wbuf].load(
                        weng(wbuf),
                        sb[wbuf][:, 0:1024].rearrange("p (k m) -> p k m", k=8),
                        wbf["w_nl"].rearrange("(k p) m -> p k m", p=128)
                        [:, :, ot * 128:(ot + 1) * 128], wait=INIT + [castm["w_nl"]])
                    glast = None
                    for half in range(2):
                        bk = (ot * 2 + half) % 4
                        for kch in range(8):
                            r.emit("tensor", lambda h, kch=kch, half=half, bk=bk,
                                   wbuf=wbuf: h.matmul(
                                psum[bk][:, :],
                                sb[wbuf][:, kch * 128:(kch + 1) * 128],
                                sb["xlt"][:, kch * 1024 + half * 512:
                                          kch * 1024 + half * 512 + 512],
                                start=(kch == 0), stop=False),
                                wait=[wm, bankres[bk].wait(), tdone]
                                     if kch == 0 else ())
                        r.emit("tensor", lambda h, ot=ot, bk=bk: h.matmul(
                            psum[bk][:, :], sb["brow"][0:1, ot * 128:(ot + 1) * 128],
                            sb["ones_rowb"][0:1, 0:512], start=False, stop=False),
                            wait=[brm] if ot == 0 and half == 0 else ())
                        gl = r.emit("tensor", lambda h, half=half, bk=bk: h.matmul(
                            psum[bk][:, :], sb["ones_rowb"][0:1, 0:128],
                            sb["nmb"][0:1, half * 512:half * 512 + 512],
                            start=False, stop=True),
                            wait=[rowm] if ot == 0 and half == 0 else (),
                            inc=("mm", 1))
                        glast = gl
                        xrb = "xr0" if half == 0 else "xr1"
                        xv = bufres[xrb].load(
                            "sync", sb["xr"][:, half * 512:half * 512 + 512],
                            x_cur.rearrange("(c p) t -> p c t", p=128)
                            [:, ot:ot + 1, t0 + half * 512:t0 + half * 512 + 512]
                            .rearrange("p o t -> p (o t)"),
                            wait=list(xcur_ready))
                        m1_ = r.emit("vector", lambda h, half=half: h.tensor_mul(
                            sb["xsq"][:, XS0:XS0 + 512],
                            sb["xr"][:, half * 512:half * 512 + 512],
                            psum[6 + half][:, 0:512]),
                            wait=[xv, rb[half], bufres["xsq"].wait()])
                        bufres["xsq"].set(m1_)
                        m2_ = r.emit("vector", lambda h, half=half, bk=bk:
                               h.tensor_add(
                            sb["xr"][:, half * 512:half * 512 + 512],
                            sb["xsq"][:, XS0:XS0 + 512], psum[bk][:, :]),
                            wait=[gl])
                        bankres[bk].set(m2_)
                        stm = r.dma("sync",
                                    x_cur.rearrange("(c p) t -> p c t", p=128)
                                    [:, ot:ot + 1, t0 + half * 512:t0 + half * 512 + 512]
                                    .rearrange("p o t -> p (o t)"),
                                    sb["xr"][:, half * 512:half * 512 + 512],
                                    xrb, wait=[m2_])
                        bufres[xrb].set(stm)
                    bufres[wbuf].set(glast)
                bankres[6].set(r.mark("dv"))
                bankres[7].set(r.mark("dv"))
                bufres["xlt"].set(r.mark("mm"))
                bufres["xt"].set(r.mark("mm"))
                bufres["hsb"].set(r.mark("mm"))
                bufres["big0"].set(r.mark("mm"))
                bufres["big1"].set(r.mark("mm"))
                bufres["cs"].set(r.mark("mm"))
            xcur_ready[:] = [r.mark("xr0"), r.mark("xr1")]
            bufres["lcr"].set(r.mark("mm"))
            bufres["car"].set(r.mark("mm"))
            bufres["vsb"].set(r.mark("dv"))
            bufres["brow"].set(r.mark("mm"))

        def phase_ffn1(m_ready):
            brm = bufres["brow"].load("gpsimd", sb["brow"][0:1, :], dr["bsp_row"])
            mld = bufres["hsb"].load(
                "sync", sb["hsb"][:, 0:8192].rearrange("p (k n) -> p k n", k=8),
                m_hbm.rearrange("(k p) n -> p k n", p=128), wait=[m_ready])
            xlres = [None, None]
            for tb in range(8):
                t0 = tb * 512
                xl_base = (tb % 2) * 4096
                bufres["xlt"].set(xlres[tb % 2])
                lnm, xv = emit_ln(t0, 512, xl_base, 512)
                for mt in range(8):
                    bk = mt % 4
                    for kch in range(8):
                        gm = r.emit("tensor", lambda h, bk=bk, kch=kch, mt=mt,
                                    xl_base=xl_base: h.matmul(
                            psum[bk][:, :],
                            sb["hsb"][:, kch * 1024 + mt * 128:
                                      kch * 1024 + (mt + 1) * 128],
                            sb["xlt"][:, xl_base + kch * 512:
                                      xl_base + (kch + 1) * 512],
                            start=(kch == 0), stop=False),
                            wait=[mld, lnm, bankres[bk].wait()] if kch == 0 else ())
                    gm = r.emit("tensor", lambda h, bk=bk, mt=mt: h.matmul(
                        psum[bk][:, :], sb["brow"][0:1, mt * 128:(mt + 1) * 128],
                        sb["ones_rowb"][0:1, :], start=False, stop=True),
                        wait=[brm] if (tb == 0 and mt == 0) else (), inc=("mm", 1))
                    ev = r.emit("vector", lambda h, bk=bk, mt=mt: h.scalar_tensor_tensor(
                        sb["big0"][:, :].bitcast(F32)[:, mt * 512:(mt + 1) * 512],
                        psum[bk][:, :], 0.5,
                        sb["xt"][:, mt * 512:(mt + 1) * 512],
                        op0=ALU.mult, op1=ALU.add),
                        wait=[gm] + ([bufres["big0"].wait()]
                                     if mt == 0 else []))
                    bankres[bk].set(ev)
                dvm = r.mark("dv")
                sv = r.dma("sync",
                           x_cur.rearrange("(c p) t -> p c t", p=128)[:, :, t0:t0 + 512],
                           sb["big0"][:, :].bitcast(F32)[:, 0:8 * 512]
                           .rearrange("p (c w) -> p c w", c=8),
                           "xs", wait=[dvm])
                bufres["big0"].set(sv)
                bufres["xt"].set(dvm)
                xlres[tb % 2] = r.mark("mm")
            bufres["xlt"].set(r.mark("mm"))
            bufres["hsb"].set(r.mark("mm"))
            xcur_ready[:] = [r.mark("xs")]
            bufres["brow"].set(r.mark("mm"))

        m_ready = None
        if "ffn0_a" in phases:
            phase_ffn0()
        if "attn" in phases:
            phase_attn()
        if "ffn0_b" in phases:
            phase_ffn0()
        if "mkM" in phases:
            m_ready = phase_mkM()
        if "ffn1_a" in phases:
            phase_ffn1(m_ready)
            _dm = dbg_dump(r, "xmid", x_cur, list(xcur_ready))
            if _dm is not None:
                xcur_ready.append(_dm)
        if "mamba" in phases:
            phase_mamba()
        if "ffn1_b" in phases:
            phase_ffn1(m_ready)

        for dn, src in (("e_consts", sb["consts"][:, :]),
                        ("e_onesrf", sb["ones_rowf"][0:1, :]),
                        ("e_onesrb", sb["ones_rowb"][0:1, :]),
                        ("e_idb", sb["idb"][:, :]),
                        ("e_stat", sb["stat"][0:1, :])):
            if dn in dbg:
                ew = [(s, v) for s, v in sorted(r.cnt.items())]
                r.dma("sync", dbg[dn], src, "s5", wait=ew)

        fv = r.dma("sync", out_t, x_cur, "s0", wait=list(xcur_ready))
        finals = [(s, v) for s, v in sorted(r.cnt.items())]
        r.emit("sync", lambda h: h.nop(), wait=finals)

        sems = {}
        for sn in SEM_NAMES:
            sems[sn] = es.enter_context(nc.semaphore(sn))
        block = es.enter_context(nc.Block())
        r.replay(block, sems)
    return nc


def kernel(**inputs):
    sh, per_core = host_prep(inputs)
    nc = build_program()
    in_maps = []
    for i in range(8):
        m = dict(sh)
        m.update(per_core[i])
        in_maps.append(m)
    res = run_bass_kernel_spmd(nc, in_maps, core_ids=list(range(8)))
    outs = [np.ascontiguousarray(np.asarray(rr["out"]).T) for rr in res.results]
    return np.stack(outs).astype(np.float32)
